# version 06
# baseline (speedup 1.0000x reference)
import math
import numpy as np
import ml_dtypes
import concourse.bass as bass
import concourse.mybir as mybir
from concourse.bass_utils import run_bass_kernel_spmd
from contextlib import ExitStack

F32 = mybir.dt.float32
BF16 = mybir.dt.bfloat16
I32 = mybir.dt.int32
U8 = mybir.dt.uint8
AF = mybir.ActivationFunctionType
ALU = mybir.AluOpType
AX = mybir.AxisListType

T = 2048
NT = 16
C = 1024
NC = 8
G = 4
DI = 32
H = 16
HD = 64
NSORT = 64
RROUNDS = NSORT // 8
TCA = 256
NTCA = T // TCA
NS = 16
EMOE = 8
HMOE = 2048
LN_EPS = 1e-5
SINK = 4


def _host_consts():
    MAXPOS, ORIG, BASE, SCALE = 262144, 8192, 10000.0, 32.0
    BETA_FAST, BETA_SLOW = 1.0, 32.0
    dim = HD
    ext = MAXPOS / ORIG
    scaled_base = BASE * ext ** (dim / (dim - 2))
    idx = np.arange(0, dim, 2, dtype=np.float64) / dim
    inv_freq = 1.0 / scaled_base ** idx
    freq_extra = 1.0 / BASE ** idx
    wavelen = 2.0 * math.pi / freq_extra
    ramp = np.clip((wavelen - BETA_FAST) / (BETA_SLOW - BETA_FAST), 0.0, 1.0)
    scale = 1.0 + (SCALE - 1.0) * ramp
    t = np.arange(T, dtype=np.float64)
    freqs = t[:, None] / scale[None, :] * inv_freq[None, :]
    emb = np.concatenate([freqs, freqs], axis=-1)
    cos = np.cos(emb).astype(np.float32)
    sin = np.sin(emb).astype(np.float32)
    c_half = cos[:, 0::2]
    s_half = sin[:, 0::2]
    CC1 = np.zeros((T, HD), np.float32)
    SS1 = np.zeros((T, HD), np.float32)
    CC1[:, 0::2] = c_half
    CC1[:, 1::2] = c_half
    SS1[:, 0::2] = -s_half
    SS1[:, 1::2] = s_half
    CC = np.tile(CC1, (1, H)).astype(ml_dtypes.bfloat16)
    SS = np.tile(SS1, (1, H)).astype(ml_dtypes.bfloat16)

    tril_u8 = np.tril(np.ones((128, 128), np.uint8))
    triu_u8 = (1 - tril_u8).astype(np.uint8)
    neg240 = np.full((128, 128), -240.0, ml_dtypes.bfloat16)
    sr = np.full((SINK, T), 240.0, np.float32)
    for s in range(SINK):
        sr[s, :s] = -240.0
    sinkrow = sr.astype(ml_dtypes.bfloat16)
    iota64 = np.broadcast_to(np.arange(NSORT, dtype=np.float32),
                             (128, NSORT)).copy()
    iota_t = np.arange(T, dtype=np.float32).reshape(NT, 128).T.copy()
    idn_bf16 = np.eye(128, dtype=ml_dtypes.bfloat16)
    idn_f32 = np.eye(128, dtype=np.float32)
    ones1_bf16 = np.ones((1, 128), ml_dtypes.bfloat16)
    ones1_f32 = np.ones((1, 128), np.float32)
    ones64 = np.ones((128, 64), ml_dtypes.bfloat16)
    d0 = np.zeros((128, TCA), np.uint8)
    d1 = np.zeros((128, TCA), np.uint8)
    for sl in range(128):
        d0[sl, :] = (np.arange(TCA) < sl)
        d1[sl, :] = (np.arange(TCA) < sl + 128)
    neg240w = np.full((128, TCA), -240.0, ml_dtypes.bfloat16)
    bands = {}
    for name, delta in [("band_m", -128), ("band_0", 0), ("band_p", 128)]:
        sl = np.arange(128)[:, None]
        tl = np.arange(TCA)[None, :]
        dd = sl - tl + delta
        keep = ~((dd >= -16) & (dd <= 0))
        bands[name] = keep.astype(ml_dtypes.bfloat16)
    return dict(cc=CC, ss=SS, tril=tril_u8, triu=triu_u8,
                neg240=neg240, sinkrow=sinkrow,
                iota64=iota64, iota_t=iota_t, idn=idn_bf16, idnf=idn_f32,
                ones1=ones1_bf16, ones1f=ones1_f32, ones64=ones64,
                dm0=d0, dm1=d1, neg240w=neg240w,
                cvals=np.broadcast_to(np.array(
                    [1e-5, -30.0, 0.5, 1.0, 0.0, 0.0, 0.0, 0.0],
                    np.float32), (128, 8)).copy(), **bands)


_CONSTS = None


def host_consts():
    global _CONSTS
    if _CONSTS is None:
        _CONSTS = _host_consts()
    return _CONSTS


def build_nc(dbg=(), nphase=99):
    import os
    nphase = int(os.environ.get("NPHASE", nphase))
    p2sub = int(os.environ.get("P2SUB", 9))
    nosel = bool(int(os.environ.get("P5NOSEL", 0)))
    nogo = bool(int(os.environ.get("P5NOGO", 0)))
    ntca = int(os.environ.get("P5NTC", NTCA))
    ncc5 = int(os.environ.get("P5NCC", NC))
    dbg = set(dbg)
    nc = bass.Bass("TRN2", target_bir_lowering=False)
    d = {}

    def din(name, shape, dtype=F32):
        d[name] = nc.dram_tensor(name, list(shape), dtype,
                                 kind="ExternalInput").ap()

    din("x", [T, C])
    din("w_iq", [C, 128]); din("w_ik", [C, 128]); din("w_iw", [C, G])
    din("gate_bias", [1, G]); din("head_bias", [1, H]); din("var_ema", [1, 1])
    for w in ["w_q", "w_k", "w_v", "w_gv", "w_go", "w_o"]:
        din(w, [C, C])
    din("gate_w", [EMOE, C])
    din("ew1", [EMOE, C, HMOE]); din("ew2", [EMOE, HMOE, C])
    din("ln_g", [C]); din("ln_b", [C])
    din("inw", [3 * C, C]); din("inb", [3 * C])
    din("outw", [C, C]); din("outb", [C])
    din("rhs", [C, C])
    din("cc", [T, C], BF16); din("ss", [T, C], BF16)
    din("tril", [128, 128], U8); din("triu", [128, 128], U8)
    din("neg240", [128, 128], BF16)
    din("sinkrow", [SINK, T], BF16)
    din("iota64", [128, NSORT]); din("iota_t", [128, NT])
    din("idn", [128, 128], BF16); din("idnf", [128, 128])
    din("ones1", [1, 128], BF16); din("ones1f", [1, 128])
    din("ones64", [128, 64], BF16)
    din("band_m", [128, TCA], BF16); din("band_0", [128, TCA], BF16)
    din("band_p", [128, TCA], BF16)
    din("cvals", [128, 8])
    din("dm0", [128, TCA], U8); din("dm1", [128, TCA], U8)
    din("neg240w", [128, TCA], BF16)

    y = nc.dram_tensor("y", [T, C], F32, kind="ExternalOutput").ap()
    x1_hbm = nc.dram_tensor("x1_hbm", [T, C], F32, kind="Internal").ap()
    th_hbm = nc.dram_tensor("th_hbm", [16, T], BF16, kind="Internal").ap()
    oh_hbm = nc.dram_tensor("oh_hbm", [EMOE, T], BF16, kind="Internal").ap()
    s_hbm = nc.dram_tensor("s_hbm", [C, T], F32, kind="Internal").ap()
    qxn_hbm = nc.dram_tensor("qxn_hbm", [C, T], BF16, kind="Internal").ap()

    taps = {}

    st_ctx = ExitStack()
    with st_ctx as stx:
        en = stx.enter_context

        def sb(name, shape, dtype=F32):
            return en(nc.sbuf_tensor(name, list(shape), dtype))

        def ps(name, shape, dtype=F32):
            return en(nc.psum_tensor(name, list(shape), dtype))

        sems = {}

        def sem(name):
            if name not in sems:
                sems[name] = en(nc.semaphore(name))
            return sems[name]

        xn = [sb(f"xn{c}", [128, T], BF16) for c in range(NC)]
        qi = sb("qi", [128, T], BF16)
        ki = sb("ki", [128, T], BF16)
        PL = [sb(f"pl{i}", [128, 4096]) for i in range(6)]
        wa = sb("wa", [128, 4096])
        wb = sb("wb", [128, 4096])
        bscr = sb("bscr", [128, 3072])

        def bf(t, i=0, n=2048):
            return t.ap().bitcast(BF16)[:, i * 2048:i * 2048 + n]

        wtok = sb("wtok", [128, NT * G])
        theta = sb("theta", [128, NT * H])
        sortv = sb("sortv", [128, NSORT * G])
        onehotT = sb("onehotT", [EMOE, T], BF16)
        factor_row = sb("factor_row", [1, T], BF16)
        lsc_row = sb("lsc_row", [1, T], BF16)
        amax_col = sb("amax_col", [128, NT])
        gb_bc = sb("gb_bc", [128, G])
        sbh = sb("sbh", [128, H])
        c512av = sb("c512av", [128, 1])
        iota_t_sb = sb("iota_t_sb", [128, NT])
        iota64_sb = sb("iota64_sb", [128, NSORT])
        triu_sb = sb("triu_sb", [128, 128], U8)
        sinkrow_sb = sb("sinkrow_sb", [SINK, T], BF16)
        idn_sb = sb("idn_sb", [128, 128], BF16)
        idnf_sb = sb("idnf_sb", [128, 128])
        ones1_sb = sb("ones1_sb", [1, 128], BF16)
        ones1f_sb = sb("ones1f_sb", [1, 128])
        ones64_sb = sb("ones64_sb", [128, 64], BF16)
        band_sb = {nm: sb(nm + "_sb", [128, TCA], BF16)
                   for nm in ["band_m", "band_0", "band_p"]}
        cvals = sb("cvals_sb", [128, 8])
        dm0_sb = sb("dm0_sb", [128, TCA], U8)
        dm1_sb = sb("dm1_sb", [128, TCA], U8)
        neg240w_sb = sb("neg240w_sb", [128, TCA], BF16)
        small = sb("small", [128, 64])
        small2 = sb("small2", [128, 64])
        scr8 = sb("scr8", [128, 8], BF16)
        ln_g_sb = sb("ln_g_sb", [128, NC])
        ln_b_sb = sb("ln_b_sb", [128, NC])
        inb_sb = sb("inb_sb", [128, 24])
        outb_sb = sb("outb_sb", [128, NC])

        pbig = ps("pbig", [128, 2048])
        pa = ps("pa", [128, 512])
        pb = ps("pb", [128, 512])
        ptr = ps("ptr", [128, 1024], BF16)
        ptr2 = ps("ptr2", [128, 1024], BF16)

        block = en(nc.Block())

        LD = sem("LD")
        nld = [0]

        @block.sync
        def _(sync):
            loads = [
                (gb_bc[0:1, :], d["gate_bias"]), (sbh[0:1, :], d["head_bias"]),
                (c512av[0:1, :], d["var_ema"]), (iota_t_sb[:], d["iota_t"]),
                (iota64_sb[:], d["iota64"]),
                (triu_sb[:], d["triu"]),
                (sinkrow_sb[:], d["sinkrow"]),
                (idn_sb[:], d["idn"]), (idnf_sb[:], d["idnf"]),
                (ones1_sb[:], d["ones1"]), (ones1f_sb[:], d["ones1f"]),
                (ones64_sb[:], d["ones64"]),
                (band_sb["band_m"][:], d["band_m"]),
                (band_sb["band_0"][:], d["band_0"]),
                (band_sb["band_p"][:], d["band_p"]),
                (cvals[:], d["cvals"]),
                (dm0_sb[:], d["dm0"]), (dm1_sb[:], d["dm1"]),
                (neg240w_sb[:], d["neg240w"]),
            ]
            for dst, src in loads:
                sync.dma_start(out=dst, in_=src).then_inc(LD, 16)
                nld[0] += 1
            for nm, dst in [("ln_g", ln_g_sb), ("ln_b", ln_b_sb),
                            ("outb", outb_sb), ("inb", inb_sb)]:
                sync.dma_start(out=dst[:],
                               in_=d[nm].rearrange("(c p) -> p c", p=128),
                               allow_slow_non_contiguous=True).then_inc(LD, 16)
                nld[0] += 1
        NCONST = nld[0]

        P0T = sem("P0T"); P0V = sem("P0V"); P0S = sem("P0S")

        @block.tensor
        def _(tensor):
            tensor.wait_ge(LD, NCONST * 16)
            tensor.matmul(pa[:, 0:G], ones1f_sb[:], gb_bc[0:1, :],
                          start=True, stop=True)
            tensor.matmul(pa[:, 16:16 + H], ones1f_sb[:], sbh[0:1, :],
                          start=True, stop=True)
            tensor.matmul(pa[:, 48:49], ones1f_sb[:], c512av[0:1, :],
                          start=True, stop=True).then_inc(P0T, 1)

        @block.vector
        def _(vector):
            vector.wait_ge(P0T, 1)
            vector.tensor_copy(gb_bc[:], pa[:, 0:G])
            vector.tensor_copy(sbh[:], pa[:, 16:16 + H])
            vector.tensor_scalar(out=small[:, 0:1], in0=pa[:, 48:49],
                                 scalar1=1e-6, scalar2=None,
                                 op0=ALU.max).then_inc(P0V, 1)

        @block.vector
        def _(vector):
            vector.wait_ge(P0V, 1)
            vector.reciprocal(small[:, 1:2], small[:, 0:1]).then_inc(P0V, 1)
            vector.wait_ge(P0V, 2)
            vector.tensor_scalar(out=c512av[:], in0=small[:, 1:2],
                                 scalar1=512.0, scalar2=None,
                                 op0=ALU.mult).then_inc(P0V, 1)

        @block.scalar
        def _(scalar):
            scalar.wait_ge(P0V, 3)
            scalar.activation(out=sbh[:], in_=sbh[:],
                              func=AF.Sigmoid).then_inc(P0S, 1)

        def emit_ln(tag, xsrc_hbm, out_chunks):
            ld_s = sem(tag + "ld")
            v_s = sem(tag + "v")
            s_s = sem(tag + "s")
            t_s = sem(tag + "t")
            e_s = sem(tag + "e")

            @block.sync
            def _(sync):
                for tt in range(NT):
                    sync.wait_ge(ld_s, 16 * tt)
                    if tt >= 2:
                        sync.wait_ge(v_s, 5 * (tt - 2) + 5)
                    dst = wa[:, (tt % 2) * 1024:(tt % 2) * 1024 + 1024]
                    sync.dma_start(out=dst,
                                   in_=xsrc_hbm[tt * 128:(tt + 1) * 128, :]
                                   ).then_inc(ld_s, 16)

            @block.vector
            def _(vector):
                for tt in range(NT):
                    vector.wait_ge(ld_s, 16 * (tt + 1))
                    xt = wa[:, (tt % 2) * 1024:(tt % 2) * 1024 + 1024]
                    vector.tensor_reduce(small[:, 0:1], xt, AX.X,
                                         ALU.add).then_inc(v_s, 1)
                    vector.wait_ge(v_s, 5 * tt + 1)
                    vector.tensor_scalar(out=small[:, 1:2], in0=small[:, 0:1],
                                         scalar1=1.0 / C, scalar2=None,
                                         op0=ALU.mult).then_inc(v_s, 1)
                    vector.wait_ge(s_s, 3 * tt + 2)
                    vector.scalar_tensor_tensor(
                        out=small[:, 4:5], in0=small[:, 2:3], scalar=1.0 / C,
                        in1=small[:, 3:4], op0=ALU.mult,
                        op1=ALU.subtract).then_inc(v_s, 1)
                    vector.wait_ge(s_s, 3 * tt + 3)
                    vector.reciprocal(small[:, 5:6],
                                      small[:, 6:7]).then_inc(v_s, 1)
                    vector.wait_ge(v_s, 5 * tt + 4)
                    if tt >= 1:
                        vector.wait_ge(t_s, tt)
                    vector.tensor_scalar(out=bf(bscr, 0, 1024), in0=xt,
                                         scalar1=small[:, 1:2],
                                         scalar2=small[:, 5:6],
                                         op0=ALU.subtract,
                                         op1=ALU.mult).then_inc(v_s, 1)

            def evac(scalar, j):
                scalar.wait_ge(t_s, j + 1)
                pdst = ptr if j % 2 == 0 else ptr2
                for c in range(NC):
                    ins = scalar.copy(out_chunks[c][:, j * 128:(j + 1) * 128],
                                      pdst[:, c * 128:(c + 1) * 128])
                ins.then_inc(e_s, 1)

            @block.scalar
            def _(scalar):
                for tt in range(NT):
                    if tt >= 1:
                        evac(scalar, tt - 1)
                    scalar.wait_ge(ld_s, 16 * (tt + 1))
                    if tt >= 1:
                        scalar.wait_ge(v_s, 5 * (tt - 1) + 3)
                    xt = wa[:, (tt % 2) * 1024:(tt % 2) * 1024 + 1024]
                    scalar.activation(out=wb[:, 0:1024], in_=xt,
                                      func=AF.Square,
                                      accum_out=small[:, 2:3]).then_inc(s_s, 1)
                    scalar.wait_ge(v_s, 5 * tt + 2)
                    scalar.activation(out=small[:, 3:4], in_=small[:, 1:2],
                                      func=AF.Square).then_inc(s_s, 1)
                    scalar.wait_ge(v_s, 5 * tt + 3)
                    scalar.activation(out=small[:, 6:7], in_=small[:, 4:5],
                                      func=AF.Sqrt, bias=cvals[:, 0:1],
                                      scale=1.0).then_inc(s_s, 1)
                evac(scalar, NT - 1)

            @block.tensor
            def _(tensor):
                for tt in range(NT):
                    tensor.wait_ge(v_s, 5 * tt + 5)
                    if tt >= 2:
                        tensor.wait_ge(e_s, tt - 1)
                    pdst = ptr if tt % 2 == 0 else ptr2
                    for c in range(NC):
                        ins = tensor.transpose(
                            pdst[:, c * 128:(c + 1) * 128],
                            bf(bscr, 0, 1024)[:, c * 128:(c + 1) * 128],
                            idn_sb[:])
                    ins.then_inc(t_s, 1)
            return e_s

        ln0_done = emit_ln("LN0", d["x"], xn)

        emit_p2 = nphase >= 2
        P2ld = sem("P2ld"); P2v = sem("P2v"); P2t = sem("P2t"); P2s = sem("P2s")
        if emit_p2:
            wiq_bf = bf(bscr, 0, 1024)
            wik_bf = bf(bscr, 0, 2048)[:, 1024:2048]
            wiw_bf = bf(bscr, 1, 1024)[:, 0:32]
            gw_bf = bf(bscr, 1, 1024)[:, 64:128]
            gwrow_bf = bf(bscr, 1, 2048)[:, 1024:2048]
            combo_bf = bf(bscr, 2, 1024)[:, 0:64]

            def scr_lg(tt):
                return wb[:, 1024 + (tt % 2) * 16:1024 + (tt % 2) * 16 + EMOE]

            @block.sync
            def _(sync):
                sync.wait_ge(ln0_done, NT)
                sync.dma_start(out=wa[:, 0:1024].rearrange("p (c m) -> p c m", m=128),
                               in_=d["w_iq"].rearrange("(c p) m -> p c m", p=128)
                               ).then_inc(P2ld, 16)
                sync.wait_ge(P2ld, 16)
                sync.dma_start(out=wa[:, 1024:2048].rearrange("p (c m) -> p c m", m=128),
                               in_=d["w_ik"].rearrange("(c p) m -> p c m", p=128)
                               ).then_inc(P2ld, 16)
                sync.wait_ge(P2ld, 32)
                sync.dma_start(out=wa[:, 2048:2080].rearrange("p (c m) -> p c m", m=G),
                               in_=d["w_iw"].rearrange("(c p) m -> p c m", p=128)
                               ).then_inc(P2ld, 16)
                sync.wait_ge(P2ld, 48)
                sync.dma_start(out=wa[0:8, 2560:3584],
                               in_=d["gate_w"]).then_inc(P2ld, 16)

            @block.vector
            def _(vector):
                vector.wait_ge(P2ld, 16)
                vector.tensor_copy(wiq_bf, wa[:, 0:1024]).then_inc(P2v, 1)
                vector.wait_ge(P2ld, 32)
                vector.tensor_copy(wik_bf, wa[:, 1024:2048]).then_inc(P2v, 1)
                vector.wait_ge(P2ld, 48)
                vector.tensor_copy(wiw_bf, wa[:, 2048:2080]).then_inc(P2v, 1)
                vector.wait_ge(P2ld, 64)
                vector.tensor_copy(gwrow_bf[0:8, :], wa[0:8, 2560:3584]
                                   ).then_inc(P2v, 1)

            @block.tensor
            def _(tensor):
                tensor.wait_ge(P2v, 4)
                for c in range(NC):
                    ins = tensor.transpose(ptr[:, c * 8:(c + 1) * 8],
                                           gwrow_bf[0:8, c * 128:(c + 1) * 128],
                                           idn_sb[0:8, 0:8])
                ins.then_inc(P2t, 1)
                for tq in range(4):
                    for c in range(NC):
                        ins = tensor.matmul(pbig[:, tq * 512:(tq + 1) * 512],
                                            wiq_bf[:, c * 128:(c + 1) * 128],
                                            xn[c][:, tq * 512:(tq + 1) * 512],
                                            start=(c == 0), stop=(c == NC - 1))
                ins.then_inc(P2t, 1)
                tensor.wait_ge(P2s, 2)
                for tq in range(4):
                    for c in range(NC):
                        ins = tensor.matmul(pbig[:, tq * 512:(tq + 1) * 512],
                                            wik_bf[:, c * 128:(c + 1) * 128],
                                            xn[c][:, tq * 512:(tq + 1) * 512],
                                            start=(c == 0), stop=(c == NC - 1))
                ins.then_inc(P2t, 2)
                for tt in range(NT if p2sub >= 3 else 0):
                    if tt >= 1:
                        tensor.wait_ge(P2s, 3 + 4 * (tt - 1) + 2)
                    for c in range(NC):
                        tensor.matmul(pa[:, 0:G],
                                      xn[c][:, tt * 128:(tt + 1) * 128],
                                      wiw_bf[:, c * G:(c + 1) * G],
                                      start=(c == 0), stop=(c == NC - 1))
                    for c in range(NC):
                        ins = tensor.matmul(pb[:, 0:EMOE],
                                            xn[c][:, tt * 128:(tt + 1) * 128],
                                            gw_bf[:, c * EMOE:(c + 1) * EMOE],
                                            start=(c == 0), stop=(c == NC - 1))
                    ins.then_inc(P2t, 1)
                    tensor.wait_ge(P2v, 4 + 7 * tt + 7)
                    if tt >= 1:
                        tensor.wait_ge(P2s, 3 + 4 * (tt - 1) + 4)
                    tensor.transpose(ptr2[0:64, 0:128], combo_bf[:, 0:64],
                                     idn_sb[:]).then_inc(P2t, 1)

            @block.scalar
            def _(scalar):
                scalar.wait_ge(P2t, 1)
                for c in range(NC):
                    ins = scalar.copy(gw_bf[:, c * 8:(c + 1) * 8],
                                      ptr[:, c * 8:(c + 1) * 8])
                ins.then_inc(P2s, 1)
                scalar.wait_ge(P2t, 2)
                for tq in range(4):
                    ins = scalar.copy(qi[:, tq * 512:(tq + 1) * 512],
                                      pbig[:, tq * 512:(tq + 1) * 512])
                ins.then_inc(P2s, 1)
                scalar.wait_ge(P2t, 4)
                for tq in range(4):
                    ins = scalar.copy(ki[:, tq * 512:(tq + 1) * 512],
                                      pbig[:, tq * 512:(tq + 1) * 512])
                ins.then_inc(P2s, 1)
                for tt in range(NT if p2sub >= 3 else 0):
                    base_s = 3 + 4 * tt
                    scalar.wait_ge(P2t, 5 + 2 * tt)
                    scalar.activation(out=wtok[:, G * tt:G * (tt + 1)],
                                      in_=pa[:, 0:G],
                                      func=AF.Sigmoid).then_inc(P2s, 1)
                    lg = scr_lg(tt)
                    scalar.copy(lg, pb[:, 0:EMOE]).then_inc(P2s, 1)
                    scalar.wait_ge(P2v, 4 + 7 * tt + 2)
                    scalar.activation(out=wb[:, 0:EMOE], in_=lg, func=AF.Exp,
                                      bias=small[:, 1:2],
                                      accum_out=small[:, 2:3]).then_inc(P2s, 1)
                    scalar.wait_ge(P2t, 6 + 2 * tt)
                    scalar.copy(onehotT[:, tt * 128:(tt + 1) * 128],
                                ptr2[0:EMOE, 0:128])
                    scalar.copy(factor_row[:, tt * 128:(tt + 1) * 128],
                                ptr2[32:33, 0:128]).then_inc(P2s, 1)

            @block.vector
            def _(vector):
                for tt in range(NT if p2sub >= 3 else 0):
                    base_s = 3 + 4 * tt
                    vector.wait_ge(P2s, base_s + 2)
                    lg = scr_lg(tt)
                    vector.tensor_reduce(small[:, 0:1], lg, AX.X,
                                         ALU.max).then_inc(P2v, 1)
                    vector.wait_ge(P2v, 4 + 7 * tt + 1)
                    vector.tensor_scalar(out=small[:, 1:2], in0=small[:, 0:1],
                                         scalar1=-1.0, scalar2=None,
                                         op0=ALU.mult).then_inc(P2v, 1)
                    vector.wait_ge(P2s, base_s + 3)
                    vector.reciprocal(small[:, 3:4],
                                      small[:, 2:3]).then_inc(P2v, 1)
                    vector.wait_ge(P2v, 4 + 7 * tt + 3)
                    vector.tensor_scalar(out=small[:, 4:5], in0=small[:, 3:4],
                                         scalar1=1e-8, scalar2=None,
                                         op0=ALU.add).then_inc(P2v, 1)
                    vector.wait_ge(P2v, 4 + 7 * tt + 4)
                    vector.reciprocal(small[:, 5:6],
                                      small[:, 4:5]).then_inc(P2v, 1)
                    vector.wait_ge(P2v, 4 + 7 * tt + 5)
                    vector.tensor_mul(small[:, 6:7], small[:, 3:4], small[:, 5:6])
                    vector.tensor_scalar(out=small2[:, 0:EMOE], in0=lg,
                                         scalar1=small[:, 0:1], scalar2=None,
                                         op0=ALU.is_ge).then_inc(P2v, 1)
                    vector.wait_ge(P2v, 4 + 7 * tt + 6)
                    vector.tensor_copy(combo_bf[:, 0:8], small2[:, 0:8])
                    vector.tensor_copy(combo_bf[:, 32:33],
                                       small[:, 6:7]).then_inc(P2v, 1)

        if nphase >= 3:
            P3t = sem("P3t"); P3s = sem("P3s"); P3v = sem("P3v")
            VPT = 139
            lts_t = [bf(PL[0], g) for g in range(G)]
            mg_scr = bf(wa, 0)
            extA = bf(bscr, 0)
            extB = bf(bscr, 1)
            p2_end = (3 + 4 * (NT if p2sub >= 3 else 0)) if emit_p2 else 0

            def vmark(tt, k):
                return VPT * tt + k

            @block.tensor
            def _(tensor):
                for tt in range(NT):
                    EXT = 128 * (tt + 1)
                    NSCc = (EXT + 511) // 512
                    for g in range(G):
                        if tt == 0 and g == 0 and emit_p2:
                            tensor.wait_ge(P2s, p2_end)
                        tensor.wait_ge(P3s, 12 * tt + g)
                        if g == 0 and tt >= 1:
                            tensor.wait_ge(P3v, VPT * (tt - 1) + 72)
                        for sc in range(NSCc):
                            n = min(512, EXT - 512 * sc)
                            ins = tensor.matmul(
                                pbig[:, 512 * sc:512 * sc + n],
                                qi[32 * g:32 * g + 32,
                                   tt * 128:(tt + 1) * 128],
                                ki[32 * g:32 * g + 32, 512 * sc:512 * sc + n],
                                start=True, stop=True,
                                tile_position=(32 * g, 0))
                        ins.then_inc(P3t, 1)

            @block.scalar
            def _(scalar):
                for tt in range(NT):
                    EXT = 128 * (tt + 1)
                    NSCc = (EXT + 511) // 512
                    for g in range(G):
                        scalar.wait_ge(P3t, 4 * tt + g + 1)
                        for sc in range(NSCc):
                            n = min(512, EXT - 512 * sc)
                            ins = scalar.copy(
                                lts_t[g][:, 512 * sc:512 * sc + n],
                                pbig[:, 512 * sc:512 * sc + n])
                        ins.then_inc(P3s, 1)
                    for g in range(G):
                        scalar.wait_ge(P3v, vmark(tt, 18 * g + 1))
                        scalar.activation(out=mg_scr[:, 0:EXT],
                                          in_=lts_t[g][:, 0:EXT],
                                          func=AF.Sigmoid,
                                          bias=gb_bc[:, g:g + 1],
                                          accum_out=small2[:, 32 + g:33 + g]
                                          ).then_inc(P3s, 1)
                        scalar.wait_ge(P3s, 12 * tt + 5 + 2 * g)
                        scalar.activation(out=bf(wa, 1)[:, 0:EXT],
                                          in_=mg_scr[:, 0:EXT],
                                          func=AF.Square,
                                          accum_out=small2[:, 36 + g:37 + g]
                                          ).then_inc(P3s, 1)

            @block.vector
            def _(vector):
                vc = [0]

                def ch(ins):
                    ins.then_inc(P3v, 1)
                    vc[0] += 1
                    vector.wait_ge(P3v, vc[0])

                for tt in range(NT):
                    EXT = 128 * (tt + 1)
                    for g in range(G):
                        vector.wait_ge(P3s, 12 * tt + g + 1)
                        dslice = lts_t[g][:, tt * 128:(tt + 1) * 128]
                        ch(vector.copy_predicated(out=dslice, mask=triu_sb[:],
                                                  data=neg240w_sb[:, 0:128]))
                        vector.wait_ge(P3s, 12 * tt + 6 + 2 * g)
                        ch(vector.memset(lts_t[g][:, 0:SINK], -240.0))
                        cur = lts_t[g]
                        for r in range(RROUNDS):
                            nxt = extA if r % 2 == 0 else extB
                            ch(vector.max(out=scr8[:], in_=cur[:, 0:EXT]))
                            vector.tensor_copy(
                                sortv[:, NSORT * g + 8 * r:
                                      NSORT * g + 8 * r + 8], scr8[:])
                            ch(vector.match_replace(out=nxt[:, 0:EXT],
                                                    in_to_replace=scr8[:],
                                                    in_values=cur[:, 0:EXT],
                                                    imm_value=-240.0))
                            cur = nxt
                    vector.wait_ge(P3s, 12 * tt + 12)
                    s1c = small2[:, 32:36]
                    s2c = small2[:, 36:40]
                    m1 = small2[:, 40:44]
                    m1sq = small2[:, 44:48]
                    varg = small2[:, 48:52]
                    vs = small[:, 16:32]
                    vs2 = small[:, 32:48]
                    kraw = small2[:, 0:16]
                    ch(vector.tensor_scalar(out=m1, in0=s1c, scalar1=1.0 / T,
                                            scalar2=None, op0=ALU.mult))
                    ch(vector.tensor_mul(m1sq, m1, m1))
                    ch(vector.scalar_tensor_tensor(out=varg, in0=s2c,
                                                   scalar=1.0 / T, in1=m1sq,
                                                   op0=ALU.mult,
                                                   op1=ALU.subtract))
                    for g in range(G):
                        ch(vector.tensor_scalar(
                            out=vs[:, 4 * g:4 * g + 4],
                            in0=sbh[:, 4 * g:4 * g + 4],
                            scalar1=wtok[:, G * tt + g:G * tt + g + 1],
                            scalar2=None, op0=ALU.mult))
                    ch(vector.tensor_mul(vs2, vs, vs))
                    for g in range(G):
                        ch(vector.tensor_scalar(
                            out=kraw[:, 4 * g:4 * g + 4],
                            in0=vs2[:, 4 * g:4 * g + 4],
                            scalar1=varg[:, g:g + 1],
                            scalar2=None, op0=ALU.mult))
                    ch(vector.tensor_scalar(out=kraw, in0=kraw,
                                            scalar1=c512av[:, 0:1],
                                            scalar2=None, op0=ALU.mult))
                    t15 = small[:, 48:49]
                    ch(vector.tensor_scalar(out=t15,
                                            in0=iota_t_sb[:, tt:tt + 1],
                                            scalar1=1.5, scalar2=None,
                                            op0=ALU.add))
                    ch(vector.tensor_scalar(out=kraw, in0=kraw, scalar1=32.0,
                                            scalar2=1024.5, op0=ALU.max,
                                            op1=ALU.min))
                    ch(vector.tensor_scalar(out=kraw, in0=kraw, scalar1=t15,
                                            scalar2=float(NSORT + 4.5),
                                            op0=ALU.min, op1=ALU.min))
                    c2m6 = small[:, 0:16]
                    c2m5 = small2[:, 16:32]
                    ch(vector.tensor_scalar(out=c2m6, in0=kraw, scalar1=-6.0,
                                            scalar2=None, op0=ALU.add))
                    ch(vector.tensor_scalar(out=c2m5, in0=kraw, scalar1=-5.0,
                                            scalar2=None, op0=ALU.add))
                    acc1 = small2[:, 52:53]
                    acc2 = small2[:, 53:54]
                    th = theta[:, H * tt:H * (tt + 1)]
                    for h in range(H):
                        g = h // 4
                        ins = vector.scalar_tensor_tensor(
                            out=mg_scr[:, 0:NSORT], in0=iota64_sb[:],
                            scalar=c2m6[:, h:h + 1],
                            in1=sortv[:, NSORT * g:NSORT * (g + 1)],
                            op0=ALU.is_gt, op1=ALU.mult, accum_out=acc1)
                        ins.then_inc(P3v, 1); vc[0] += 1
                        ins = vector.scalar_tensor_tensor(
                            out=mg_scr[:, 64:64 + NSORT], in0=iota64_sb[:],
                            scalar=c2m5[:, h:h + 1],
                            in1=sortv[:, NSORT * g:NSORT * (g + 1)],
                            op0=ALU.is_gt, op1=ALU.mult, accum_out=acc2)
                        ins.then_inc(P3v, 1); vc[0] += 1
                        vector.wait_ge(P3v, vc[0])
                        ch(vector.tensor_sub(th[:, h:h + 1], acc1, acc2))
                    ch(vector.tensor_scalar(out=th, in0=th, scalar1=-200.0,
                                            scalar2=None, op0=ALU.max))
                    assert vc[0] == VPT * (tt + 1), (tt, vc[0])

        if nphase >= 4:
            qf_t = [bf(PL[1], i) for i in range(4)] + [bf(PL[2], i - 4) for i in range(4, NC)]
            kf_t = [bf(PL[3], i) for i in range(4)] + [bf(PL[4], i - 4) for i in range(4, NC)]
            vtok_t = ([bf(PL[5], 0, 8192)[:, i * 1024:(i + 1) * 1024] for i in range(8)]
                      + [bf(PL[0], 0, 8192)[:, i * 1024:(i + 1) * 1024] for i in range(8)])
            wbig = bf(wa, 0, 8192)
            wbig2 = bf(wb, 0, 8192)
            bv = bf(bscr, 0, 6144)
            p3_end = 139 * NT if nphase >= 3 else 0

            def emit_wload(tag, w_hbm, dst_view, stage, nchunks=8, pre=None,
                           pre_v=None, nslots=4):
                lw = sem(tag + "lw")
                cw = sem(tag + "cw")

                @block.sync
                def _(sync):
                    if pre is not None:
                        sync.wait_ge(pre[0], pre[1])
                    for c in range(nchunks):
                        sync.wait_ge(lw, 16 * c)
                        if c >= nslots:
                            sync.wait_ge(cw, c - nslots + 1)
                        sync.dma_start(out=stage(c % nslots),
                                       in_=w_hbm[c * 128:(c + 1) * 128, :]
                                       ).then_inc(lw, 16)

                @block.vector
                def _(vector):
                    if pre_v is not None:
                        vector.wait_ge(pre_v[0], pre_v[1])
                    for c in range(nchunks):
                        vector.wait_ge(lw, 16 * (c + 1))
                        vector.tensor_copy(
                            dst_view[:, c * 1024:(c + 1) * 1024],
                            stage(c % nslots)).then_inc(cw, 1)
                return cw

            stage_wb = lambda p: wb[:, p * 1024:p * 1024 + 1024]
            stage_bs = lambda p: bscr[:, p * 1024:p * 1024 + 1024]

            def emit_qk_pass(tag, w_hbm, out_chunks, first, prev=None):
                cw = emit_wload(tag, w_hbm, wbig, stage_wb, nslots=3,
                                pre=(P3v, p3_end) if first and nphase >= 3
                                else (sems["P4qcw"], NC))
                t_s = sem(tag + "t")
                s_s = sem(tag + "s")
                v_s = sem(tag + "v")
                ld2 = sem(tag + "ld2")
                ccb = lambda p: bv[:, p * 2048:p * 2048 + 1024]
                ssb = lambda p: bv[:, p * 2048 + 1024:p * 2048 + 2048]
                qtok = bv[:, 4096:5120]
                rot = bv[:, 5120:6144]

                @block.sync
                def _(sync):
                    if first and nphase >= 3:
                        sync.wait_ge(P3v, p3_end)
                    for tt in range(NT):
                        sync.wait_ge(ld2, 32 * tt)
                        if tt >= 2:
                            sync.wait_ge(v_s, 3 * (tt - 2) + 3)
                        sync.dma_start(out=ccb(tt % 2),
                                       in_=d["cc"][tt * 128:(tt + 1) * 128, :]
                                       ).then_inc(ld2, 16)
                        sync.dma_start(out=ssb(tt % 2),
                                       in_=d["ss"][tt * 128:(tt + 1) * 128, :]
                                       ).then_inc(ld2, 16)

                @block.tensor
                def _(tensor):
                    tensor.wait_ge(cw, NC)
                    if first and nphase >= 3:
                        tensor.wait_ge(P3v, p3_end)
                    if prev is not None:
                        tensor.wait_ge(prev, 2 * NT)
                    for tt in range(NT):
                        tensor.wait_ge(s_s, 2 * tt)
                        for c in range(NC):
                            for half in range(2):
                                ins = tensor.matmul(
                                    pbig[:, half * 512:half * 512 + 512],
                                    xn[c][:, tt * 128:(tt + 1) * 128],
                                    wbig[:, c * 1024 + half * 512:
                                         c * 1024 + half * 512 + 512],
                                    start=(c == 0), stop=(c == NC - 1))
                        ins.then_inc(t_s, 1)
                        tensor.wait_ge(v_s, 3 * tt + 3)
                        if tt >= 2:
                            tensor.wait_ge(s_s, 2 * (tt - 1))
                        pdst = ptr if tt % 2 == 0 else ptr2
                        for c in range(NC):
                            ins = tensor.transpose(
                                pdst[:, c * 128:(c + 1) * 128],
                                rot[:, c * 128:(c + 1) * 128], idn_sb[:])
                        ins.then_inc(t_s, 1)

                @block.scalar
                def _(scalar):
                    for tt in range(NT):
                        scalar.wait_ge(t_s, 2 * tt + 1)
                        if tt >= 1:
                            scalar.wait_ge(v_s, 3 * tt)
                        scalar.copy(qtok[:, 0:512], pbig[:, 0:512])
                        scalar.copy(qtok[:, 512:1024],
                                    pbig[:, 512:1024]).then_inc(s_s, 1)
                        scalar.wait_ge(t_s, 2 * tt + 2)
                        pdst = ptr if tt % 2 == 0 else ptr2
                        for c in range(NC):
                            ins = scalar.copy(
                                out_chunks[c][:, tt * 128:(tt + 1) * 128],
                                pdst[:, c * 128:(c + 1) * 128])
                        ins.then_inc(s_s, 1)

                @block.vector
                def _(vector):
                    q3 = qtok.rearrange("p (a b) -> p a b", b=2)
                    t2 = wbig2[:, 6144:7168]
                    t23 = t2.rearrange("p (a b) -> p a b", b=2)
                    for tt in range(NT):
                        vector.wait_ge(s_s, 2 * tt + 1)
                        vector.wait_ge(ld2, 32 * (tt + 1))
                        ss3 = ssb(tt % 2).rearrange("p (a b) -> p a b", b=2)
                        vector.tensor_tensor(rot[:], qtok[:], ccb(tt % 2),
                                             ALU.mult).then_inc(v_s, 1)
                        vector.tensor_tensor(t23, q3[:, :, ::-1], ss3,
                                             ALU.mult).then_inc(v_s, 1)
                        vector.wait_ge(v_s, 3 * tt + 2)
                        vector.tensor_add(rot[:], rot[:],
                                          t2).then_inc(v_s, 1)

            emit_qk_pass("P4q", d["w_q"], qf_t, True)
            P4qt = sems["P4qt"]

            @block.vector
            def _(vector):
                vector.wait_ge(P4qt, 2 * NT)

            emit_qk_pass("P4k", d["w_k"], kf_t, False, prev=sems["P4qs"])
            P4kt = sems["P4kt"]

            @block.vector
            def _(vector):
                vector.wait_ge(P4kt, 2 * NT)

            cwv = emit_wload("P4wv", d["w_v"], wbig, stage_wb, nslots=3,
                             pre=(sems["P4kcw"], NC))
            cwg = emit_wload("P4wg", d["w_gv"], wbig2, stage_bs, nslots=3,
                             pre=(sems["P4kv"], 3 * NT),
                             pre_v=(sems["P4wvcw"], NC))

            P4vt = sem("P4vt"); P4vs = sem("P4vs"); P4vv = sem("P4vv")

            @block.tensor
            def _(tensor):
                tensor.wait_ge(cwv, NC)
                tensor.wait_ge(cwg, NC)
                tensor.wait_ge(sems["P4ks"], 2 * NT)
                for tt in range(NT):
                    tensor.wait_ge(P4vs, tt)
                    for c in range(NC):
                        for half in range(2):
                            tensor.matmul(
                                pbig[:, half * 512:half * 512 + 512],
                                xn[c][:, tt * 128:(tt + 1) * 128],
                                wbig[:, c * 1024 + half * 512:
                                     c * 1024 + half * 512 + 512],
                                start=(c == 0), stop=(c == NC - 1))
                        for half in range(2):
                            ins = tensor.matmul(
                                pbig[:, 1024 + half * 512:1024 + half * 512 + 512],
                                xn[c][:, tt * 128:(tt + 1) * 128],
                                wbig2[:, c * 1024 + half * 512:
                                      c * 1024 + half * 512 + 512],
                                start=(c == 0), stop=(c == NC - 1))
                    ins.then_inc(P4vt, 1)

            @block.scalar
            def _(scalar):
                va = bv[:, 4096:5120]
                gs = bv[:, 5120:6144]
                for tt in range(NT):
                    scalar.wait_ge(P4vt, tt + 1)
                    if tt >= 1:
                        scalar.wait_ge(P4vv, tt)
                    scalar.copy(va[:, 0:512], pbig[:, 0:512])
                    scalar.copy(va[:, 512:1024], pbig[:, 512:1024])
                    scalar.activation(out=gs[:, 0:512], in_=pbig[:, 1024:1536],
                                      func=AF.Sigmoid)
                    scalar.activation(out=gs[:, 512:1024],
                                      in_=pbig[:, 1536:2048],
                                      func=AF.Sigmoid).then_inc(P4vs, 1)

            @block.vector
            def _(vector):
                va = bv[:, 4096:5120]
                gs = bv[:, 5120:6144]
                for tt in range(NT):
                    vector.wait_ge(P4vs, tt + 1)
                    vector.tensor_tensor(vtok_t[tt], va, gs,
                                         ALU.mult).then_inc(P4vv, 1)


        def mk_sched(tag):
            semset = {e: sem(tag + e) for e in "tsvde"}
            sched = []
            cnts = {e: 0 for e in "tsvde"}

            def em(eng, fn, deps=(), extra=(), par=False):
                cnts[eng] += 1
                flat = []
                for dd in deps:
                    if dd is None:
                        continue
                    if isinstance(dd, list):
                        flat.extend(x for x in dd if x is not None)
                    else:
                        flat.append(dd)
                assert all(isinstance(x, dict) for x in flat), flat
                it = dict(eng=eng, fn=fn, deps=flat,
                          extra=list(extra), val=cnts[eng], par=par)
                sched.append(it)
                return it

            def replay_all():
                gend = {}
                for lane in "de":
                    dit = [it for it in sched if it["eng"] == lane]
                    g2 = {}
                    i = 0
                    while i < len(dit):
                        j = i
                        while j + 1 < len(dit) and dit[j + 1]["par"]:
                            j += 1
                        for k2 in range(i, j + 1):
                            g2[dit[k2]["val"]] = dit[j]["val"]
                        i = j + 1
                    gend[lane] = g2

                def replay(eng_key):
                    def run(e):
                        for it in sched:
                            if it["eng"] != eng_key:
                                continue
                            if eng_key in "de" and not it["par"]:
                                e.wait_ge(semset[eng_key],
                                          (it["val"] - 1) * 16)
                            for dd in it["deps"]:
                                e.wait_ge(semset[dd["eng"]],
                                          gend[dd["eng"]][dd["val"]] * 16
                                          if dd["eng"] in "de"
                                          else dd["val"])
                            for sm, vv in it["extra"]:
                                e.wait_ge(sems[sm] if isinstance(sm, str)
                                          else sm, vv)
                            ins = it["fn"](e)
                            ins.then_inc(semset[eng_key],
                                         16 if eng_key in "de" else 1)
                    return run
                block.tensor(replay("t"))
                block.scalar(replay("s"))
                block.vector(replay("v"))
                block.gpsimd(replay("d"))
                if cnts["e"]:
                    block.sync(replay("e"))
            return semset, sched, cnts, em, replay_all

        if nphase >= 5:
            throws = sb("throws", [16, T], BF16)
            P5, sched, cnt5, em, replay_all5 = mk_sched("P5")
            scores_ps = ptr.ap().bitcast(F32)
            lst_ps = ptr2.ap().bitcast(F32)
            bv5 = bf(bscr, 0, 6144)
            thbc = lambda h: bv5[:, 256 * h:256 * h + 256]
            s0pre = lambda k: bv5[:, 256 * k:256 * k + 256]
            lstb = lambda g: bv5[:, 4096 + 256 * g:4096 + 256 * g + 256]
            e_sl = lambda i: bv5[:, 5120 + 256 * i:5120 + 256 * i + 256]
            sel_sl = lambda i: bv5[:, 5632 + 256 * i:5632 + 256 * i + 256]
            s0stage = lambda i: bscr[:, 2048 + 256 * i:2048 + 256 * i + 256]
            rsum_bc = sortv[:]
            cw_go = emit_wload("P5go", d["w_go"], wbig, stage_wb,
                               pre=(sems["P4vv"], NT),
                               pre_v=(sems["P4vt"], NT))
            cw_o = emit_wload("P5o", d["w_o"], wbig2, stage_bs,
                              pre=(cw_go, NC), pre_v=(cw_go, NC), nslots=3)

            pro_prev = None
            for tt in range(NT):
                itt = em("t", (lambda tt: lambda e: e.transpose(
                    lst_ps[0:16, 0:128],
                    theta[:, 16 * tt:16 * tt + 16], idnf_sb[:]))(tt),
                    deps=[pro_prev],
                    extra=[(P3v, 139 * NT), (sems["P4ks"], 2 * NT)]
                    if tt == 0 else [])
                pro_prev = em("s", (lambda tt: lambda e: e.copy(
                    throws[:, tt * 128:(tt + 1) * 128],
                    lst_ps[0:16, 0:128]))(tt), deps=[itt])
            pro_last = pro_prev
            pro_st = em("d", (lambda: lambda e: e.dma_start(
                out=th_hbm, in_=throws[:]))(), deps=[pro_last])

            prev_epi = pro_last
            bankA = lambda: pbig[0:64, 0:256]
            bankB = lambda: pbig[64:128, 512:768]
            bank2 = lambda sl: pbig[:, 1024 + 256 * sl:1024 + 256 * sl + 256]
            bank3 = pbig[:, 1536:1792]
            last_wo_ev = None

            def th_load_grp(tcc2, grp, lead_deps, extra=()):
                items = []
                for k2 in range(8):
                    h = 8 * grp + k2
                    stg = lsc_row[0:1, 256 * k2:256 * k2 + 256]
                    itd = em("d", (lambda h=h, stg=stg, tcc2=tcc2: lambda e:
                                   e.dma_start(
                        out=stg,
                        in_=th_hbm[h:h + 1,
                                   256 * tcc2:256 * tcc2 + 256]))(),
                        deps=lead_deps if k2 == 0 else [],
                        extra=extra if k2 == 0 else [])
                    items.append(itd)
                return items

            def emit_tc(tcc, prev_epi, prev):
                tsl = slice(tcc * 256, tcc * 256 + 256)
                nss = 2 * tcc + 2
                th_evs = []
                th_mms = []
                for h in range(H):
                    slot = bank2(h % 2)
                    stg = lsc_row[0:1, 256 * (h % 8):256 * (h % 8) + 256]
                    if h % 8 == 0:
                        deps = [prev["th15"] if h == 0 else th_mms[7],
                                pro_st if (h == 0 and tcc == 0) else None]
                    else:
                        deps = []
                    itd = em("d", (lambda h, stg, tcc=tcc: lambda e:
                                   e.dma_start(
                        out=stg, in_=th_hbm[h:h + 1, tsl]))(h, stg),
                        deps=deps,
                        extra=[("P5ocw", NC)] if (tcc == 0 and h == 0)
                        else [])
                    itm = em("t", (lambda h, slot, stg: lambda e: e.matmul(
                        slot, ones1_sb[:], stg,
                        start=True, stop=True))(h, slot, stg),
                        deps=[itd] + ([th_evs[h - 2]] if h >= 2 else
                                      [prev["its0_5"] if h == 0 else None]))
                    th_mms.append(itm)
                    ite = em("s", (lambda h, slot: lambda e: e.copy(
                        thbc(h), slot))(h, slot), deps=[itm])
                    th_evs.append(ite)
                nxt_thld0 = None
                s0_done = []
                pair_done = [None]

                def emit_pair(cc):
                    h1 = 2 * cc
                    g = cc // 2
                    pp = cc % 2
                    last_sums = None
                    lev = None
                    for ss in range(nss):
                        sl = 2 * pp + (ss % 2)
                        lpsl = lst_ps[:, 256 * pp:256 * pp + 256]
                        itm = em("t", (lambda g=g, ss=ss, lpsl=lpsl:
                                       lambda e: e.matmul(
                            lpsl,
                            ki[32 * g:32 * g + 32, ss * 128:(ss + 1) * 128],
                            qi[32 * g:32 * g + 32, tsl],
                            start=True, stop=True,
                            tile_position=(32 * g, 0)))(),
                            deps=[lst_ps_free[pp]])
                        itv = em("s", (lambda sl=sl, lpsl=lpsl: lambda e:
                                       e.copy(lstbS(sl), lpsl))(),
                            deps=[itm, lstb_free[sl]])
                        lst_ps_free[pp] = itv
                        lev = itv
                        if ss >= 2 * tcc:
                            dmsk = dm0_sb if ss == 2 * tcc else dm1_sb
                            lev = em("v", (lambda dmsk=dmsk, sl=sl: lambda e:
                                e.copy_predicated(out=lstbS(sl),
                                                  mask=dmsk[:],
                                                  data=neg240w_sb[:]))(),
                                deps=[lev])
                        if ss == 0:
                            lev = em("v", (lambda sl=sl: lambda e:
                                e.tensor_copy(lstbS(sl)[0:SINK, :],
                                              sinkrow_sb[:, tsl]))(),
                                deps=[lev])
                        for hh in range(2):
                            h = h1 + hh
                            esl = e_slot(hh, pp)
                            eslot_id = hh + 2 * pp
                            ssl = scores_ps[:, 256 * hh:256 * hh + 256]
                            itm2 = em("t", (lambda h=h, ss=ss, ssl=ssl:
                                            lambda e: e.matmul(
                                ssl,
                                kf_t[h // 2][64 * (h % 2):64 * (h % 2) + 64,
                                             ss * 128:(ss + 1) * 128],
                                qf_t[h // 2][64 * (h % 2):64 * (h % 2) + 64,
                                             tsl],
                                start=True, stop=True,
                                tile_position=(64 * (h % 2), 0)))(),
                                deps=[escore_free[hh]])
                            last_score[0] = itm2
                            ite = em("s", (lambda esl=esl, ssl=ssl:
                                           lambda e: e.activation(
                                out=esl, in_=ssl, func=AF.Exp,
                                scale=0.125))(),
                                deps=[itm2, e_sl_free[eslot_id]])
                            escore_free[hh] = ite
                            if nosel:
                                itv2 = em("v", (lambda hh=hh, esl=esl:
                                                lambda e: e.tensor_copy(
                                    sel_sl(hh), esl))(),
                                    deps=[ite, th_evs[h], lev])
                            else:
                                itv1 = em("v", (lambda hh=hh, h=h, sl=sl:
                                                lambda e: e.tensor_tensor(
                                    sel_sl(hh), lstbS(sl), thbc(h),
                                    ALU.is_ge))(),
                                    deps=[ite, th_evs[h], lev])
                                if hh == 1:
                                    lstb_free[sl] = itv1
                                itv2 = em("v", (lambda hh=hh, esl=esl:
                                                lambda e: e.tensor_tensor(
                                    esl, esl, sel_sl(hh), ALU.mult))(),
                                    deps=[itv1])
                            bk = bankA() if hh == 0 else bankB()
                            av_deps = [itv2]
                            if ss == 0:
                                av_deps.append(pair_done[0] if pair_done[0]
                                               is not None else
                                               (th_evs[-1] if cc == 0
                                                else None))
                            itav = em("t", (lambda hh=hh, ss=ss, nss=nss,
                                            bk=bk, esl=esl, h1=h1:
                                            lambda e: e.matmul(
                                bk, vtok_t[ss][:, 64 * (h1 + hh):
                                               64 * (h1 + hh) + 64],
                                esl, start=(ss == 0),
                                stop=(ss == nss - 1),
                                tile_position=(0, 64 * hh)))(),
                                deps=av_deps)
                            su = pa[0:64, 0:256] if hh == 0 \
                                else pb[64:128, 0:256]
                            itsu = em("t", (lambda hh=hh, ss=ss, nss=nss,
                                            su=su, esl=esl:
                                            lambda e: e.matmul(
                                su, ones64_sb[:], esl,
                                start=(ss == 0),
                                stop=(ss == nss - 1),
                                tile_position=(0, 64 * hh)))(),
                                deps=[])
                            e_sl_free[eslot_id] = itsu
                            last_sums = itsu
                    def mk_rs(e):
                        e.reciprocal(rsum_bc[0:64, :], pa[0:64, 0:256])
                        return e.reciprocal(rsum_bc[64:128, :],
                                            pb[64:128, 0:256])
                    itr = em("v", mk_rs,
                             deps=[last_sums,
                                   s0_done[-1] if s0_done else None])
                    itr2 = em("v", (lambda cc=cc: lambda e: [
                        e.tensor_tensor(s0pre(cc)[0:64, :], pbig[0:64, 0:256],
                                        rsum_bc[0:64, :], ALU.mult),
                        e.tensor_tensor(s0pre(cc)[64:128, :],
                                        pbig[64:128, 512:768],
                                        rsum_bc[64:128, :], ALU.mult)][-1]
                        )(), deps=[itr, th_evs[-1] if cc == 0 else None])
                    pair_done[0] = itr2
                    s0_done.append(itr2)

                for cc in range(ncc5):
                    emit_pair(cc)
                go_prev = None
                sg_done = []
                for cc in range(ncc5):
                    itg = em("t", (lambda cc: lambda e: [e.matmul(
                        bank3, wbig[:, kk * 1024 + 128 * cc:
                                    kk * 1024 + 128 * cc + 128],
                        xn[kk][:, tsl], start=(kk == 0), stop=(kk == NC - 1))
                        for kk in range(NC)][-1])(cc),
                        deps=[s0_done[-1] if cc == 0 else None,
                              prev["its0_6"] if cc == 0 else None,
                              go_prev],
                        extra=[("P5gocw", NC)] if tcc == 0 and cc == 0 else [])
                    itgs = em("s", (lambda: lambda e: e.activation(
                        out=sel_sl(0), in_=bank3,
                        func=AF.Sigmoid))(),
                        deps=[itg, sg_done[-1] if sg_done else None])
                    go_prev = itgs
                    itp = em("v", (lambda cc: lambda e: e.tensor_copy(
                        sel_sl(1), s0pre(cc)))(cc),
                        deps=[itgs]) if nogo else \
                        em("v", (lambda cc: lambda e: e.tensor_tensor(
                            s0pre(cc), s0pre(cc), sel_sl(0), ALU.mult))(cc),
                           deps=[itgs])
                    sg_done.append(itp)
                wo_list = []
                for mc in range(NC if ncc5 == NC else 0):
                    itwo = em("t", (lambda mc: lambda e: [e.matmul(
                        pbig[:, 256 * mc:256 * mc + 256],
                        wbig2[:, kk * 1024 + 128 * mc:
                              kk * 1024 + 128 * mc + 128],
                        s0pre(kk), start=(kk == 0), stop=(kk == NC - 1))
                        for kk in range(NC)][-1])(mc),
                        deps=(sg_done if mc == 0 else []),
                        extra=[("P5ocw", NC)] if tcc == 0 and mc == 0 else [])
                    wo_list.append(itwo)
                st_prev = None
                its0_l = []
                st_l = []
                for mc in range(NC if ncc5 == NC else 0):
                    its0 = em("s", (lambda mc: lambda e: e.copy(
                        s0stage(mc % 4), pbig[:, 256 * mc:256 * mc + 256])
                        )(mc), deps=[wo_list[mc | 1], st_prev])
                    its0_l.append(its0)
                    itd2 = em("d", (lambda mc, tsl: lambda e: e.dma_start(
                        out=s_hbm[128 * mc:128 * mc + 128, tsl],
                        in_=s0stage(mc % 4)))(mc, tsl), deps=[its0],
                        par=(mc % 4 != 0))
                    st_l.append(itd2)
                    if mc >= 3:
                        st_prev = st_l[mc - 3]
                    last_wo_ev = itd2
                if ncc5 != NC:
                    return sg_done[-1], None
                lstb_free[0] = st_l[4]
                lstb_free[1] = st_l[4]
                lstb_free[2] = st_l[5]
                lstb_free[3] = st_l[5]
                e_sl_free[0] = st_l[6]
                e_sl_free[1] = st_l[6]
                nxt = {"th15": th_mms[15], "its0_5": its0_l[5],
                       "its0_6": its0_l[6], "thld0": nxt_thld0}
                return last_wo_ev, nxt

            last_score = [None]
            lst_ps_free = [pro_last, pro_last]
            lstb_free = [None, None, None, None]
            escore_free = [pro_last, pro_last]
            e_sl_free = [None, None, pro_last, pro_last]
            lstbS = lambda sl: bv5[:, 4096 + 256 * sl:4096 + 256 * sl + 256]
            et_sl = lambda hh: theta.ap().bitcast(BF16)[:, 256 * hh:
                                                        256 * hh + 256]
            e_slot = lambda hh, pp: e_sl(hh) if pp == 0 else et_sl(hh)
            prev = {"th15": None, "its0_5": None, "its0_6": None,
                    "thld0": None}
            for tcc in range(ntca):
                prev_epi, nxt = emit_tc(tcc, prev_epi, prev)
                if nxt is not None:
                    prev = nxt
            replay_all5()

        if nphase >= 6:
            P6, sched6, cnt6, em6, replay6 = mk_sched("P6")
            w1v = [bf(PL[1], 0, 8192), bf(PL[2], 0, 8192)]
            w2v = [bf(PL[3], 0, 8192), bf(PL[4], 0, 8192)]
            g0v = [bf(PL[5], 0, 8192), bf(PL[0], 0, 8192)]
            hem = lambda hc: bf(wa, 0, 8192)[:, 512 * hc:512 * hc + 512]
            mbc = bf(bscr, 0, 2048)
            evsc = lambda i: bf(bscr, 1, 2048)[:, 512 * i:512 * i + 512]

            def w1tile(kk, hc):
                v = w1v[0] if kk < 4 else w1v[1]
                return v[:, (kk % 4) * 2048 + 128 * hc:
                         (kk % 4) * 2048 + 128 * hc + 128]

            def w2tile(hc, c):
                v = w2v[0] if hc < 8 else w2v[1]
                return v[:, (hc % 8) * 1024 + 128 * c:
                         (hc % 8) * 1024 + 128 * c + 128]

            def g0sl(c, t0, n):
                v = g0v[0] if c < 4 else g0v[1]
                return v[:, (c % 4) * 2048 + t0:(c % 4) * 2048 + t0 + n]

            p5_end_t = cnt5["t"]
            p5_end_v = cnt5["v"]
            p5_end_d = cnt5["d"] * 16
            p5_score_t = last_score[0]["val"] if nphase >= 5 else 0

            def emit_we(e, prev_cast, first):
                casts = []
                prev_d = None
                for wi in range(2):
                    nch = NC if wi == 0 else 16
                    for ch in range(nch):
                        stg = wb[:, (len(casts) % 2) * 1024:
                                 (len(casts) % 2) * 1024 + 1024]
                        if wi == 0:
                            src = d["ew1"][e, ch * 128:(ch + 1) * 128, :] \
                                .rearrange("p (a m) -> p a m", a=2)
                            dst = (lambda e2, stg=stg: e2.dma_start(
                                out=stg.rearrange("p (a m) -> p a m", a=2),
                                in_=src))
                        else:
                            src = d["ew2"][e, ch * 128:(ch + 1) * 128, :]
                            dst = (lambda e2, stg=stg, src=src: e2.dma_start(
                                out=stg, in_=src))
                        deps = [prev_d, casts[-2] if len(casts) >= 2 else
                                (prev_cast if first or prev_cast else None)]
                        if first and len(casts) == 0:
                            pass
                        itd = em6("d", dst, deps=deps,
                                  extra=([("P5d", p5_end_d),
                                          ("P5t", p5_end_t)]
                                         if first and len(casts) == 0 else []))
                        prev_d = itd
                        if wi == 0:
                            k2 = ch
                            dv = (lambda e2, k2=k2, stg=stg: e2.tensor_copy(
                                w1tile(k2, 0).tensor if False else
                                (w1v[0] if k2 < 4 else w1v[1])[
                                    :, (k2 % 4) * 2048:(k2 % 4) * 2048 + 2048],
                                stg))
                        else:
                            dv = (lambda e2, ch=ch, stg=stg: e2.tensor_copy(
                                (w2v[0] if ch < 8 else w2v[1])[
                                    :, (ch % 8) * 1024:(ch % 8) * 1024 + 1024],
                                stg))
                        itc = em6("v", dv, deps=[itd],
                                  extra=([("P5v", p5_end_v)]
                                         if first and len(casts) == 0 else []))
                        casts.append(itc)
                return casts

            def emit_we2(e, prev_last, first):
                casts = []
                seq = [("ew1", ch, half) for ch in range(NC)
                       for half in range(2)] + \
                      [("ew2", ch, 0) for ch in range(16)]
                for i, (nm, ch, half) in enumerate(seq):
                    stg = wb[:, (i % 4) * 1024:(i % 4) * 1024 + 1024]
                    if nm == "ew1":
                        src = d["ew1"][e, ch * 128:(ch + 1) * 128,
                                       half * 1024:half * 1024 + 1024]
                        dst_view = (w1v[0] if ch < 4 else w1v[1])[
                            :, (ch % 4) * 2048 + half * 1024:
                            (ch % 4) * 2048 + half * 1024 + 1024]
                    else:
                        src = d["ew2"][e, ch * 128:(ch + 1) * 128, :]
                        dst_view = (w2v[0] if ch < 8 else w2v[1])[
                            :, (ch % 8) * 1024:(ch % 8) * 1024 + 1024]
                    if i % 4 == 0:
                        itd = em6("d", (lambda src=src, stg=stg: lambda e2:
                                        e2.dma_start(out=stg, in_=src))(),
                                  deps=[casts[-1] if casts else prev_last],
                                  extra=([("P5d", p5_end_d),
                                          ("P5t", p5_end_t)]
                                         if first and i == 0 else []))
                    else:
                        itd = em6("d", (lambda src=src, stg=stg: lambda e2:
                                        e2.dma_start(out=stg, in_=src))(),
                                  deps=[], par=True)
                    itc = em6("v", (lambda dst_view=dst_view, stg=stg:
                                    lambda e2: e2.tensor_copy(dst_view, stg)
                                    )(), deps=[itd],
                              extra=([("P5v", p5_end_v),
                                      ("P5t", p5_end_t)]
                                     if first and i == 0 else []))
                    casts.append(itc)
                return casts

            last_g0 = None
            last_w_use = None
            last_mask_mm = None
            itoh = em6("d", (lambda: lambda e2: e2.dma_start(
                out=oh_hbm, in_=onehotT[:]))(),
                extra=[("P5d", p5_end_d)])
            for e in range(EMOE):
                itd6 = em6("d", (lambda e=e: lambda e2: e2.dma_start(
                    out=lsc_row[0:1, 0:2048], in_=oh_hbm[e:e + 1, :]))(),
                    deps=[last_mask_mm, itoh if e == 0 else None],
                    extra=([("P5d", p5_end_d), ("P5t", p5_end_t)]
                           if e == 0 else []))
                casts = emit_we2(e, last_w_use, e == 0)
                mb_evs = []
                for qd in range(4):
                    stg6 = lsc_row[0:1, 512 * qd:512 * qd + 512]
                    itm = em6("t", (lambda stg6=stg6: lambda e2: e2.matmul(
                        pa[:, 0:512], ones1_sb[:], stg6,
                        start=True, stop=True))(),
                        deps=[itd6,
                              mb_evs[-1] if mb_evs else
                              (last_g0 if last_g0 else None)],
                        extra=([("P5t", p5_end_t), ("P5s", cnt5["s"])]
                               if e == 0 and qd == 0 else []))
                    last_mask_mm = itm
                    ite = em6("s", (lambda qd=qd: lambda e2: e2.copy(
                        mbc[:, 512 * qd:512 * qd + 512], pa[:, 0:512]))(),
                        deps=[itm])
                    mb_evs.append(ite)
                for tq in range(4):
                    t0 = 512 * tq
                    he_done = []
                    for hc in range(16):
                        pslot = pa[:, 0:512] if False else \
                            (pb[:, 0:512] if hc % 2 == 0 else
                             ptr.ap().bitcast(F32)[:, 0:512])
                        mmdeps = [casts[2 * 3 + 1] if False else casts[-1]] \
                            if hc == 0 and tq == 0 else []
                        if len(he_done) >= 2:
                            mmdeps.append(he_done[-2])
                        elif tq > 0 or e > 0:
                            pass
                        itm = em6("t", (lambda hc=hc, t0=t0, pslot=pslot,
                                        e=e: lambda e2: [e2.matmul(
                            pslot, w1tile(kk, hc),
                            xn[kk][:, t0:t0 + 512],
                            start=(kk == 0), stop=(kk == NC - 1))
                            for kk in range(NC)][-1])(),
                            deps=mmdeps + [mb_evs[-1]] if hc == 0 and
                            tq == 0 else mmdeps)
                        itg = em6("s", (lambda hc=hc, pslot=pslot: lambda e2:
                                        e2.activation(
                            out=evsc(hc % 2), in_=pslot,
                            func=AF.Gelu))(), deps=[itm])
                        itk = em6("v", (lambda hc=hc, t0=t0: lambda e2:
                                        e2.tensor_tensor(
                            hem(hc), evsc(hc % 2),
                            mbc[:, t0:t0 + 512], ALU.mult))(),
                            deps=[itg])
                        he_done.append(itk)
                    oev = []
                    for c in range(NC):
                        oslot = ptr2.ap().bitcast(F32)[:, 0:512] if c % 2 \
                            else pbig[:, 0:512]
                        itw = em6("t", (lambda c=c, oslot=oslot, e=e:
                                        lambda e2: [e2.matmul(
                            oslot, w2tile(hc2, c), hem(hc2),
                            start=(hc2 == 0), stop=(hc2 == 15))
                            for hc2 in range(16)][-1])(),
                            deps=(he_done if c == 0 else []) +
                            ([last_g0] if last_g0 is not None and c < 2
                             else []) +
                            ([oev[c - 2]] if c >= 2 else []))
                        if e == 0:
                            itv = em6("s", (lambda c=c, t0=t0, oslot=oslot:
                                            lambda e2: e2.copy(
                                g0sl(c, t0, 512), oslot))(), deps=[itw])
                            oev.append(itv)
                        else:
                            ite2 = em6("s", (lambda c=c, oslot=oslot,
                                             t0=t0: lambda e2: e2.copy(
                                evsc(2 + c % 2), oslot))(),
                                deps=[itw] + ([last_g0] if c >= 2 else []))
                            itv = em6("v", (lambda c=c, t0=t0: lambda e2:
                                            e2.tensor_tensor(
                                g0sl(c, t0, 512), g0sl(c, t0, 512),
                                evsc(2 + c % 2), ALU.add))(),
                                deps=[ite2])
                            oev.append(ite2)
                        last_g0 = itv
                        last_w_use = itw

            bb = bf(bscr, 0, 6144)
            ub = lambda c: bb[:, 512 * c:512 * c + 512]
            u2s = lambda i: bb[:, 4096 + 512 * i:4608 + 512 * i]
            mu_bc = bb[:, 5120:5632]
            rs_bc = bb[:, 5632:6144]
            x1b = lambda b2: wa[:, 1024 * b2:1024 * b2 + 1024]
            g_f32 = lambda i: wb[:, 1024 + 512 * i:1536 + 512 * i]
            s0sl = lambda i: wb[:, 2048 + 512 * i:2560 + 512 * i]
            stg6b = theta.ap().bitcast(BF16)[0:1, 0:512]
            epi_prev = [last_g0]

            def emit_epi(tq):
                t0 = 512 * tq
                x0ld = []
                for b2 in range(4):
                    itx = em6("d", (lambda b2=b2, t0=t0: lambda e2:
                                    e2.dma_start(
                        out=x1b(b2),
                        in_=d["x"][t0 + 128 * b2:t0 + 128 * b2 + 128, :]))(),
                        deps=[epi_prev[0]] if b2 == 0 else [],
                        par=(b2 != 0))
                    x0ld.append(itx)
                itmf = em6("t", (lambda t0=t0: lambda e2: e2.matmul(
                    pa[:, 0:512], ones1_sb[:],
                    factor_row[0:1, t0:t0 + 512],
                    start=True, stop=True))(), deps=[epi_prev[0]])
                itef = em6("s", (lambda: lambda e2: e2.copy(
                    bb[:, 4096:4608], pa[:, 0:512]))(), deps=[itmf])
                uch = []
                for c in range(NC):
                    itu = em6("v", (lambda c=c, t0=t0: lambda e2:
                                    e2.tensor_tensor(
                        ub(c), g0sl(c, t0, 512), bb[:, 4096:4608],
                        ALU.mult))(), deps=[itef])
                    uch.append(itu)
                its1 = em6("t", (lambda: lambda e2: [e2.matmul(
                    pa[0:1, 0:512], ones64_sb[:, 0:1], ub(c2),
                    start=(c2 == 0), stop=(c2 == NC - 1))
                    for c2 in range(NC)][-1])(), deps=[itef] + uch)
                s2items = []
                for c in range(NC):
                    itq2 = em6("s", (lambda c=c: lambda e2: e2.activation(
                        out=u2s(c % 2), in_=ub(c), func=AF.Square))(),
                        deps=[uch[c], uch[-1]] +
                        ([s2items[c - 2]] if c >= 2 else []))
                    itm2 = em6("t", (lambda c=c: lambda e2: e2.matmul(
                        pb[0:1, 0:512], ones64_sb[:, 0:1], u2s(c % 2),
                        start=(c == 0), stop=(c == NC - 1)))(),
                        deps=[itq2])
                    s2items.append(itm2)
                itmu = em6("v", (lambda: lambda e2: e2.tensor_scalar(
                    out=wb[0:1, 0:512], in0=pa[0:1, 0:512],
                    scalar1=1.0 / C, scalar2=None, op0=ALU.mult))(),
                    deps=[its1])
                itm2r = em6("v", (lambda: lambda e2: e2.tensor_mul(
                    wb[0:1, 512:1024], wb[0:1, 0:512], wb[0:1, 0:512]))(),
                    deps=[itmu])
                itvar = em6("v", (lambda: lambda e2:
                                  e2.scalar_tensor_tensor(
                    out=wb[0:1, 512:1024], in0=pb[0:1, 0:512],
                    scalar=1.0 / C, in1=wb[0:1, 512:1024],
                    op0=ALU.mult, op1=ALU.subtract))(),
                    deps=[itm2r, s2items[-1]])
                itsd = em6("s", (lambda: lambda e2: e2.activation(
                    out=wb[0:1, 512:1024], in_=wb[0:1, 512:1024],
                    func=AF.Sqrt, bias=cvals[0:1, 0:1]))(), deps=[itvar])
                itrs = em6("v", (lambda: lambda e2: e2.reciprocal(
                    wb[0:1, 512:1024], wb[0:1, 512:1024]))(), deps=[itsd])
                itc1 = em6("v", (lambda: lambda e2: e2.tensor_copy(
                    stg6b, wb[0:1, 0:512]))(), deps=[itrs])
                itb1 = em6("t", (lambda: lambda e2: e2.matmul(
                    pa[:, 0:512], ones1_sb[:], stg6b,
                    start=True, stop=True))(), deps=[itc1, itmu])
                ite1 = em6("s", (lambda: lambda e2: e2.copy(
                    mu_bc, pa[:, 0:512]))(), deps=[itb1])
                itc2 = em6("v", (lambda: lambda e2: e2.tensor_copy(
                    stg6b, wb[0:1, 512:1024]))(), deps=[ite1, itb1])
                itb2 = em6("t", (lambda: lambda e2: e2.matmul(
                    pa[:, 0:512], ones1_sb[:], stg6b,
                    start=True, stop=True))(), deps=[itc2, ite1])
                ite2 = em6("s", (lambda: lambda e2: e2.copy(
                    rs_bc, pa[:, 0:512]))(), deps=[itb2])
                last_tr = None
                addl = []
                for c in range(NC):
                    itl1 = em6("v", (lambda c=c: lambda e2:
                                     e2.tensor_tensor(
                        ub(c), ub(c), mu_bc, ALU.subtract))(),
                        deps=[ite1, s2items[-1]])
                    itl2 = em6("v", (lambda c=c: lambda e2:
                                     e2.tensor_tensor(
                        ub(c), ub(c), rs_bc, ALU.mult))(), deps=[ite2, itl1])
                    itl3 = em6("v", (lambda c=c: lambda e2:
                                     e2.tensor_scalar(
                        out=ub(c), in0=ub(c), scalar1=ln_g_sb[:, c:c + 1],
                        scalar2=ln_b_sb[:, c:c + 1], op0=ALU.mult,
                        op1=ALU.add))(), deps=[itl2])
                    itsi = em6("s", (lambda c=c: lambda e2: e2.activation(
                        out=g_f32(c % 2), in_=ub(c), func=AF.Silu))(),
                        deps=[itl3] + ([addl[c - 2]] if c >= 2 else []))
                    itds = em6("d", (lambda c=c, t0=t0: lambda e2:
                                     e2.dma_start(
                        out=s0sl(c % 2),
                        in_=s_hbm[128 * c:128 * c + 128, t0:t0 + 512]))(),
                        deps=[itsi] if False else
                        [addl[c - 2]] if c >= 2 else [epi_prev[0]],
                        extra=[("P5d", p5_end_d)] if tq == 0 and c < 2
                        else [])
                    itsg = em6("v", (lambda c=c: lambda e2:
                                     e2.tensor_tensor(
                        g_f32(c % 2), g_f32(c % 2), s0sl(c % 2),
                        ALU.mult))(), deps=[itsi, itds])
                    ittr = em6("t", (lambda c=c: lambda e2: [e2.transpose(
                        ptr.ap().bitcast(F32)[:, 128 * b2:128 * b2 + 128],
                        g_f32(c % 2)[:, 128 * b2:128 * b2 + 128],
                        idnf_sb[:]) for b2 in range(4)][-1])(),
                        deps=[itsg] + ([last_tr] if last_tr is not None
                                       else []))
                    itad = em6("v", (lambda c=c: lambda e2: [
                        e2.tensor_tensor(
                            x1b(b2)[:, 128 * c:128 * c + 128],
                            x1b(b2)[:, 128 * c:128 * c + 128],
                            ptr.ap().bitcast(F32)[:, 128 * b2:128 * b2 + 128],
                            ALU.add) for b2 in range(4)][-1])(),
                        deps=[ittr, x0ld[3]])
                    last_tr = itad
                    addl.append(itad)
                sts = []
                for b2 in range(4):
                    itst = em6("d", (lambda b2=b2, t0=t0: lambda e2:
                                     e2.dma_start(
                        out=x1_hbm[t0 + 128 * b2:t0 + 128 * b2 + 128, :],
                        in_=x1b(b2)))(),
                        deps=[addl[-1]] if b2 == 0 else [],
                        par=(b2 != 0))
                    sts.append(itst)
                epi_prev[0] = sts[-1]

            for tq in range(4):
                emit_epi(tq)

            replay6()
        if nphase >= 7:
            P7, sched7, cnt7, em7, replay7 = mk_sched("P7")
            b7 = bf(bscr, 0, 6144)
            xn_tok7 = b7[:, 0:1024]
            q_tok7 = b7[:, 1024:2048]
            qstage7 = b7[:, 2048:3072]
            lsccol = b7[0:1, 3072:3200]
            p6_t = cnt6["t"]; p6_s = cnt6["s"]; p6_v = cnt6["v"]
            p6_d = cnt6["d"] * 16
            prev7 = None
            prevq = None
            prev_tr = None
            prev_dq = None
            for tt in range(NT):
                first = (tt == 0)
                itx = em7("d", (lambda tt=tt: lambda e2: e2.dma_start(
                    out=wa[:, (tt % 2) * 1024:(tt % 2) * 1024 + 1024],
                    in_=x1_hbm[tt * 128:(tt + 1) * 128, :]))(),
                    deps=[prev7],
                    extra=[("P6d", p6_d), ("P6t", p6_t), ("P6v", p6_v),
                           ("P6s", p6_s), ("P6e", cnt6["e"] * 16)]
                    if first else [])
                xt = wa[:, (tt % 2) * 1024:(tt % 2) * 1024 + 1024]
                it1 = em7("v", (lambda xt=xt: lambda e2: e2.tensor_reduce(
                    small[:, 0:1], xt, AX.X, ALU.add))(), deps=[itx],
                    extra=[("P6v", p6_v), ("P6s", p6_s)] if first else [])
                it2 = em7("v", (lambda: lambda e2: e2.tensor_scalar(
                    out=small[:, 1:2], in0=small[:, 0:1], scalar1=1.0 / C,
                    scalar2=None, op0=ALU.mult))(), deps=[it1])
                it3 = em7("s", (lambda xt=xt: lambda e2: e2.activation(
                    out=wb[:, 0:1024], in_=xt, func=AF.Square,
                    accum_out=small[:, 2:3]))(), deps=[itx, prev7],
                    extra=[("P6s", p6_s)] if first else [])
                it4 = em7("s", (lambda: lambda e2: e2.activation(
                    out=small[:, 3:4], in_=small[:, 1:2],
                    func=AF.Square))(), deps=[it2])
                it5 = em7("v", (lambda: lambda e2: e2.scalar_tensor_tensor(
                    out=small[:, 4:5], in0=small[:, 2:3], scalar=1.0 / C,
                    in1=small[:, 3:4], op0=ALU.mult, op1=ALU.subtract))(),
                    deps=[it3, it4])
                it6 = em7("s", (lambda: lambda e2: e2.activation(
                    out=small[:, 6:7], in_=small[:, 4:5], func=AF.Sqrt,
                    bias=cvals[:, 0:1]))(), deps=[it5])
                it7 = em7("v", (lambda: lambda e2: e2.reciprocal(
                    small[:, 5:6], small[:, 6:7]))(), deps=[it6])
                it8 = em7("v", (lambda xt=xt: lambda e2: e2.tensor_scalar(
                    out=xn_tok7, in0=xt, scalar1=small[:, 1:2],
                    scalar2=small[:, 5:6], op0=ALU.subtract,
                    op1=ALU.mult))(), deps=[it7, prev_tr])
                prev7 = it8
                it9 = em7("v", (lambda tt=tt: lambda e2: e2.tensor_reduce(
                    amax_col[:, tt:tt + 1], xn_tok7, AX.X, ALU.max,
                    apply_absolute_value=True))(), deps=[it8])
                it10 = em7("v", (lambda: lambda e2: e2.tensor_scalar(
                    out=small[:, 8:9], in0=small[:, 7:8] if False else
                    amax_col[:, 0:1], scalar1=1e-12, scalar2=None,
                    op0=ALU.max))(), deps=[it9]) if False else None
                it10 = em7("v", (lambda tt=tt: lambda e2: e2.tensor_scalar(
                    out=small[:, 8:9], in0=amax_col[:, tt:tt + 1],
                    scalar1=1e-12, scalar2=None, op0=ALU.max))(),
                    deps=[it9])
                it11 = em7("v", (lambda: lambda e2: e2.reciprocal(
                    small[:, 9:10], small[:, 8:9]))(), deps=[it10])
                it12 = em7("v", (lambda: lambda e2: e2.tensor_scalar(
                    out=small[:, 10:11], in0=small[:, 9:10], scalar1=127.0,
                    scalar2=None, op0=ALU.mult))(), deps=[it11])
                it13 = em7("v", (lambda: lambda e2: e2.tensor_scalar(
                    out=small[:, 11:12], in0=small[:, 8:9],
                    scalar1=1.0 / 127.0, scalar2=None, op0=ALU.mult))(),
                    deps=[it10])
                it14 = em7("v", (lambda: lambda e2: e2.tensor_copy(
                    lsccol.tensor if False else b7[:, 3072:3073][0:128, :],
                    small[:, 11:12]))(), deps=[it13, prevq])
                it15 = em7("d", (lambda tt=tt: lambda e2: e2.dma_start(
                    out=lsc_row[0:1, tt * 128:(tt + 1) * 128],
                    in_=b7[:, 3072:3073].rearrange("p m -> (p m)")
                    if False else b7[:, 3072:3073]))(), deps=[it14])
                it16 = em7("v", (lambda: lambda e2: e2.tensor_scalar(
                    out=wb[:, 1024:2048], in0=xn_tok7,
                    scalar1=small[:, 10:11], scalar2=None,
                    op0=ALU.mult))(), deps=[it12, it8])
                it17 = em7("v", (lambda: lambda e2: e2.tensor_copy(
                    wb[:, 2048:3072].ap().bitcast(I32)
                    if False else wb[:, 2048:3072].bitcast(I32),
                    wb[:, 1024:2048]))(), deps=[it16])
                it18 = em7("v", (lambda: lambda e2: e2.tensor_copy(
                    q_tok7, wb[:, 2048:3072].bitcast(I32)))(),
                    deps=[it17])
                prevq = it18
                ittx = em7("t", (lambda tt=tt: lambda e2: [e2.transpose(
                    ptr[:, cb * 128:(cb + 1) * 128],
                    xn_tok7[:, cb * 128:(cb + 1) * 128], idn_sb[:])
                    for cb in range(NC)][-1])(),
                    deps=[it8, prev_tr],
                    extra=[("P6t", p6_t)] if first else [])
                ittq = em7("t", (lambda tt=tt: lambda e2: [e2.transpose(
                    ptr2[:, cb * 128:(cb + 1) * 128],
                    q_tok7[:, cb * 128:(cb + 1) * 128], idn_sb[:])
                    for cb in range(NC)][-1])(), deps=[it18])
                itex = em7("s", (lambda tt=tt: lambda e2: [e2.copy(
                    xn[cb][:, tt * 128:(tt + 1) * 128],
                    ptr[:, cb * 128:(cb + 1) * 128])
                    for cb in range(NC)][-1])(), deps=[ittx])
                iteq = em7("s", (lambda: lambda e2: [e2.copy(
                    qstage7[:, cb * 128:(cb + 1) * 128],
                    ptr2[:, cb * 128:(cb + 1) * 128])
                    for cb in range(NC)][-1])(), deps=[ittq, prev_dq])
                itdq = None
                for cb in range(NC):
                    itdq = em7("d", (lambda tt=tt, cb=cb: lambda e2:
                                     e2.dma_start(
                        out=qxn_hbm[cb * 128:(cb + 1) * 128,
                                    tt * 128:(tt + 1) * 128],
                        in_=qstage7[:, cb * 128:(cb + 1) * 128]))(),
                        deps=[iteq] if cb == 0 else [], par=(cb != 0))
                prev_dq = itdq
                prev_tr = iteq
            replay7()
        if nphase >= 8:
            P8, sched8, cnt8, em8, replay8 = mk_sched("P8")
            b8 = bf(bscr, 0, 6144)
            binT = b8[:, 0:1024]
            winT = b8[:, 1024:2048]
            vstg = b8[:, 2048:2560]
            q1_t = [bf(PL[1], i) for i in range(4)] + \
                [bf(PL[2], i - 4) for i in range(4, NC)]
            k1_t = [bf(PL[3], i) for i in range(4)] + \
                [bf(PL[4], i - 4) for i in range(4, NC)]
            v1tok_t = ([bf(PL[5], 0, 8192)[:, i * 1024:(i + 1) * 1024]
                        for i in range(8)]
                       + [bf(PL[0], 0, 8192)[:, i * 1024:(i + 1) * 1024]
                          for i in range(8)])
            p7_t = cnt7["t"]; p7_s = cnt7["s"]; p7_v = cnt7["v"]
            p7_d = cnt7["d"] * 16

            prev_ev = None
            prev_tr8 = None
            prev_cast8 = None
            cast8_l = []
            for fb in range(24):
                first = (fb == 0)
                itl = em8("d", (lambda fb=fb: lambda e2: e2.dma_start(
                    out=wb[:, (fb % 4) * 1024:(fb % 4) * 1024 + 1024],
                    in_=d["inw"][fb * 128:(fb + 1) * 128, :]))(),
                    deps=[cast8_l[-1] if cast8_l else None]
                    if fb % 4 == 0 else [],
                    par=(fb % 4 != 0),
                    extra=[("P7d", p7_d), ("P7s", p7_s), ("P7v", p7_v)]
                    if first else [])
                itc = em8("v", (lambda fb=fb: lambda e2: e2.tensor_copy(
                    binT, wb[:, (fb % 4) * 1024:(fb % 4) * 1024 + 1024]))(),
                    deps=[itl, prev_tr8],
                    extra=[("P7v", p7_v), ("P7t", p7_t), ("P7s", p7_s)]
                    if first else [])
                cast8_l.append(itc)
                prev_cast8 = itc
                ittr = em8("t", (lambda: lambda e2: [e2.transpose(
                    ptr2[:, cb * 128:(cb + 1) * 128],
                    binT[:, cb * 128:(cb + 1) * 128], idn_sb[:])
                    for cb in range(NC)][-1])(),
                    deps=[itc],
                    extra=[("P7t", p7_t)] if first else [])
                itev = em8("s", (lambda: lambda e2: [e2.copy(
                    winT[:, cb * 128:(cb + 1) * 128],
                    ptr2[:, cb * 128:(cb + 1) * 128])
                    for cb in range(NC)][-1])(), deps=[ittr])
                prev_tr8 = itev
                evs_fb = []
                for tq in range(4):
                    t0 = 512 * tq
                    pslot = pa[:, 0:512] if tq % 2 == 0 else pb[:, 0:512]
                    itm = em8("t", (lambda t0=t0, pslot=pslot: lambda e2:
                                    [e2.matmul(
                        pslot, winT[:, ec * 128:(ec + 1) * 128],
                        xn[ec][:, t0:t0 + 512],
                        start=(ec == 0), stop=(ec == NC - 1))
                        for ec in range(NC)][-1])(),
                        deps=[itev] + ([prev_ev] if tq == 0 else []) +
                        ([evs_fb[tq - 2]] if tq >= 2 else []))
                    if fb < 16:
                        dstl = q1_t if fb < 8 else k1_t
                        dst = dstl[fb % 8]
                        itv = em8("s", (lambda dst=dst, t0=t0, fb=fb,
                                        pslot=pslot: lambda e2: e2.add(
                            dst[:, t0:t0 + 512], pslot,
                            inb_sb[:, fb:fb + 1]))(),
                            deps=[itm])
                        prev_ev = itv
                        evs_fb.append(itv)
                    else:
                        itv = em8("s", (lambda fb=fb, pslot=pslot:
                                        lambda e2: e2.add(
                            vstg, pslot, inb_sb[:, fb:fb + 1]))(),
                            deps=[itm])
                        ittv = em8("t", (lambda: lambda e2: [e2.transpose(
                            ptr[:, bb2 * 128:(bb2 + 1) * 128],
                            vstg[:, bb2 * 128:(bb2 + 1) * 128], idn_sb[:])
                            for bb2 in range(4)][-1])(), deps=[itv])
                        itve = em8("s", (lambda fb=fb, tq=tq: lambda e2:
                                         [e2.copy(
                            v1tok_t[tq * 4 + bb2][:, (fb - 16) * 128:
                                                  (fb - 16) * 128 + 128],
                            ptr[:, bb2 * 128:(bb2 + 1) * 128])
                            for bb2 in range(4)][-1])(), deps=[ittv])
                        prev_ev = itve
                        evs_fb.append(itv)

            woT = bf(wa, 0, 8192)
            prev_c2 = None
            for fb in range(NC):
                itl = em8("d", (lambda fb=fb: lambda e2: e2.dma_start(
                    out=wb[:, (fb % 4) * 1024:(fb % 4) * 1024 + 1024],
                    in_=d["outw"][fb * 128:(fb + 1) * 128, :]))(),
                    deps=[cast8_l[-1] if fb == 0 else
                          (prev_c2 if fb == 4 else None)]
                    if fb % 4 == 0 else [],
                    par=(fb % 4 != 0),
                    extra=[("P7d", p7_d)] if fb == 0 else [])
                itc = em8("v", (lambda fb=fb: lambda e2: e2.tensor_copy(
                    binT, wb[:, (fb % 4) * 1024:(fb % 4) * 1024 + 1024]))(),
                    deps=[itl, prev_cast8 if fb == 0 else None, prev_tr8])
                prev_cast8 = itc
                ittr = em8("t", (lambda: lambda e2: [e2.transpose(
                    ptr2[:, cb * 128:(cb + 1) * 128],
                    binT[:, cb * 128:(cb + 1) * 128], idn_sb[:])
                    for cb in range(NC)][-1])(),
                    deps=[itc])
                itev = em8("s", (lambda fb=fb: lambda e2: [e2.copy(
                    woT[:, cb * 1024 + fb * 128:cb * 1024 + fb * 128 + 128],
                    ptr2[:, cb * 128:(cb + 1) * 128])
                    for cb in range(NC)][-1])(), deps=[ittr])
                prev_c2 = itev
                prev_tr8 = itev
            wo_ready = prev_c2

            e_sl8 = lambda i: b8[:, 2560 + 256 * i:2560 + 256 * i + 256]
            s1pre = lambda k2: b8[:, 256 * k2:256 * k2 + 256]
            prev_epi8 = [prev_ev]
            lastwo8 = [None]
            escore8_free = [None, None, None, None]
            e8_free = [None, None, None, None]
            pbig01_free = [None]
            pbig23_free = [None]
            st_carry8 = [None]
            sc_slot8 = lambda scs: scores_ps[:, 256 * scs:256 * scs + 256] \
                if scs < 2 else lst_ps[:, 256 * (scs - 2):
                                       256 * (scs - 2) + 256]

            def emit_tc8(tcc):
                tsl = slice(tcc * 256, tcc * 256 + 256)
                pair_done = [None]
                s0_done = []

                def emit_pair(cc):
                    h1 = 2 * cc
                    pp = cc % 2
                    last_sums = None
                    for ss in range(NS):
                        for hh in range(2):
                            h = h1 + hh
                            eslot = hh + 2 * pp
                            esl = e_sl8(eslot)
                            ssl = scores_ps[:, 256 * hh:256 * hh + 256]
                            deps = [escore8_free[hh]]
                            if ss == 0 and cc == 0 and hh == 0 and tcc == 0:
                                deps.append(prev_epi8[0])
                                deps.append(wo_ready)
                            itm = em8("t", (lambda h=h, ss=ss, ssl=ssl,
                                            tsl=tsl: lambda e2:
                                           e2.matmul(
                                ssl,
                                k1_t[h // 2][64 * (h % 2):64 * (h % 2) + 64,
                                             ss * 128:(ss + 1) * 128],
                                q1_t[h // 2][64 * (h % 2):64 * (h % 2) + 64,
                                             tsl],
                                start=True, stop=True,
                                tile_position=(64 * (h % 2), 0)))(),
                                deps=deps)
                            ite = em8("s", (lambda esl=esl, ssl=ssl:
                                            lambda e2: e2.activation(
                                out=esl, in_=ssl, func=AF.Exp,
                                scale=0.125))(),
                                deps=[itm, e8_free[eslot]])
                            escore8_free[hh] = ite
                            itlast = ite
                            delta = 128 * ss - 256 * tcc
                            if delta in (-128, 0, 128):
                                bnm = {-128: "band_m", 0: "band_0",
                                       128: "band_p"}[delta]
                                itlast = em8("v", (lambda esl=esl, bnm=bnm:
                                                   lambda e2:
                                                   e2.tensor_tensor(
                                    esl, esl, band_sb[bnm][:],
                                    ALU.mult))(), deps=[ite])
                            bk = pbig[0:64, 0:256] if hh == 0 else \
                                pbig[64:128, 512:768]
                            av_deps = [itlast]
                            if ss == 0:
                                av_deps.append(pair_done[0] if pair_done[0]
                                               is not None else
                                               pbig01_free[0])
                            itav = em8("t", (lambda hh=hh, ss=ss, bk=bk,
                                             h=h, esl=esl: lambda e2:
                                            e2.matmul(
                                bk, v1tok_t[ss][:, 64 * h:64 * h + 64],
                                esl, start=(ss == 0),
                                stop=(ss == NS - 1),
                                tile_position=(0, 64 * hh)))(),
                                deps=av_deps)
                            su = pa[0:64, 0:256] if hh == 0 \
                                else pb[64:128, 0:256]
                            itsu = em8("t", (lambda hh=hh, ss=ss, su=su,
                                             esl=esl: lambda e2:
                                            e2.matmul(
                                su, ones64_sb[:], esl,
                                start=(ss == 0), stop=(ss == NS - 1),
                                tile_position=(0, 64 * hh)))(),
                                deps=[])
                            e8_free[eslot] = itsu
                            last_sums = itsu

                    def mk_rs8(e2):
                        e2.reciprocal(rsum_bc[0:64, :], pa[0:64, 0:256])
                        return e2.reciprocal(rsum_bc[64:128, :],
                                             pb[64:128, 0:256])
                    itr = em8("v", mk_rs8,
                              deps=[last_sums,
                                    s0_done[-1] if s0_done else None])
                    itr2 = em8("v", (lambda cc=cc: lambda e2: [
                        e2.tensor_tensor(s1pre(cc)[0:64, :],
                                         pbig[0:64, 0:256],
                                         rsum_bc[0:64, :], ALU.mult),
                        e2.tensor_tensor(s1pre(cc)[64:128, :],
                                         pbig[64:128, 512:768],
                                         rsum_bc[64:128, :], ALU.mult)][-1]
                        )(), deps=[itr])
                    pair_done[0] = itr2
                    s0_done.append(itr2)

                for cc in range(NC):
                    emit_pair(cc)
                wo_list = []
                for mc in range(NC):
                    itwo = em8("t", (lambda mc=mc, tsl=tsl: lambda e2:
                                     [e2.matmul(
                        pbig[:, 256 * mc:256 * mc + 256],
                        woT[:, kk * 1024 + 128 * mc:
                            kk * 1024 + 128 * mc + 128],
                        s1pre(kk), start=(kk == 0), stop=(kk == NC - 1))
                        for kk in range(NC)][-1])(),
                        deps=(s0_done + [pbig23_free[0]] if mc == 0
                              else []))
                    wo_list.append(itwo)
                st_prev = st_carry8[0]
                its0_l = []
                st_l = []
                for mc in range(NC):
                    its0 = em8("s", (lambda mc=mc: lambda e2: e2.add(
                        s0stage(mc % 4),
                        pbig[:, 256 * mc:256 * mc + 256],
                        outb_sb[:, mc:mc + 1]))(),
                        deps=[wo_list[mc | 1], st_prev])
                    its0_l.append(its0)
                    itd2 = em8("d", (lambda mc=mc, tsl=tsl: lambda e2:
                                     e2.dma_start(
                        out=s_hbm[128 * mc:128 * mc + 128, tsl],
                        in_=s0stage(mc % 4)))(), deps=[its0],
                        par=(mc % 4 != 0))
                    st_l.append(itd2)
                    if mc >= 3:
                        st_prev = st_l[mc - 3]
                    lastwo8[0] = itd2
                pbig01_free[0] = its0_l[3]
                pbig23_free[0] = its0_l[7]
                st_carry8[0] = st_l[4]
                prev_epi8[0] = lastwo8[0]

            escore8_free[0] = prev_epi8[0]
            escore8_free[1] = prev_epi8[0]
            escore8_free[2] = wo_ready
            escore8_free[3] = wo_ready
            for tcc in range(NTCA):
                emit_tc8(tcc)
            replay8()
        if nphase >= 9:
            P9, sched9, cnt9, em9, replay9 = mk_sched("P9")
            b9 = bf(bscr, 0, 6144)
            qx9 = lambda par, dc: b9[:, 2048 * par + 256 * dc:
                                     2048 * par + 256 * dc + 256]
            qrhs = lambda dc, nc2: bf(PL[1], 0, 8192)[
                :, nc2 * 1024 + 128 * dc:nc2 * 1024 + 128 * dc + 128]
            rhsT9 = wa[:, 0:1024]
            qf32_9 = wa[:, 1024:2048]
            qi32_9 = wa[:, 2048:3072]
            qbf_9 = bf(wa, 0, 8192)[:, 6144:7168]
            scale_n9 = small2[:, 32:40]
            lscbc9 = lambda i: wb[:, 512 * i:512 * i + 256]
            g1_9 = lambda i: wb[:, 1024 + 512 * i:1024 + 512 * i + 256]
            s1sl9 = lambda i: wb[:, 2048 + 256 * i:2048 + 256 * i + 256]
            ybuf = lambda b2: wb[:, 3072 + 0:3072 + 1024] if False else None
            p8_t = cnt8["t"]; p8_s = cnt8["s"]; p8_v = cnt8["v"]
            p8_d = cnt8["d"] * 16

            prev9 = None
            qr_done = []
            for rc in range(NC):
                trs = []
                for dc in range(NC):
                    itl = em9("d", (lambda rc=rc, dc=dc: lambda e2:
                                    e2.dma_start(
                        out=wb[:, 128 * dc:128 * dc + 128],
                        in_=d["rhs"][dc * 128:(dc + 1) * 128,
                                     rc * 128:(rc + 1) * 128]))(),
                        deps=[prev9] if dc == 0 else [], par=False,
                        extra=[("P8d", p8_d), ("P8t", p8_t), ("P8s", p8_s),
                               ("P8v", p8_v)]
                        if rc == 0 and dc == 0 else [])
                    ittr = em9("t", (lambda dc=dc: lambda e2: e2.transpose(
                        ptr.ap().bitcast(F32)[:, 128 * (dc % 4):
                                              128 * (dc % 4) + 128],
                        wb[:, 128 * dc:128 * dc + 128],
                        idnf_sb[:]))(),
                        deps=[itl, trs[dc - 4] if dc >= 4 else None],
                        extra=[("P8t", p8_t), ("P8s", cnt8["s"]),
                               ("P8v", p8_v)]
                        if rc == 0 and dc == 0 else [])
                    itev = em9("s", (lambda dc=dc: lambda e2: e2.copy(
                        rhsT9[:, 128 * dc:128 * dc + 128],
                        ptr.ap().bitcast(F32)[:, 128 * (dc % 4):
                                              128 * (dc % 4) + 128]))(),
                        deps=[ittr])
                    trs.append(itev)
                prev9 = trs[-1]
                ita = em9("v", (lambda: lambda e2: e2.tensor_reduce(
                    small[:, 0:1], rhsT9, AX.X, ALU.max,
                    apply_absolute_value=True))(), deps=trs)
                itb = em9("v", (lambda: lambda e2: e2.tensor_scalar(
                    out=small[:, 1:2], in0=small[:, 0:1], scalar1=1e-12,
                    scalar2=None, op0=ALU.max))(), deps=[ita])
                itc9 = em9("v", (lambda rc=rc: lambda e2: e2.tensor_scalar(
                    out=scale_n9[:, rc:rc + 1], in0=small[:, 1:2],
                    scalar1=1.0 / 127.0, scalar2=None, op0=ALU.mult))(),
                    deps=[itb])
                itd9 = em9("v", (lambda: lambda e2: e2.reciprocal(
                    small[:, 2:3], small[:, 1:2]))(), deps=[itc9])
                ite9 = em9("v", (lambda: lambda e2: e2.tensor_scalar(
                    out=small[:, 3:4], in0=small[:, 2:3], scalar1=127.0,
                    scalar2=None, op0=ALU.mult))(), deps=[itd9])
                itf9 = em9("v", (lambda: lambda e2: e2.tensor_scalar(
                    out=qf32_9, in0=rhsT9, scalar1=small[:, 3:4],
                    scalar2=None, op0=ALU.mult))(), deps=[ite9])
                itg9 = em9("v", (lambda: lambda e2: e2.tensor_copy(
                    qi32_9.bitcast(I32), qf32_9))(), deps=[itf9])
                ith9 = em9("v", (lambda: lambda e2: e2.tensor_copy(
                    qbf_9, qi32_9.bitcast(I32)))(), deps=[itg9])
                itj = em9("t", (lambda: lambda e2: [e2.transpose(
                    ptr2[:, 128 * b3:128 * b3 + 128],
                    qbf_9[:, 128 * b3:128 * b3 + 128], idn_sb[:])
                    for b3 in range(NC)][-1])(), deps=[ith9])
                itk = em9("s", (lambda rc=rc: lambda e2: [e2.copy(
                    qrhs(dc2, rc), ptr2[:, 128 * dc2:128 * dc2 + 128])
                    for dc2 in range(NC)][-1])(), deps=[itj])
                qr_done.append(itk)
                prev9 = itk

            stg9 = theta.ap().bitcast(BF16)[0:1, 0:256]
            yprev = [qr_done[-1]]
            for tcc in range(NTCA):
                t0 = 256 * tcc
                par = tcc % 2
                qld = []
                for dc in range(NC):
                    itq = em9("d", (lambda dc=dc, t0=t0, par=par: lambda e2:
                                    e2.dma_start(
                        out=qx9(par, dc),
                        in_=qxn_hbm[dc * 128:(dc + 1) * 128,
                                    t0:t0 + 256]))(),
                        deps=[yprev[0] if tcc >= 2 else qr_done[-1]]
                        if dc == 0 else [],
                        par=(dc != 0))
                    qld.append(itq)
                accs = []
                for nc2 in range(NC):
                    itm = em9("t", (lambda nc2=nc2, par=par: lambda e2:
                                    [e2.matmul(
                        pbig[:, 256 * nc2:256 * nc2 + 256],
                        qrhs(dc2, nc2), qx9(par, dc2),
                        start=(dc2 == 0), stop=(dc2 == NC - 1))
                        for dc2 in range(NC)][-1])(),
                        deps=qld + ([yprev[0]] if nc2 < 2 and tcc >= 1
                                    else []) +
                        ([qr_done[-1]] if tcc == 0 and nc2 == 0 else []))
                    accs.append(itm)
                itml = em9("t", (lambda t0=t0: lambda e2: e2.matmul(
                    pa[:, 0:256], ones1_sb[:],
                    lsc_row[0:1, t0:t0 + 256],
                    start=True, stop=True))(), deps=[accs[-1]])
                itel = em9("s", (lambda: lambda e2: e2.copy(
                    lscbc9(0), pa[:, 0:256]))(), deps=[itml])
                ytile = [wa[:, 1024 + 1024 * b2:1024 + 1024 * b2 + 1024]
                         for b2 in range(2)]
                x1ld = []
                for b2 in range(2):
                    itx = em9("d", (lambda b2=b2, t0=t0: lambda e2:
                                    e2.dma_start(
                        out=ytile[b2],
                        in_=x1_hbm[t0 + 128 * b2:t0 + 128 * b2 + 128, :]))(),
                        deps=[yprev[0] if tcc >= 1 else qr_done[-1]]
                        if b2 == 0 else [], par=(b2 != 0))
                    x1ld.append(itx)
                addl = []
                last_tr9 = None
                for c in range(NC):
                    itd1 = em9("v", (lambda c=c: lambda e2: e2.tensor_tensor(
                        g1_9(c % 2), pbig[:, 256 * c:256 * c + 256],
                        lscbc9(0), ALU.mult))(),
                        deps=[itel, accs[c | 1]] +
                        ([addl[c - 2]] if c >= 2 else []))
                    itsi = em9("s", (lambda c=c: lambda e2: e2.activation(
                        out=g1_9(c % 2), in_=g1_9(c % 2), func=AF.Silu,
                        scale=scale_n9[:, c:c + 1]))(), deps=[itd1])
                    itds = em9("d", (lambda c=c, t0=t0: lambda e2:
                                     e2.dma_start(
                        out=s1sl9(c % 4),
                        in_=s_hbm[128 * c:128 * c + 128, t0:t0 + 256]))(),
                        deps=[addl[c - 4]] if c >= 4 else [x1ld[-1]],
                        extra=[("P8d", p8_d)] if tcc == 0 and c < 2 else [])
                    itsg = em9("v", (lambda c=c: lambda e2: e2.tensor_tensor(
                        g1_9(c % 2), g1_9(c % 2), s1sl9(c % 4),
                        ALU.mult))(), deps=[itsi, itds])
                    ittr = em9("t", (lambda c=c: lambda e2: [e2.transpose(
                        ptr.ap().bitcast(F32)[:, 128 * b2:128 * b2 + 128],
                        g1_9(c % 2)[:, 128 * b2:128 * b2 + 128],
                        idnf_sb[:]) for b2 in range(2)][-1])(),
                        deps=[itsg] + ([last_tr9] if last_tr9 else []))
                    itad = em9("v", (lambda c=c: lambda e2: [
                        e2.tensor_tensor(
                            ytile[b2][:, 128 * c:128 * c + 128],
                            ytile[b2][:, 128 * c:128 * c + 128],
                            ptr.ap().bitcast(F32)[:, 128 * b2:128 * b2 + 128],
                            ALU.add) for b2 in range(2)][-1])(),
                        deps=[ittr] + x1ld)
                    last_tr9 = itad
                    addl.append(itad)
                sts = []
                for b2 in range(2):
                    itst = em9("d", (lambda b2=b2, t0=t0: lambda e2:
                                     e2.dma_start(
                        out=y[t0 + 128 * b2:t0 + 128 * b2 + 128, :],
                        in_=ytile[b2]))(), deps=[addl[-1]] if b2 == 0
                        else [], par=(b2 != 0))
                    sts.append(itst)
                yprev[0] = sts[-1]
            replay9()
        tap_specs = {
            "xn0": ([C, T], BF16, lambda: xn),
            "qi": ([128, T], BF16, lambda: qi),
            "ki": ([128, T], BF16, lambda: ki),
            "wtok": ([128, NT * G], F32, lambda: wtok),
            "onehotT": ([EMOE, T], BF16, lambda: onehotT),
            "factor_row": ([1, T], BF16, lambda: factor_row),
            "theta": ([128, NT * H], F32, lambda: theta),
            "sortv": ([128, NSORT * G], F32, lambda: sortv),
            "qf": ([C, T], BF16, lambda: qf_t),
            "kf": ([C, T], BF16, lambda: kf_t),
            "vtok": ([T, C], BF16, lambda: vtok_t),
            "s_hbm": ([C, T], F32, lambda: s_hbm[:, 0:256 * ntca]
                      if ntca < NTCA else s_hbm),
            "throws": ([16, T], BF16, lambda: throws),
            "bv5": ([128, 6144], BF16, lambda: bf(bscr, 0, 6144)),
            "g0acc": ([C, T], BF16, lambda: [g0sl(c, 0, 2048)
                                             for c in range(NC)]),
            "x1": ([T, C], F32, lambda: x1_hbm),
            "xn1": ([C, T], BF16, lambda: xn),
            "qxn": ([C, T], BF16, lambda: qxn_hbm),
            "lscr": ([1, T], BF16, lambda: lsc_row),
        }
        FIN = sem("FIN")
        chosen = [(nm, tap_specs[nm]) for nm in dbg if nm in tap_specs]
        for nm, spec in chosen:
            taps[nm] = nc.dram_tensor("dbg_" + nm, list(spec[0]), spec[1],
                                      kind="ExternalOutput").ap()

        @block.gpsimd
        def _(gpsimd):
            gpsimd.wait_ge(ln0_done, NT)
            if nphase >= 2:
                gpsimd.wait_ge(sems["P2s"], 3 + 4 * (NT if p2sub >= 3 else 0))
            if nphase >= 3:
                gpsimd.wait_ge(sems["P3v"], 139 * NT)
            if nphase >= 4:
                gpsimd.wait_ge(sems["P4vv"], NT)
                gpsimd.wait_ge(sems["P4qs"], 2 * NT)
                gpsimd.wait_ge(sems["P4ks"], 2 * NT)
            if nphase >= 5:
                for e5 in "tsv":
                    gpsimd.wait_ge(sems["P5" + e5], cnt5[e5])
                gpsimd.wait_ge(sems["P5d"], cnt5["d"] * 16)
                gpsimd.wait_ge(sems["P5e"], cnt5["e"] * 16)
            if nphase >= 6:
                for e6 in "tsv":
                    gpsimd.wait_ge(sems["P6" + e6], cnt6[e6])
                gpsimd.wait_ge(sems["P6d"], cnt6["d"] * 16)
                gpsimd.wait_ge(sems["P6e"], cnt6["e"] * 16)
            if nphase >= 7:
                for e7 in "tsv":
                    gpsimd.wait_ge(sems["P7" + e7], cnt7[e7])
                gpsimd.wait_ge(sems["P7d"], cnt7["d"] * 16)
                gpsimd.wait_ge(sems["P7e"], cnt7["e"] * 16)
            if nphase >= 8:
                for e8 in "tsv":
                    gpsimd.wait_ge(sems["P8" + e8], cnt8[e8])
                gpsimd.wait_ge(sems["P8d"], cnt8["d"] * 16)
                gpsimd.wait_ge(sems["P8e"], cnt8["e"] * 16)
            if nphase >= 9:
                for e9 in "tsv":
                    gpsimd.wait_ge(sems["P9" + e9], cnt9[e9])
                gpsimd.wait_ge(sems["P9d"], cnt9["d"] * 16)
                gpsimd.wait_ge(sems["P9e"], cnt9["e"] * 16)
            nst = 0
            for nm, (shape, dt_, get) in chosen:
                src = get()
                if isinstance(src, list):
                    for c, tile in enumerate(src):
                        gpsimd.dma_start(
                            out=taps[nm][c * 128:(c + 1) * 128, :],
                            in_=tile[:]).then_inc(FIN, 16)
                        nst += 1
                else:
                    try:
                        src_ap = src[:]
                    except TypeError:
                        src_ap = src
                    dst_ap = taps[nm]
                    if list(src_ap.shape) != list(dst_ap.shape):
                        dst_ap = dst_ap[:, 0:src_ap.shape[1]]
                    gpsimd.dma_start(out=dst_ap, in_=src_ap).then_inc(FIN, 16)
                    nst += 1
            if nphase < 9:
                gpsimd.dma_start(out=y[0:128, :],
                                 in_=wa[:, 0:1024]).then_inc(FIN, 16)
                nst += 1
            gpsimd.wait_ge(FIN, 16 * nst)

    return nc, d


def make_in_maps(inputs):
    cst = host_consts()
    base = {
        "w_iq": inputs["L0_attn_W_Iq"], "w_ik": inputs["L0_attn_W_Ik"],
        "w_iw": inputs["L0_attn_W_Iw"],
        "gate_bias": np.asarray(inputs["L0_attn_gate_bias"],
                                np.float32).reshape(1, G),
        "head_bias": np.asarray(inputs["L0_attn_head_bias"],
                                np.float32).reshape(1, H),
        "var_ema": np.asarray(inputs["L0_attn_variance_ema"],
                              np.float32).reshape(1, 1),
        "w_q": inputs["L0_attn_W_q"], "w_k": inputs["L0_attn_W_k"],
        "w_v": inputs["L0_attn_W_v"], "w_gv": inputs["L0_attn_W_gv"],
        "w_go": inputs["L0_attn_W_go"], "w_o": inputs["L0_attn_W_o"],
        "gate_w": inputs["L0_ffn_gate_w"],
        "ew1": inputs["L0_ffn_expert_w1"], "ew2": inputs["L0_ffn_expert_w2"],
        "ln_g": inputs["L0_ffn_ln_gamma"], "ln_b": inputs["L0_ffn_ln_beta"],
        "inw": inputs["L1_attn_in_proj_weight"],
        "inb": inputs["L1_attn_in_proj_bias"],
        "outw": inputs["L1_attn_out_proj_weight"],
        "outb": inputs["L1_attn_out_proj_bias"],
        "rhs": inputs["L1_ffn_rhs"],
        "cc": cst["cc"], "ss": cst["ss"], "tril": cst["tril"],
        "neg240": cst["neg240"], "sinkrow": cst["sinkrow"], "triu": cst["triu"],
        "iota64": cst["iota64"], "iota_t": cst["iota_t"],
        "idn": cst["idn"], "idnf": cst["idnf"],
        "ones1": cst["ones1"], "ones1f": cst["ones1f"],
        "ones64": cst["ones64"],
        "band_m": cst["band_m"], "band_0": cst["band_0"],
        "band_p": cst["band_p"], "cvals": cst["cvals"],
        "dm0": cst["dm0"], "dm1": cst["dm1"], "neg240w": cst["neg240w"],
    }
    base = {k: np.ascontiguousarray(np.asarray(v)) for k, v in base.items()}
    xfull = np.asarray(inputs["L0_attn_x"], np.float32)
    maps = []
    for b in range(8):
        m = dict(base)
        m["x"] = np.ascontiguousarray(xfull[b])
        maps.append(m)
    return maps


def kernel(**inputs):
    nc, _ = build_nc()
    maps = make_in_maps(inputs)
    res = run_bass_kernel_spmd(nc, maps, core_ids=list(range(8)))
    return np.stack([np.asarray(r["y"], np.float32) for r in res.results])


# version 07 (final)
# speedup vs baseline: 1.0075x; 1.0075x over previous
import math
import numpy as np
import ml_dtypes
import concourse.bass as bass
import concourse.mybir as mybir
from concourse.bass_utils import run_bass_kernel_spmd
from contextlib import ExitStack

F32 = mybir.dt.float32
BF16 = mybir.dt.bfloat16
I32 = mybir.dt.int32
U8 = mybir.dt.uint8
AF = mybir.ActivationFunctionType
ALU = mybir.AluOpType
AX = mybir.AxisListType

T = 2048
NT = 16
C = 1024
NC = 8
G = 4
DI = 32
H = 16
HD = 64
NSORT = 64
RROUNDS = NSORT // 8
TCA = 256
NTCA = T // TCA
NS = 16
EMOE = 8
HMOE = 2048
LN_EPS = 1e-5
SINK = 4


def _host_consts():
    MAXPOS, ORIG, BASE, SCALE = 262144, 8192, 10000.0, 32.0
    BETA_FAST, BETA_SLOW = 1.0, 32.0
    dim = HD
    ext = MAXPOS / ORIG
    scaled_base = BASE * ext ** (dim / (dim - 2))
    idx = np.arange(0, dim, 2, dtype=np.float64) / dim
    inv_freq = 1.0 / scaled_base ** idx
    freq_extra = 1.0 / BASE ** idx
    wavelen = 2.0 * math.pi / freq_extra
    ramp = np.clip((wavelen - BETA_FAST) / (BETA_SLOW - BETA_FAST), 0.0, 1.0)
    scale = 1.0 + (SCALE - 1.0) * ramp
    t = np.arange(T, dtype=np.float64)
    freqs = t[:, None] / scale[None, :] * inv_freq[None, :]
    emb = np.concatenate([freqs, freqs], axis=-1)
    cos = np.cos(emb).astype(np.float32)
    sin = np.sin(emb).astype(np.float32)
    c_half = cos[:, 0::2]
    s_half = sin[:, 0::2]
    CC1 = np.zeros((T, HD), np.float32)
    SS1 = np.zeros((T, HD), np.float32)
    CC1[:, 0::2] = c_half
    CC1[:, 1::2] = c_half
    SS1[:, 0::2] = -s_half
    SS1[:, 1::2] = s_half
    CC = np.tile(CC1, (1, H)).astype(ml_dtypes.bfloat16)
    SS = np.tile(SS1, (1, H)).astype(ml_dtypes.bfloat16)

    tril_u8 = np.tril(np.ones((128, 128), np.uint8))
    triu_u8 = (1 - tril_u8).astype(np.uint8)
    neg240 = np.full((128, 128), -240.0, ml_dtypes.bfloat16)
    sr = np.full((SINK, T), 240.0, np.float32)
    for s in range(SINK):
        sr[s, :s] = -240.0
    sinkrow = sr.astype(ml_dtypes.bfloat16)
    iota64 = np.broadcast_to(np.arange(NSORT, dtype=np.float32),
                             (128, NSORT)).copy()
    iota_t = np.arange(T, dtype=np.float32).reshape(NT, 128).T.copy()
    idn_bf16 = np.eye(128, dtype=ml_dtypes.bfloat16)
    idn_f32 = np.eye(128, dtype=np.float32)
    ones1_bf16 = np.ones((1, 128), ml_dtypes.bfloat16)
    ones1_f32 = np.ones((1, 128), np.float32)
    ones64 = np.ones((128, 64), ml_dtypes.bfloat16)
    d0 = np.zeros((128, TCA), np.uint8)
    d1 = np.zeros((128, TCA), np.uint8)
    for sl in range(128):
        d0[sl, :] = (np.arange(TCA) < sl)
        d1[sl, :] = (np.arange(TCA) < sl + 128)
    neg240w = np.full((128, TCA), -240.0, ml_dtypes.bfloat16)
    bands = {}
    for name, delta in [("band_m", -128), ("band_0", 0), ("band_p", 128)]:
        sl = np.arange(128)[:, None]
        tl = np.arange(TCA)[None, :]
        dd = sl - tl + delta
        keep = ~((dd >= -16) & (dd <= 0))
        bands[name] = keep.astype(ml_dtypes.bfloat16)
    return dict(cc=CC, ss=SS, tril=tril_u8, triu=triu_u8,
                neg240=neg240, sinkrow=sinkrow,
                iota64=iota64, iota_t=iota_t, idn=idn_bf16, idnf=idn_f32,
                ones1=ones1_bf16, ones1f=ones1_f32, ones64=ones64,
                dm0=d0, dm1=d1, neg240w=neg240w,
                cvals=np.broadcast_to(np.array(
                    [1e-5, -30.0, 0.5, 1.0, 0.0, 0.0, 0.0, 0.0],
                    np.float32), (128, 8)).copy(), **bands)


_CONSTS = None


def host_consts():
    global _CONSTS
    if _CONSTS is None:
        _CONSTS = _host_consts()
    return _CONSTS


def build_nc(dbg=(), nphase=99):
    import os
    nphase = int(os.environ.get("NPHASE", nphase))
    p2sub = int(os.environ.get("P2SUB", 9))
    nosel = bool(int(os.environ.get("P5NOSEL", 0)))
    nogo = bool(int(os.environ.get("P5NOGO", 0)))
    ntca = int(os.environ.get("P5NTC", NTCA))
    ncc5 = int(os.environ.get("P5NCC", NC))
    dbg = set(dbg)
    nc = bass.Bass("TRN2", target_bir_lowering=False)
    d = {}

    def din(name, shape, dtype=F32):
        d[name] = nc.dram_tensor(name, list(shape), dtype,
                                 kind="ExternalInput").ap()

    din("x", [T, C])
    din("w_iq", [C, 128]); din("w_ik", [C, 128]); din("w_iw", [C, G])
    din("gate_bias", [1, G]); din("head_bias", [1, H]); din("var_ema", [1, 1])
    for w in ["w_q", "w_k", "w_v", "w_gv", "w_go", "w_o"]:
        din(w, [C, C])
    din("gate_w", [EMOE, C])
    din("ew1", [EMOE, C, HMOE]); din("ew2", [EMOE, HMOE, C])
    din("ln_g", [C]); din("ln_b", [C])
    din("inw", [3 * C, C]); din("inb", [3 * C])
    din("outw", [C, C]); din("outb", [C])
    din("rhs", [C, C])
    din("cc", [T, C], BF16); din("ss", [T, C], BF16)
    din("tril", [128, 128], U8); din("triu", [128, 128], U8)
    din("neg240", [128, 128], BF16)
    din("sinkrow", [SINK, T], BF16)
    din("iota64", [128, NSORT]); din("iota_t", [128, NT])
    din("idn", [128, 128], BF16); din("idnf", [128, 128])
    din("ones1", [1, 128], BF16); din("ones1f", [1, 128])
    din("ones64", [128, 64], BF16)
    din("band_m", [128, TCA], BF16); din("band_0", [128, TCA], BF16)
    din("band_p", [128, TCA], BF16)
    din("cvals", [128, 8])
    din("dm0", [128, TCA], U8); din("dm1", [128, TCA], U8)
    din("neg240w", [128, TCA], BF16)

    y = nc.dram_tensor("y", [T, C], F32, kind="ExternalOutput").ap()
    x1_hbm = nc.dram_tensor("x1_hbm", [T, C], F32, kind="Internal").ap()
    th_hbm = nc.dram_tensor("th_hbm", [16, T], BF16, kind="Internal").ap()
    oh_hbm = nc.dram_tensor("oh_hbm", [EMOE, T], BF16, kind="Internal").ap()
    s_hbm = nc.dram_tensor("s_hbm", [C, T], F32, kind="Internal").ap()
    qxn_hbm = nc.dram_tensor("qxn_hbm", [C, T], BF16, kind="Internal").ap()

    taps = {}

    st_ctx = ExitStack()
    with st_ctx as stx:
        en = stx.enter_context

        def sb(name, shape, dtype=F32):
            return en(nc.sbuf_tensor(name, list(shape), dtype))

        def ps(name, shape, dtype=F32):
            return en(nc.psum_tensor(name, list(shape), dtype))

        sems = {}

        def sem(name):
            if name not in sems:
                sems[name] = en(nc.semaphore(name))
            return sems[name]

        xn = [sb(f"xn{c}", [128, T], BF16) for c in range(NC)]
        qi = sb("qi", [128, T], BF16)
        ki = sb("ki", [128, T], BF16)
        PL = [sb(f"pl{i}", [128, 4096]) for i in range(6)]
        wa = sb("wa", [128, 4096])
        wb = sb("wb", [128, 4096])
        bscr = sb("bscr", [128, 3072])

        def bf(t, i=0, n=2048):
            return t.ap().bitcast(BF16)[:, i * 2048:i * 2048 + n]

        wtok = sb("wtok", [128, NT * G])
        theta = sb("theta", [128, NT * H])
        sortv = sb("sortv", [128, NSORT * G])
        onehotT = sb("onehotT", [EMOE, T], BF16)
        factor_row = sb("factor_row", [1, T], BF16)
        lsc_row = sb("lsc_row", [1, T], BF16)
        amax_col = sb("amax_col", [128, NT])
        gb_bc = sb("gb_bc", [128, G])
        sbh = sb("sbh", [128, H])
        c512av = sb("c512av", [128, 1])
        iota_t_sb = sb("iota_t_sb", [128, NT])
        iota64_sb = sb("iota64_sb", [128, NSORT])
        triu_sb = sb("triu_sb", [128, 128], U8)
        sinkrow_sb = sb("sinkrow_sb", [SINK, T], BF16)
        idn_sb = sb("idn_sb", [128, 128], BF16)
        idnf_sb = sb("idnf_sb", [128, 128])
        ones1_sb = sb("ones1_sb", [1, 128], BF16)
        ones1f_sb = sb("ones1f_sb", [1, 128])
        ones64_sb = sb("ones64_sb", [128, 64], BF16)
        band_sb = {nm: sb(nm + "_sb", [128, TCA], BF16)
                   for nm in ["band_m", "band_0", "band_p"]}
        cvals = sb("cvals_sb", [128, 8])
        dm0_sb = sb("dm0_sb", [128, TCA], U8)
        dm1_sb = sb("dm1_sb", [128, TCA], U8)
        neg240w_sb = sb("neg240w_sb", [128, TCA], BF16)
        small = sb("small", [128, 64])
        small2 = sb("small2", [128, 64])
        scr8 = sb("scr8", [128, 8], BF16)
        ln_g_sb = sb("ln_g_sb", [128, NC])
        ln_b_sb = sb("ln_b_sb", [128, NC])
        inb_sb = sb("inb_sb", [128, 24])
        outb_sb = sb("outb_sb", [128, NC])

        pbig = ps("pbig", [128, 2048])
        pa = ps("pa", [128, 512])
        pb = ps("pb", [128, 512])
        ptr = ps("ptr", [128, 1024], BF16)
        ptr2 = ps("ptr2", [128, 1024], BF16)

        block = en(nc.Block())

        LD = sem("LD")
        nld = [0]

        @block.sync
        def _(sync):
            loads = [
                (gb_bc[0:1, :], d["gate_bias"]), (sbh[0:1, :], d["head_bias"]),
                (c512av[0:1, :], d["var_ema"]), (iota_t_sb[:], d["iota_t"]),
                (iota64_sb[:], d["iota64"]),
                (triu_sb[:], d["triu"]),
                (sinkrow_sb[:], d["sinkrow"]),
                (idn_sb[:], d["idn"]), (idnf_sb[:], d["idnf"]),
                (ones1_sb[:], d["ones1"]), (ones1f_sb[:], d["ones1f"]),
                (ones64_sb[:], d["ones64"]),
                (band_sb["band_m"][:], d["band_m"]),
                (band_sb["band_0"][:], d["band_0"]),
                (band_sb["band_p"][:], d["band_p"]),
                (cvals[:], d["cvals"]),
                (dm0_sb[:], d["dm0"]), (dm1_sb[:], d["dm1"]),
                (neg240w_sb[:], d["neg240w"]),
            ]
            for dst, src in loads:
                sync.dma_start(out=dst, in_=src).then_inc(LD, 16)
                nld[0] += 1
            for nm, dst in [("ln_g", ln_g_sb), ("ln_b", ln_b_sb),
                            ("outb", outb_sb), ("inb", inb_sb)]:
                sync.dma_start(out=dst[:],
                               in_=d[nm].rearrange("(c p) -> p c", p=128),
                               allow_slow_non_contiguous=True).then_inc(LD, 16)
                nld[0] += 1
        NCONST = nld[0]

        P0T = sem("P0T"); P0V = sem("P0V"); P0S = sem("P0S")

        @block.tensor
        def _(tensor):
            tensor.wait_ge(LD, NCONST * 16)
            tensor.matmul(pa[:, 0:G], ones1f_sb[:], gb_bc[0:1, :],
                          start=True, stop=True)
            tensor.matmul(pa[:, 16:16 + H], ones1f_sb[:], sbh[0:1, :],
                          start=True, stop=True)
            tensor.matmul(pa[:, 48:49], ones1f_sb[:], c512av[0:1, :],
                          start=True, stop=True).then_inc(P0T, 1)

        @block.vector
        def _(vector):
            vector.wait_ge(P0T, 1)
            vector.tensor_copy(gb_bc[:], pa[:, 0:G])
            vector.tensor_copy(sbh[:], pa[:, 16:16 + H])
            vector.tensor_scalar(out=small[:, 0:1], in0=pa[:, 48:49],
                                 scalar1=1e-6, scalar2=None,
                                 op0=ALU.max).then_inc(P0V, 1)

        @block.vector
        def _(vector):
            vector.wait_ge(P0V, 1)
            vector.reciprocal(small[:, 1:2], small[:, 0:1]).then_inc(P0V, 1)
            vector.wait_ge(P0V, 2)
            vector.tensor_scalar(out=c512av[:], in0=small[:, 1:2],
                                 scalar1=512.0, scalar2=None,
                                 op0=ALU.mult).then_inc(P0V, 1)

        @block.scalar
        def _(scalar):
            scalar.wait_ge(P0V, 3)
            scalar.activation(out=sbh[:], in_=sbh[:],
                              func=AF.Sigmoid).then_inc(P0S, 1)

        def emit_ln(tag, xsrc_hbm, out_chunks):
            ld_s = sem(tag + "ld")
            v_s = sem(tag + "v")
            s_s = sem(tag + "s")
            t_s = sem(tag + "t")
            e_s = sem(tag + "e")

            @block.sync
            def _(sync):
                for tt in range(NT):
                    sync.wait_ge(ld_s, 16 * tt)
                    if tt >= 2:
                        sync.wait_ge(v_s, 5 * (tt - 2) + 5)
                    dst = wa[:, (tt % 2) * 1024:(tt % 2) * 1024 + 1024]
                    sync.dma_start(out=dst,
                                   in_=xsrc_hbm[tt * 128:(tt + 1) * 128, :]
                                   ).then_inc(ld_s, 16)

            @block.vector
            def _(vector):
                for tt in range(NT):
                    vector.wait_ge(ld_s, 16 * (tt + 1))
                    xt = wa[:, (tt % 2) * 1024:(tt % 2) * 1024 + 1024]
                    vector.tensor_reduce(small[:, 0:1], xt, AX.X,
                                         ALU.add).then_inc(v_s, 1)
                    vector.wait_ge(v_s, 5 * tt + 1)
                    vector.tensor_scalar(out=small[:, 1:2], in0=small[:, 0:1],
                                         scalar1=1.0 / C, scalar2=None,
                                         op0=ALU.mult).then_inc(v_s, 1)
                    vector.wait_ge(s_s, 3 * tt + 2)
                    vector.scalar_tensor_tensor(
                        out=small[:, 4:5], in0=small[:, 2:3], scalar=1.0 / C,
                        in1=small[:, 3:4], op0=ALU.mult,
                        op1=ALU.subtract).then_inc(v_s, 1)
                    vector.wait_ge(s_s, 3 * tt + 3)
                    vector.reciprocal(small[:, 5:6],
                                      small[:, 6:7]).then_inc(v_s, 1)
                    vector.wait_ge(v_s, 5 * tt + 4)
                    if tt >= 1:
                        vector.wait_ge(t_s, tt)
                    vector.tensor_scalar(out=bf(bscr, 0, 1024), in0=xt,
                                         scalar1=small[:, 1:2],
                                         scalar2=small[:, 5:6],
                                         op0=ALU.subtract,
                                         op1=ALU.mult).then_inc(v_s, 1)

            def evac(scalar, j):
                scalar.wait_ge(t_s, j + 1)
                pdst = ptr if j % 2 == 0 else ptr2
                for c in range(NC):
                    ins = scalar.copy(out_chunks[c][:, j * 128:(j + 1) * 128],
                                      pdst[:, c * 128:(c + 1) * 128])
                ins.then_inc(e_s, 1)

            @block.scalar
            def _(scalar):
                for tt in range(NT):
                    if tt >= 1:
                        evac(scalar, tt - 1)
                    scalar.wait_ge(ld_s, 16 * (tt + 1))
                    if tt >= 1:
                        scalar.wait_ge(v_s, 5 * (tt - 1) + 3)
                    xt = wa[:, (tt % 2) * 1024:(tt % 2) * 1024 + 1024]
                    scalar.activation(out=wb[:, 0:1024], in_=xt,
                                      func=AF.Square,
                                      accum_out=small[:, 2:3]).then_inc(s_s, 1)
                    scalar.wait_ge(v_s, 5 * tt + 2)
                    scalar.activation(out=small[:, 3:4], in_=small[:, 1:2],
                                      func=AF.Square).then_inc(s_s, 1)
                    scalar.wait_ge(v_s, 5 * tt + 3)
                    scalar.activation(out=small[:, 6:7], in_=small[:, 4:5],
                                      func=AF.Sqrt, bias=cvals[:, 0:1],
                                      scale=1.0).then_inc(s_s, 1)
                evac(scalar, NT - 1)

            @block.tensor
            def _(tensor):
                for tt in range(NT):
                    tensor.wait_ge(v_s, 5 * tt + 5)
                    if tt >= 2:
                        tensor.wait_ge(e_s, tt - 1)
                    pdst = ptr if tt % 2 == 0 else ptr2
                    for c in range(NC):
                        ins = tensor.transpose(
                            pdst[:, c * 128:(c + 1) * 128],
                            bf(bscr, 0, 1024)[:, c * 128:(c + 1) * 128],
                            idn_sb[:])
                    ins.then_inc(t_s, 1)
            return e_s

        ln0_done = emit_ln("LN0", d["x"], xn)

        emit_p2 = nphase >= 2
        P2ld = sem("P2ld"); P2v = sem("P2v"); P2t = sem("P2t"); P2s = sem("P2s")
        if emit_p2:
            wiq_bf = bf(bscr, 0, 1024)
            wik_bf = bf(bscr, 0, 2048)[:, 1024:2048]
            wiw_bf = bf(bscr, 1, 1024)[:, 0:32]
            gw_bf = bf(bscr, 1, 1024)[:, 64:128]
            gwrow_bf = bf(bscr, 1, 2048)[:, 1024:2048]
            combo_bf = bf(bscr, 2, 1024)[:, 0:64]

            def scr_lg(tt):
                return wb[:, 1024 + (tt % 2) * 16:1024 + (tt % 2) * 16 + EMOE]

            @block.sync
            def _(sync):
                sync.wait_ge(ln0_done, NT)
                sync.dma_start(out=wa[:, 0:1024].rearrange("p (c m) -> p c m", m=128),
                               in_=d["w_iq"].rearrange("(c p) m -> p c m", p=128)
                               ).then_inc(P2ld, 16)
                sync.wait_ge(P2ld, 16)
                sync.dma_start(out=wa[:, 1024:2048].rearrange("p (c m) -> p c m", m=128),
                               in_=d["w_ik"].rearrange("(c p) m -> p c m", p=128)
                               ).then_inc(P2ld, 16)
                sync.wait_ge(P2ld, 32)
                sync.dma_start(out=wa[:, 2048:2080].rearrange("p (c m) -> p c m", m=G),
                               in_=d["w_iw"].rearrange("(c p) m -> p c m", p=128)
                               ).then_inc(P2ld, 16)
                sync.wait_ge(P2ld, 48)
                sync.dma_start(out=wa[0:8, 2560:3584],
                               in_=d["gate_w"]).then_inc(P2ld, 16)

            @block.vector
            def _(vector):
                vector.wait_ge(P2ld, 16)
                vector.tensor_copy(wiq_bf, wa[:, 0:1024]).then_inc(P2v, 1)
                vector.wait_ge(P2ld, 32)
                vector.tensor_copy(wik_bf, wa[:, 1024:2048]).then_inc(P2v, 1)
                vector.wait_ge(P2ld, 48)
                vector.tensor_copy(wiw_bf, wa[:, 2048:2080]).then_inc(P2v, 1)
                vector.wait_ge(P2ld, 64)
                vector.tensor_copy(gwrow_bf[0:8, :], wa[0:8, 2560:3584]
                                   ).then_inc(P2v, 1)

            @block.tensor
            def _(tensor):
                tensor.wait_ge(P2v, 4)
                for c in range(NC):
                    ins = tensor.transpose(ptr[:, c * 8:(c + 1) * 8],
                                           gwrow_bf[0:8, c * 128:(c + 1) * 128],
                                           idn_sb[0:8, 0:8])
                ins.then_inc(P2t, 1)
                for tq in range(4):
                    for c in range(NC):
                        ins = tensor.matmul(pbig[:, tq * 512:(tq + 1) * 512],
                                            wiq_bf[:, c * 128:(c + 1) * 128],
                                            xn[c][:, tq * 512:(tq + 1) * 512],
                                            start=(c == 0), stop=(c == NC - 1))
                ins.then_inc(P2t, 1)
                tensor.wait_ge(P2s, 2)
                for tq in range(4):
                    for c in range(NC):
                        ins = tensor.matmul(pbig[:, tq * 512:(tq + 1) * 512],
                                            wik_bf[:, c * 128:(c + 1) * 128],
                                            xn[c][:, tq * 512:(tq + 1) * 512],
                                            start=(c == 0), stop=(c == NC - 1))
                ins.then_inc(P2t, 2)
                for tt in range(NT if p2sub >= 3 else 0):
                    if tt >= 1:
                        tensor.wait_ge(P2s, 3 + 4 * (tt - 1) + 2)
                    for c in range(NC):
                        tensor.matmul(pa[:, 0:G],
                                      xn[c][:, tt * 128:(tt + 1) * 128],
                                      wiw_bf[:, c * G:(c + 1) * G],
                                      start=(c == 0), stop=(c == NC - 1))
                    for c in range(NC):
                        ins = tensor.matmul(pb[:, 0:EMOE],
                                            xn[c][:, tt * 128:(tt + 1) * 128],
                                            gw_bf[:, c * EMOE:(c + 1) * EMOE],
                                            start=(c == 0), stop=(c == NC - 1))
                    ins.then_inc(P2t, 1)
                    tensor.wait_ge(P2v, 4 + 7 * tt + 7)
                    if tt >= 1:
                        tensor.wait_ge(P2s, 3 + 4 * (tt - 1) + 4)
                    tensor.transpose(ptr2[0:64, 0:128], combo_bf[:, 0:64],
                                     idn_sb[:]).then_inc(P2t, 1)

            @block.scalar
            def _(scalar):
                scalar.wait_ge(P2t, 1)
                for c in range(NC):
                    ins = scalar.copy(gw_bf[:, c * 8:(c + 1) * 8],
                                      ptr[:, c * 8:(c + 1) * 8])
                ins.then_inc(P2s, 1)
                scalar.wait_ge(P2t, 2)
                for tq in range(4):
                    ins = scalar.copy(qi[:, tq * 512:(tq + 1) * 512],
                                      pbig[:, tq * 512:(tq + 1) * 512])
                ins.then_inc(P2s, 1)
                scalar.wait_ge(P2t, 4)
                for tq in range(4):
                    ins = scalar.copy(ki[:, tq * 512:(tq + 1) * 512],
                                      pbig[:, tq * 512:(tq + 1) * 512])
                ins.then_inc(P2s, 1)
                for tt in range(NT if p2sub >= 3 else 0):
                    base_s = 3 + 4 * tt
                    scalar.wait_ge(P2t, 5 + 2 * tt)
                    scalar.activation(out=wtok[:, G * tt:G * (tt + 1)],
                                      in_=pa[:, 0:G],
                                      func=AF.Sigmoid).then_inc(P2s, 1)
                    lg = scr_lg(tt)
                    scalar.copy(lg, pb[:, 0:EMOE]).then_inc(P2s, 1)
                    scalar.wait_ge(P2v, 4 + 7 * tt + 2)
                    scalar.activation(out=wb[:, 0:EMOE], in_=lg, func=AF.Exp,
                                      bias=small[:, 1:2],
                                      accum_out=small[:, 2:3]).then_inc(P2s, 1)
                    scalar.wait_ge(P2t, 6 + 2 * tt)
                    scalar.copy(onehotT[:, tt * 128:(tt + 1) * 128],
                                ptr2[0:EMOE, 0:128])
                    scalar.copy(factor_row[:, tt * 128:(tt + 1) * 128],
                                ptr2[32:33, 0:128]).then_inc(P2s, 1)

            @block.vector
            def _(vector):
                for tt in range(NT if p2sub >= 3 else 0):
                    base_s = 3 + 4 * tt
                    vector.wait_ge(P2s, base_s + 2)
                    lg = scr_lg(tt)
                    vector.tensor_reduce(small[:, 0:1], lg, AX.X,
                                         ALU.max).then_inc(P2v, 1)
                    vector.wait_ge(P2v, 4 + 7 * tt + 1)
                    vector.tensor_scalar(out=small[:, 1:2], in0=small[:, 0:1],
                                         scalar1=-1.0, scalar2=None,
                                         op0=ALU.mult).then_inc(P2v, 1)
                    vector.wait_ge(P2s, base_s + 3)
                    vector.reciprocal(small[:, 3:4],
                                      small[:, 2:3]).then_inc(P2v, 1)
                    vector.wait_ge(P2v, 4 + 7 * tt + 3)
                    vector.tensor_scalar(out=small[:, 4:5], in0=small[:, 3:4],
                                         scalar1=1e-8, scalar2=None,
                                         op0=ALU.add).then_inc(P2v, 1)
                    vector.wait_ge(P2v, 4 + 7 * tt + 4)
                    vector.reciprocal(small[:, 5:6],
                                      small[:, 4:5]).then_inc(P2v, 1)
                    vector.wait_ge(P2v, 4 + 7 * tt + 5)
                    vector.tensor_mul(small[:, 6:7], small[:, 3:4], small[:, 5:6])
                    vector.tensor_scalar(out=small2[:, 0:EMOE], in0=lg,
                                         scalar1=small[:, 0:1], scalar2=None,
                                         op0=ALU.is_ge).then_inc(P2v, 1)
                    vector.wait_ge(P2v, 4 + 7 * tt + 6)
                    vector.tensor_copy(combo_bf[:, 0:8], small2[:, 0:8])
                    vector.tensor_copy(combo_bf[:, 32:33],
                                       small[:, 6:7]).then_inc(P2v, 1)

        if nphase >= 3:
            P3t = sem("P3t"); P3s = sem("P3s"); P3v = sem("P3v")
            VPT = 139
            lts_t = [bf(PL[0], g) for g in range(G)]
            mg_scr = bf(wa, 0)
            extA = bf(bscr, 0)
            extB = bf(bscr, 1)
            p2_end = (3 + 4 * (NT if p2sub >= 3 else 0)) if emit_p2 else 0

            def vmark(tt, k):
                return VPT * tt + k

            @block.tensor
            def _(tensor):
                for tt in range(NT):
                    EXT = 128 * (tt + 1)
                    NSCc = (EXT + 511) // 512
                    for g in range(G):
                        if tt == 0 and g == 0 and emit_p2:
                            tensor.wait_ge(P2s, p2_end)
                        tensor.wait_ge(P3s, 12 * tt + g)
                        if g == 0 and tt >= 1:
                            tensor.wait_ge(P3v, VPT * (tt - 1) + 72)
                        for sc in range(NSCc):
                            n = min(512, EXT - 512 * sc)
                            ins = tensor.matmul(
                                pbig[:, 512 * sc:512 * sc + n],
                                qi[32 * g:32 * g + 32,
                                   tt * 128:(tt + 1) * 128],
                                ki[32 * g:32 * g + 32, 512 * sc:512 * sc + n],
                                start=True, stop=True,
                                tile_position=(32 * g, 0))
                        ins.then_inc(P3t, 1)

            @block.scalar
            def _(scalar):
                for tt in range(NT):
                    EXT = 128 * (tt + 1)
                    NSCc = (EXT + 511) // 512
                    for g in range(G):
                        scalar.wait_ge(P3t, 4 * tt + g + 1)
                        for sc in range(NSCc):
                            n = min(512, EXT - 512 * sc)
                            ins = scalar.copy(
                                lts_t[g][:, 512 * sc:512 * sc + n],
                                pbig[:, 512 * sc:512 * sc + n])
                        ins.then_inc(P3s, 1)
                    for g in range(G):
                        scalar.wait_ge(P3v, vmark(tt, 18 * g + 1))
                        scalar.activation(out=mg_scr[:, 0:EXT],
                                          in_=lts_t[g][:, 0:EXT],
                                          func=AF.Sigmoid,
                                          bias=gb_bc[:, g:g + 1],
                                          accum_out=small2[:, 32 + g:33 + g]
                                          ).then_inc(P3s, 1)
                        scalar.wait_ge(P3s, 12 * tt + 5 + 2 * g)
                        scalar.activation(out=bf(wa, 1)[:, 0:EXT],
                                          in_=mg_scr[:, 0:EXT],
                                          func=AF.Square,
                                          accum_out=small2[:, 36 + g:37 + g]
                                          ).then_inc(P3s, 1)

            @block.vector
            def _(vector):
                vc = [0]

                def ch(ins):
                    ins.then_inc(P3v, 1)
                    vc[0] += 1
                    vector.wait_ge(P3v, vc[0])

                for tt in range(NT):
                    EXT = 128 * (tt + 1)
                    for g in range(G):
                        vector.wait_ge(P3s, 12 * tt + g + 1)
                        dslice = lts_t[g][:, tt * 128:(tt + 1) * 128]
                        ch(vector.copy_predicated(out=dslice, mask=triu_sb[:],
                                                  data=neg240w_sb[:, 0:128]))
                        vector.wait_ge(P3s, 12 * tt + 6 + 2 * g)
                        ch(vector.memset(lts_t[g][:, 0:SINK], -240.0))
                        cur = lts_t[g]
                        for r in range(RROUNDS):
                            nxt = extA if r % 2 == 0 else extB
                            ch(vector.max(out=scr8[:], in_=cur[:, 0:EXT]))
                            vector.tensor_copy(
                                sortv[:, NSORT * g + 8 * r:
                                      NSORT * g + 8 * r + 8], scr8[:])
                            ch(vector.match_replace(out=nxt[:, 0:EXT],
                                                    in_to_replace=scr8[:],
                                                    in_values=cur[:, 0:EXT],
                                                    imm_value=-240.0))
                            cur = nxt
                    vector.wait_ge(P3s, 12 * tt + 12)
                    s1c = small2[:, 32:36]
                    s2c = small2[:, 36:40]
                    m1 = small2[:, 40:44]
                    m1sq = small2[:, 44:48]
                    varg = small2[:, 48:52]
                    vs = small[:, 16:32]
                    vs2 = small[:, 32:48]
                    kraw = small2[:, 0:16]
                    ch(vector.tensor_scalar(out=m1, in0=s1c, scalar1=1.0 / T,
                                            scalar2=None, op0=ALU.mult))
                    ch(vector.tensor_mul(m1sq, m1, m1))
                    ch(vector.scalar_tensor_tensor(out=varg, in0=s2c,
                                                   scalar=1.0 / T, in1=m1sq,
                                                   op0=ALU.mult,
                                                   op1=ALU.subtract))
                    for g in range(G):
                        ch(vector.tensor_scalar(
                            out=vs[:, 4 * g:4 * g + 4],
                            in0=sbh[:, 4 * g:4 * g + 4],
                            scalar1=wtok[:, G * tt + g:G * tt + g + 1],
                            scalar2=None, op0=ALU.mult))
                    ch(vector.tensor_mul(vs2, vs, vs))
                    for g in range(G):
                        ch(vector.tensor_scalar(
                            out=kraw[:, 4 * g:4 * g + 4],
                            in0=vs2[:, 4 * g:4 * g + 4],
                            scalar1=varg[:, g:g + 1],
                            scalar2=None, op0=ALU.mult))
                    ch(vector.tensor_scalar(out=kraw, in0=kraw,
                                            scalar1=c512av[:, 0:1],
                                            scalar2=None, op0=ALU.mult))
                    t15 = small[:, 48:49]
                    ch(vector.tensor_scalar(out=t15,
                                            in0=iota_t_sb[:, tt:tt + 1],
                                            scalar1=1.5, scalar2=None,
                                            op0=ALU.add))
                    ch(vector.tensor_scalar(out=kraw, in0=kraw, scalar1=32.0,
                                            scalar2=1024.5, op0=ALU.max,
                                            op1=ALU.min))
                    ch(vector.tensor_scalar(out=kraw, in0=kraw, scalar1=t15,
                                            scalar2=float(NSORT + 4.5),
                                            op0=ALU.min, op1=ALU.min))
                    c2m6 = small[:, 0:16]
                    c2m5 = small2[:, 16:32]
                    ch(vector.tensor_scalar(out=c2m6, in0=kraw, scalar1=-6.0,
                                            scalar2=None, op0=ALU.add))
                    ch(vector.tensor_scalar(out=c2m5, in0=kraw, scalar1=-5.0,
                                            scalar2=None, op0=ALU.add))
                    acc1 = small2[:, 52:53]
                    acc2 = small2[:, 53:54]
                    th = theta[:, H * tt:H * (tt + 1)]
                    for h in range(H):
                        g = h // 4
                        ins = vector.scalar_tensor_tensor(
                            out=mg_scr[:, 0:NSORT], in0=iota64_sb[:],
                            scalar=c2m6[:, h:h + 1],
                            in1=sortv[:, NSORT * g:NSORT * (g + 1)],
                            op0=ALU.is_gt, op1=ALU.mult, accum_out=acc1)
                        ins.then_inc(P3v, 1); vc[0] += 1
                        ins = vector.scalar_tensor_tensor(
                            out=mg_scr[:, 64:64 + NSORT], in0=iota64_sb[:],
                            scalar=c2m5[:, h:h + 1],
                            in1=sortv[:, NSORT * g:NSORT * (g + 1)],
                            op0=ALU.is_gt, op1=ALU.mult, accum_out=acc2)
                        ins.then_inc(P3v, 1); vc[0] += 1
                        vector.wait_ge(P3v, vc[0])
                        ch(vector.tensor_sub(th[:, h:h + 1], acc1, acc2))
                    ch(vector.tensor_scalar(out=th, in0=th, scalar1=-200.0,
                                            scalar2=None, op0=ALU.max))
                    assert vc[0] == VPT * (tt + 1), (tt, vc[0])

        if nphase >= 4:
            qf_t = [bf(PL[1], i) for i in range(4)] + [bf(PL[2], i - 4) for i in range(4, NC)]
            kf_t = [bf(PL[3], i) for i in range(4)] + [bf(PL[4], i - 4) for i in range(4, NC)]
            vtok_t = ([bf(PL[5], 0, 8192)[:, i * 1024:(i + 1) * 1024] for i in range(8)]
                      + [bf(PL[0], 0, 8192)[:, i * 1024:(i + 1) * 1024] for i in range(8)])
            wbig = bf(wa, 0, 8192)
            wbig2 = bf(wb, 0, 8192)
            bv = bf(bscr, 0, 6144)
            p3_end = 139 * NT if nphase >= 3 else 0

            def emit_wload(tag, w_hbm, dst_view, stage, nchunks=8, pre=None,
                           pre_v=None):
                lw = sem(tag + "lw")
                cw = sem(tag + "cw")

                @block.sync
                def _(sync):
                    if pre is not None:
                        sync.wait_ge(pre[0], pre[1])
                    for c in range(nchunks):
                        sync.wait_ge(lw, 16 * c)
                        if c >= 2:
                            sync.wait_ge(cw, c - 1)
                        sync.dma_start(out=stage(c % 2),
                                       in_=w_hbm[c * 128:(c + 1) * 128, :]
                                       ).then_inc(lw, 16)

                @block.vector
                def _(vector):
                    if pre_v is not None:
                        vector.wait_ge(pre_v[0], pre_v[1])
                    for c in range(nchunks):
                        vector.wait_ge(lw, 16 * (c + 1))
                        vector.tensor_copy(
                            dst_view[:, c * 1024:(c + 1) * 1024],
                            stage(c % 2)).then_inc(cw, 1)
                return cw

            stage_wb = lambda p: wb[:, p * 1024:p * 1024 + 1024]
            stage_bs = lambda p: bscr[:, p * 1024:p * 1024 + 1024]

            def emit_qk_pass(tag, w_hbm, out_chunks, first, prev=None):
                cw = emit_wload(tag, w_hbm, wbig, stage_wb,
                                pre=(P3v, p3_end) if first and nphase >= 3
                                else (sems["P4qcw"], NC))
                t_s = sem(tag + "t")
                s_s = sem(tag + "s")
                v_s = sem(tag + "v")
                ld2 = sem(tag + "ld2")
                ccb = lambda p: bv[:, p * 2048:p * 2048 + 1024]
                ssb = lambda p: bv[:, p * 2048 + 1024:p * 2048 + 2048]
                qtok = bv[:, 4096:5120]
                rot = bv[:, 5120:6144]

                @block.sync
                def _(sync):
                    if first and nphase >= 3:
                        sync.wait_ge(P3v, p3_end)
                    for tt in range(NT):
                        sync.wait_ge(ld2, 32 * tt)
                        if tt >= 2:
                            sync.wait_ge(v_s, 3 * (tt - 2) + 3)
                        sync.dma_start(out=ccb(tt % 2),
                                       in_=d["cc"][tt * 128:(tt + 1) * 128, :]
                                       ).then_inc(ld2, 16)
                        sync.dma_start(out=ssb(tt % 2),
                                       in_=d["ss"][tt * 128:(tt + 1) * 128, :]
                                       ).then_inc(ld2, 16)

                @block.tensor
                def _(tensor):
                    tensor.wait_ge(cw, NC)
                    if first and nphase >= 3:
                        tensor.wait_ge(P3v, p3_end)
                    if prev is not None:
                        tensor.wait_ge(prev, 2 * NT)
                    for tt in range(NT):
                        tensor.wait_ge(s_s, 2 * tt)
                        for c in range(NC):
                            for half in range(2):
                                ins = tensor.matmul(
                                    pbig[:, half * 512:half * 512 + 512],
                                    xn[c][:, tt * 128:(tt + 1) * 128],
                                    wbig[:, c * 1024 + half * 512:
                                         c * 1024 + half * 512 + 512],
                                    start=(c == 0), stop=(c == NC - 1))
                        ins.then_inc(t_s, 1)
                        tensor.wait_ge(v_s, 3 * tt + 3)
                        if tt >= 2:
                            tensor.wait_ge(s_s, 2 * (tt - 1))
                        pdst = ptr if tt % 2 == 0 else ptr2
                        for c in range(NC):
                            ins = tensor.transpose(
                                pdst[:, c * 128:(c + 1) * 128],
                                rot[:, c * 128:(c + 1) * 128], idn_sb[:])
                        ins.then_inc(t_s, 1)

                @block.scalar
                def _(scalar):
                    for tt in range(NT):
                        scalar.wait_ge(t_s, 2 * tt + 1)
                        if tt >= 1:
                            scalar.wait_ge(v_s, 3 * tt)
                        scalar.copy(qtok[:, 0:512], pbig[:, 0:512])
                        scalar.copy(qtok[:, 512:1024],
                                    pbig[:, 512:1024]).then_inc(s_s, 1)
                        scalar.wait_ge(t_s, 2 * tt + 2)
                        pdst = ptr if tt % 2 == 0 else ptr2
                        for c in range(NC):
                            ins = scalar.copy(
                                out_chunks[c][:, tt * 128:(tt + 1) * 128],
                                pdst[:, c * 128:(c + 1) * 128])
                        ins.then_inc(s_s, 1)

                @block.vector
                def _(vector):
                    q3 = qtok.rearrange("p (a b) -> p a b", b=2)
                    t2 = wbig2[:, 6144:7168]
                    t23 = t2.rearrange("p (a b) -> p a b", b=2)
                    for tt in range(NT):
                        vector.wait_ge(s_s, 2 * tt + 1)
                        vector.wait_ge(ld2, 32 * (tt + 1))
                        ss3 = ssb(tt % 2).rearrange("p (a b) -> p a b", b=2)
                        vector.tensor_tensor(rot[:], qtok[:], ccb(tt % 2),
                                             ALU.mult).then_inc(v_s, 1)
                        vector.tensor_tensor(t23, q3[:, :, ::-1], ss3,
                                             ALU.mult).then_inc(v_s, 1)
                        vector.wait_ge(v_s, 3 * tt + 2)
                        vector.tensor_add(rot[:], rot[:],
                                          t2).then_inc(v_s, 1)

            emit_qk_pass("P4q", d["w_q"], qf_t, True)
            P4qt = sems["P4qt"]

            @block.vector
            def _(vector):
                vector.wait_ge(P4qt, 2 * NT)

            emit_qk_pass("P4k", d["w_k"], kf_t, False, prev=sems["P4qs"])
            P4kt = sems["P4kt"]

            @block.vector
            def _(vector):
                vector.wait_ge(P4kt, 2 * NT)

            cwv = emit_wload("P4wv", d["w_v"], wbig, stage_wb,
                             pre=(sems["P4kcw"], NC))
            cwg = emit_wload("P4wg", d["w_gv"], wbig2, stage_bs,
                             pre=(sems["P4kv"], 3 * NT),
                             pre_v=(sems["P4wvcw"], NC))

            P4vt = sem("P4vt"); P4vs = sem("P4vs"); P4vv = sem("P4vv")

            @block.tensor
            def _(tensor):
                tensor.wait_ge(cwv, NC)
                tensor.wait_ge(cwg, NC)
                tensor.wait_ge(sems["P4ks"], 2 * NT)
                for tt in range(NT):
                    tensor.wait_ge(P4vs, tt)
                    for c in range(NC):
                        for half in range(2):
                            tensor.matmul(
                                pbig[:, half * 512:half * 512 + 512],
                                xn[c][:, tt * 128:(tt + 1) * 128],
                                wbig[:, c * 1024 + half * 512:
                                     c * 1024 + half * 512 + 512],
                                start=(c == 0), stop=(c == NC - 1))
                        for half in range(2):
                            ins = tensor.matmul(
                                pbig[:, 1024 + half * 512:1024 + half * 512 + 512],
                                xn[c][:, tt * 128:(tt + 1) * 128],
                                wbig2[:, c * 1024 + half * 512:
                                      c * 1024 + half * 512 + 512],
                                start=(c == 0), stop=(c == NC - 1))
                    ins.then_inc(P4vt, 1)

            @block.scalar
            def _(scalar):
                va = bv[:, 4096:5120]
                gs = bv[:, 5120:6144]
                for tt in range(NT):
                    scalar.wait_ge(P4vt, tt + 1)
                    if tt >= 1:
                        scalar.wait_ge(P4vv, tt)
                    scalar.copy(va[:, 0:512], pbig[:, 0:512])
                    scalar.copy(va[:, 512:1024], pbig[:, 512:1024])
                    scalar.activation(out=gs[:, 0:512], in_=pbig[:, 1024:1536],
                                      func=AF.Sigmoid)
                    scalar.activation(out=gs[:, 512:1024],
                                      in_=pbig[:, 1536:2048],
                                      func=AF.Sigmoid).then_inc(P4vs, 1)

            @block.vector
            def _(vector):
                va = bv[:, 4096:5120]
                gs = bv[:, 5120:6144]
                for tt in range(NT):
                    vector.wait_ge(P4vs, tt + 1)
                    vector.tensor_tensor(vtok_t[tt], va, gs,
                                         ALU.mult).then_inc(P4vv, 1)


        def mk_sched(tag):
            semset = {e: sem(tag + e) for e in "tsvde"}
            sched = []
            cnts = {e: 0 for e in "tsvde"}

            def em(eng, fn, deps=(), extra=(), par=False):
                cnts[eng] += 1
                flat = []
                for dd in deps:
                    if dd is None:
                        continue
                    if isinstance(dd, list):
                        flat.extend(x for x in dd if x is not None)
                    else:
                        flat.append(dd)
                assert all(isinstance(x, dict) for x in flat), flat
                it = dict(eng=eng, fn=fn, deps=flat,
                          extra=list(extra), val=cnts[eng], par=par)
                sched.append(it)
                return it

            def replay_all():
                gend = {}
                for lane in "de":
                    dit = [it for it in sched if it["eng"] == lane]
                    g2 = {}
                    i = 0
                    while i < len(dit):
                        j = i
                        while j + 1 < len(dit) and dit[j + 1]["par"]:
                            j += 1
                        for k2 in range(i, j + 1):
                            g2[dit[k2]["val"]] = dit[j]["val"]
                        i = j + 1
                    gend[lane] = g2

                def replay(eng_key):
                    def run(e):
                        for it in sched:
                            if it["eng"] != eng_key:
                                continue
                            if eng_key in "de" and not it["par"]:
                                e.wait_ge(semset[eng_key],
                                          (it["val"] - 1) * 16)
                            for dd in it["deps"]:
                                e.wait_ge(semset[dd["eng"]],
                                          gend[dd["eng"]][dd["val"]] * 16
                                          if dd["eng"] in "de"
                                          else dd["val"])
                            for sm, vv in it["extra"]:
                                e.wait_ge(sems[sm] if isinstance(sm, str)
                                          else sm, vv)
                            ins = it["fn"](e)
                            ins.then_inc(semset[eng_key],
                                         16 if eng_key in "de" else 1)
                    return run
                block.tensor(replay("t"))
                block.scalar(replay("s"))
                block.vector(replay("v"))
                block.gpsimd(replay("d"))
                if cnts["e"]:
                    block.sync(replay("e"))
            return semset, sched, cnts, em, replay_all

        if nphase >= 5:
            throws = sb("throws", [16, T], BF16)
            P5, sched, cnt5, em, replay_all5 = mk_sched("P5")
            scores_ps = ptr.ap().bitcast(F32)
            lst_ps = ptr2.ap().bitcast(F32)
            bv5 = bf(bscr, 0, 6144)
            thbc = lambda h: bv5[:, 256 * h:256 * h + 256]
            s0pre = lambda k: bv5[:, 256 * k:256 * k + 256]
            lstb = lambda g: bv5[:, 4096 + 256 * g:4096 + 256 * g + 256]
            e_sl = lambda i: bv5[:, 5120 + 256 * i:5120 + 256 * i + 256]
            sel_sl = lambda i: bv5[:, 5632 + 256 * i:5632 + 256 * i + 256]
            s0stage = lambda i: bscr[:, 2048 + 256 * i:2048 + 256 * i + 256]
            rsum_bc = sortv[:]
            cw_go = emit_wload("P5go", d["w_go"], wbig, stage_wb,
                               pre=(sems["P4vv"], NT),
                               pre_v=(sems["P4vt"], NT))
            cw_o = emit_wload("P5o", d["w_o"], wbig2, stage_bs,
                              pre=(cw_go, NC), pre_v=(cw_go, NC))

            pro_prev = None
            for tt in range(NT):
                itt = em("t", (lambda tt: lambda e: e.transpose(
                    lst_ps[0:16, 0:128],
                    theta[:, 16 * tt:16 * tt + 16], idnf_sb[:]))(tt),
                    deps=[pro_prev],
                    extra=[(P3v, 139 * NT), (sems["P4ks"], 2 * NT)]
                    if tt == 0 else [])
                pro_prev = em("s", (lambda tt: lambda e: e.copy(
                    throws[:, tt * 128:(tt + 1) * 128],
                    lst_ps[0:16, 0:128]))(tt), deps=[itt])
            pro_last = pro_prev
            pro_st = em("d", (lambda: lambda e: e.dma_start(
                out=th_hbm, in_=throws[:]))(), deps=[pro_last])

            prev_epi = pro_last
            bankA = lambda: pbig[0:64, 0:256]
            bankB = lambda: pbig[64:128, 512:768]
            bank2 = lambda sl: pbig[:, 1024 + 256 * sl:1024 + 256 * sl + 256]
            bank3 = pbig[:, 1536:1792]
            last_wo_ev = None

            def th_load_grp(tcc2, grp, lead_deps, extra=()):
                items = []
                for k2 in range(8):
                    h = 8 * grp + k2
                    stg = lsc_row[0:1, 256 * k2:256 * k2 + 256]
                    itd = em("d", (lambda h=h, stg=stg, tcc2=tcc2: lambda e:
                                   e.dma_start(
                        out=stg,
                        in_=th_hbm[h:h + 1,
                                   256 * tcc2:256 * tcc2 + 256]))(),
                        deps=lead_deps if k2 == 0 else [],
                        extra=extra if k2 == 0 else [])
                    items.append(itd)
                return items

            def emit_tc(tcc, prev_epi, prev):
                tsl = slice(tcc * 256, tcc * 256 + 256)
                nss = 2 * tcc + 2
                th_evs = []
                th_mms = []
                for h in range(H):
                    slot = bank2(h % 2)
                    stg = lsc_row[0:1, 256 * (h % 8):256 * (h % 8) + 256]
                    if h % 8 == 0:
                        deps = [prev["th15"] if h == 0 else th_mms[7],
                                pro_st if (h == 0 and tcc == 0) else None]
                    else:
                        deps = []
                    itd = em("d", (lambda h, stg, tcc=tcc: lambda e:
                                   e.dma_start(
                        out=stg, in_=th_hbm[h:h + 1, tsl]))(h, stg),
                        deps=deps,
                        extra=[("P5ocw", NC)] if (tcc == 0 and h == 0)
                        else [])
                    itm = em("t", (lambda h, slot, stg: lambda e: e.matmul(
                        slot, ones1_sb[:], stg,
                        start=True, stop=True))(h, slot, stg),
                        deps=[itd] + ([th_evs[h - 2]] if h >= 2 else
                                      [prev["its0_5"] if h == 0 else None]))
                    th_mms.append(itm)
                    ite = em("s", (lambda h, slot: lambda e: e.copy(
                        thbc(h), slot))(h, slot), deps=[itm])
                    th_evs.append(ite)
                nxt_thld0 = None
                s0_done = []
                pair_done = [None]

                def emit_pair(cc):
                    h1 = 2 * cc
                    g = cc // 2
                    pp = cc % 2
                    last_sums = None
                    lev = None
                    for ss in range(nss):
                        sl = 2 * pp + (ss % 2)
                        lpsl = lst_ps[:, 256 * pp:256 * pp + 256]
                        itm = em("t", (lambda g=g, ss=ss, lpsl=lpsl:
                                       lambda e: e.matmul(
                            lpsl,
                            ki[32 * g:32 * g + 32, ss * 128:(ss + 1) * 128],
                            qi[32 * g:32 * g + 32, tsl],
                            start=True, stop=True,
                            tile_position=(32 * g, 0)))(),
                            deps=[lst_ps_free[pp]])
                        itv = em("s", (lambda sl=sl, lpsl=lpsl: lambda e:
                                       e.copy(lstbS(sl), lpsl))(),
                            deps=[itm, lstb_free[sl]])
                        lst_ps_free[pp] = itv
                        lev = itv
                        if ss >= 2 * tcc:
                            dmsk = dm0_sb if ss == 2 * tcc else dm1_sb
                            lev = em("v", (lambda dmsk=dmsk, sl=sl: lambda e:
                                e.copy_predicated(out=lstbS(sl),
                                                  mask=dmsk[:],
                                                  data=neg240w_sb[:]))(),
                                deps=[lev])
                        if ss == 0:
                            lev = em("v", (lambda sl=sl: lambda e:
                                e.tensor_copy(lstbS(sl)[0:SINK, :],
                                              sinkrow_sb[:, tsl]))(),
                                deps=[lev])
                        for hh in range(2):
                            h = h1 + hh
                            esl = e_slot(hh, pp)
                            eslot_id = hh + 2 * pp
                            ssl = scores_ps[:, 256 * hh:256 * hh + 256]
                            itm2 = em("t", (lambda h=h, ss=ss, ssl=ssl:
                                            lambda e: e.matmul(
                                ssl,
                                kf_t[h // 2][64 * (h % 2):64 * (h % 2) + 64,
                                             ss * 128:(ss + 1) * 128],
                                qf_t[h // 2][64 * (h % 2):64 * (h % 2) + 64,
                                             tsl],
                                start=True, stop=True,
                                tile_position=(64 * (h % 2), 0)))(),
                                deps=[escore_free[hh]])
                            last_score[0] = itm2
                            ite = em("s", (lambda esl=esl, ssl=ssl:
                                           lambda e: e.activation(
                                out=esl, in_=ssl, func=AF.Exp,
                                scale=0.125))(),
                                deps=[itm2, e_sl_free[eslot_id]])
                            escore_free[hh] = ite
                            if nosel:
                                itv2 = em("v", (lambda hh=hh, esl=esl:
                                                lambda e: e.tensor_copy(
                                    sel_sl(hh), esl))(),
                                    deps=[ite, th_evs[h], lev])
                            else:
                                itv1 = em("v", (lambda hh=hh, h=h, sl=sl:
                                                lambda e: e.tensor_tensor(
                                    sel_sl(hh), lstbS(sl), thbc(h),
                                    ALU.is_ge))(),
                                    deps=[ite, th_evs[h], lev])
                                if hh == 1:
                                    lstb_free[sl] = itv1
                                itv2 = em("v", (lambda hh=hh, esl=esl:
                                                lambda e: e.tensor_tensor(
                                    esl, esl, sel_sl(hh), ALU.mult))(),
                                    deps=[itv1])
                            bk = bankA() if hh == 0 else bankB()
                            av_deps = [itv2]
                            if ss == 0:
                                av_deps.append(pair_done[0] if pair_done[0]
                                               is not None else
                                               (th_evs[-1] if cc == 0
                                                else None))
                            itav = em("t", (lambda hh=hh, ss=ss, nss=nss,
                                            bk=bk, esl=esl, h1=h1:
                                            lambda e: e.matmul(
                                bk, vtok_t[ss][:, 64 * (h1 + hh):
                                               64 * (h1 + hh) + 64],
                                esl, start=(ss == 0),
                                stop=(ss == nss - 1),
                                tile_position=(0, 64 * hh)))(),
                                deps=av_deps)
                            su = pa[0:64, 0:256] if hh == 0 \
                                else pb[64:128, 0:256]
                            itsu = em("t", (lambda hh=hh, ss=ss, nss=nss,
                                            su=su, esl=esl:
                                            lambda e: e.matmul(
                                su, ones64_sb[:], esl,
                                start=(ss == 0),
                                stop=(ss == nss - 1),
                                tile_position=(0, 64 * hh)))(),
                                deps=[])
                            e_sl_free[eslot_id] = itsu
                            last_sums = itsu
                    def mk_rs(e):
                        e.reciprocal(rsum_bc[0:64, :], pa[0:64, 0:256])
                        return e.reciprocal(rsum_bc[64:128, :],
                                            pb[64:128, 0:256])
                    itr = em("v", mk_rs,
                             deps=[last_sums,
                                   s0_done[-1] if s0_done else None])
                    itr2 = em("v", (lambda cc=cc: lambda e: [
                        e.tensor_tensor(s0pre(cc)[0:64, :], pbig[0:64, 0:256],
                                        rsum_bc[0:64, :], ALU.mult),
                        e.tensor_tensor(s0pre(cc)[64:128, :],
                                        pbig[64:128, 512:768],
                                        rsum_bc[64:128, :], ALU.mult)][-1]
                        )(), deps=[itr, th_evs[-1] if cc == 0 else None])
                    pair_done[0] = itr2
                    s0_done.append(itr2)

                for cc in range(ncc5):
                    emit_pair(cc)
                go_prev = None
                sg_done = []
                for cc in range(ncc5):
                    itg = em("t", (lambda cc: lambda e: [e.matmul(
                        bank3, wbig[:, kk * 1024 + 128 * cc:
                                    kk * 1024 + 128 * cc + 128],
                        xn[kk][:, tsl], start=(kk == 0), stop=(kk == NC - 1))
                        for kk in range(NC)][-1])(cc),
                        deps=[s0_done[-1] if cc == 0 else None,
                              prev["its0_6"] if cc == 0 else None,
                              go_prev],
                        extra=[("P5gocw", NC)] if tcc == 0 and cc == 0 else [])
                    itgs = em("s", (lambda: lambda e: e.activation(
                        out=sel_sl(0), in_=bank3,
                        func=AF.Sigmoid))(),
                        deps=[itg, sg_done[-1] if sg_done else None])
                    go_prev = itgs
                    itp = em("v", (lambda cc: lambda e: e.tensor_copy(
                        sel_sl(1), s0pre(cc)))(cc),
                        deps=[itgs]) if nogo else \
                        em("v", (lambda cc: lambda e: e.tensor_tensor(
                            s0pre(cc), s0pre(cc), sel_sl(0), ALU.mult))(cc),
                           deps=[itgs])
                    sg_done.append(itp)
                wo_list = []
                for mc in range(NC if ncc5 == NC else 0):
                    itwo = em("t", (lambda mc: lambda e: [e.matmul(
                        pbig[:, 256 * mc:256 * mc + 256],
                        wbig2[:, kk * 1024 + 128 * mc:
                              kk * 1024 + 128 * mc + 128],
                        s0pre(kk), start=(kk == 0), stop=(kk == NC - 1))
                        for kk in range(NC)][-1])(mc),
                        deps=(sg_done if mc == 0 else []),
                        extra=[("P5ocw", NC)] if tcc == 0 and mc == 0 else [])
                    wo_list.append(itwo)
                st_prev = None
                its0_l = []
                st_l = []
                for mc in range(NC if ncc5 == NC else 0):
                    its0 = em("s", (lambda mc: lambda e: e.copy(
                        s0stage(mc % 4), pbig[:, 256 * mc:256 * mc + 256])
                        )(mc), deps=[wo_list[mc | 1], st_prev])
                    its0_l.append(its0)
                    itd2 = em("d", (lambda mc, tsl: lambda e: e.dma_start(
                        out=s_hbm[128 * mc:128 * mc + 128, tsl],
                        in_=s0stage(mc % 4)))(mc, tsl), deps=[its0],
                        par=(mc % 4 != 0))
                    st_l.append(itd2)
                    if mc >= 3:
                        st_prev = st_l[mc - 3]
                    last_wo_ev = itd2
                if ncc5 != NC:
                    return sg_done[-1], None
                lstb_free[0] = st_l[4]
                lstb_free[1] = st_l[4]
                lstb_free[2] = st_l[5]
                lstb_free[3] = st_l[5]
                e_sl_free[0] = st_l[6]
                e_sl_free[1] = st_l[6]
                nxt = {"th15": th_mms[15], "its0_5": its0_l[5],
                       "its0_6": its0_l[6], "thld0": nxt_thld0}
                return last_wo_ev, nxt

            last_score = [None]
            lst_ps_free = [pro_last, pro_last]
            lstb_free = [None, None, None, None]
            escore_free = [pro_last, pro_last]
            e_sl_free = [None, None, pro_last, pro_last]
            lstbS = lambda sl: bv5[:, 4096 + 256 * sl:4096 + 256 * sl + 256]
            et_sl = lambda hh: theta.ap().bitcast(BF16)[:, 256 * hh:
                                                        256 * hh + 256]
            e_slot = lambda hh, pp: e_sl(hh) if pp == 0 else et_sl(hh)
            prev = {"th15": None, "its0_5": None, "its0_6": None,
                    "thld0": None}
            for tcc in range(ntca):
                prev_epi, nxt = emit_tc(tcc, prev_epi, prev)
                if nxt is not None:
                    prev = nxt
            replay_all5()

        if nphase >= 6:
            P6, sched6, cnt6, em6, replay6 = mk_sched("P6")
            w1v = [bf(PL[1], 0, 8192), bf(PL[2], 0, 8192)]
            w2v = [bf(PL[3], 0, 8192), bf(PL[4], 0, 8192)]
            g0v = [bf(PL[5], 0, 8192), bf(PL[0], 0, 8192)]
            hem = lambda hc: bf(wa, 0, 8192)[:, 512 * hc:512 * hc + 512]
            mbc = bf(bscr, 0, 2048)
            evsc = lambda i: bf(bscr, 1, 2048)[:, 512 * i:512 * i + 512]

            def w1tile(kk, hc):
                v = w1v[0] if kk < 4 else w1v[1]
                return v[:, (kk % 4) * 2048 + 128 * hc:
                         (kk % 4) * 2048 + 128 * hc + 128]

            def w2tile(hc, c):
                v = w2v[0] if hc < 8 else w2v[1]
                return v[:, (hc % 8) * 1024 + 128 * c:
                         (hc % 8) * 1024 + 128 * c + 128]

            def g0sl(c, t0, n):
                v = g0v[0] if c < 4 else g0v[1]
                return v[:, (c % 4) * 2048 + t0:(c % 4) * 2048 + t0 + n]

            p5_end_t = cnt5["t"]
            p5_end_v = cnt5["v"]
            p5_end_d = cnt5["d"] * 16
            p5_score_t = last_score[0]["val"] if nphase >= 5 else 0

            def emit_we(e, prev_cast, first):
                casts = []
                prev_d = None
                for wi in range(2):
                    nch = NC if wi == 0 else 16
                    for ch in range(nch):
                        stg = wb[:, (len(casts) % 2) * 1024:
                                 (len(casts) % 2) * 1024 + 1024]
                        if wi == 0:
                            src = d["ew1"][e, ch * 128:(ch + 1) * 128, :] \
                                .rearrange("p (a m) -> p a m", a=2)
                            dst = (lambda e2, stg=stg: e2.dma_start(
                                out=stg.rearrange("p (a m) -> p a m", a=2),
                                in_=src))
                        else:
                            src = d["ew2"][e, ch * 128:(ch + 1) * 128, :]
                            dst = (lambda e2, stg=stg, src=src: e2.dma_start(
                                out=stg, in_=src))
                        deps = [prev_d, casts[-2] if len(casts) >= 2 else
                                (prev_cast if first or prev_cast else None)]
                        if first and len(casts) == 0:
                            pass
                        itd = em6("d", dst, deps=deps,
                                  extra=([("P5d", p5_end_d),
                                          ("P5t", p5_end_t)]
                                         if first and len(casts) == 0 else []))
                        prev_d = itd
                        if wi == 0:
                            k2 = ch
                            dv = (lambda e2, k2=k2, stg=stg: e2.tensor_copy(
                                w1tile(k2, 0).tensor if False else
                                (w1v[0] if k2 < 4 else w1v[1])[
                                    :, (k2 % 4) * 2048:(k2 % 4) * 2048 + 2048],
                                stg))
                        else:
                            dv = (lambda e2, ch=ch, stg=stg: e2.tensor_copy(
                                (w2v[0] if ch < 8 else w2v[1])[
                                    :, (ch % 8) * 1024:(ch % 8) * 1024 + 1024],
                                stg))
                        itc = em6("v", dv, deps=[itd],
                                  extra=([("P5v", p5_end_v)]
                                         if first and len(casts) == 0 else []))
                        casts.append(itc)
                return casts

            def emit_we2(e, prev_he, prev_last, prev_cast, first):
                casts = []
                seq = [("ew1", ch, half) for ch in range(NC)
                       for half in range(2)] + \
                      [("ew2", ch, 0) for ch in range(16)]
                for i, (nm, ch, half) in enumerate(seq):
                    stg = wb[:, (i % 4) * 1024:(i % 4) * 1024 + 1024]
                    if nm == "ew1":
                        src = d["ew1"][e, ch * 128:(ch + 1) * 128,
                                       half * 1024:half * 1024 + 1024]
                        dst_view = (w1v[0] if ch < 4 else w1v[1])[
                            :, (ch % 4) * 2048 + half * 1024:
                            (ch % 4) * 2048 + half * 1024 + 1024]
                    else:
                        src = d["ew2"][e, ch * 128:(ch + 1) * 128, :]
                        dst_view = (w2v[0] if ch < 8 else w2v[1])[
                            :, (ch % 8) * 1024:(ch % 8) * 1024 + 1024]
                    if i % 4 == 0:
                        dl = ([casts[-1]] if casts else
                              [prev_he, prev_cast])
                        if i == 16:
                            dl.append(prev_last)
                        itd = em6("d", (lambda src=src, stg=stg: lambda e2:
                                        e2.dma_start(out=stg, in_=src))(),
                                  deps=dl,
                                  extra=([("P5d", p5_end_d),
                                          ("P5t", p5_end_t)]
                                         if first and i == 0 else []))
                    else:
                        itd = em6("d", (lambda src=src, stg=stg: lambda e2:
                                        e2.dma_start(out=stg, in_=src))(),
                                  deps=[], par=True)
                    itc = em6("v", (lambda dst_view=dst_view, stg=stg:
                                    lambda e2: e2.tensor_copy(dst_view, stg)
                                    )(), deps=[itd],
                              extra=([("P5v", p5_end_v),
                                      ("P5t", p5_end_t)]
                                     if first and i == 0 else []))
                    casts.append(itc)
                return casts

            last_g0 = None
            last_w_use = None
            last_he_use = None
            last_mask_mm = None
            itoh = em6("d", (lambda: lambda e2: e2.dma_start(
                out=oh_hbm, in_=onehotT[:]))(),
                extra=[("P5d", p5_end_d)])
            casts = []
            for e in range(EMOE):
                itd6 = em6("d", (lambda e=e: lambda e2: e2.dma_start(
                    out=lsc_row[0:1, 0:2048], in_=oh_hbm[e:e + 1, :]))(),
                    deps=[last_mask_mm, itoh if e == 0 else None],
                    extra=([("P5d", p5_end_d), ("P5t", p5_end_t)]
                           if e == 0 else []))
                casts = emit_we2(e, last_he_use, last_w_use,
                                 casts[-1] if e > 0 else None, e == 0)
                mb_evs = []
                for qd in range(4):
                    stg6 = lsc_row[0:1, 512 * qd:512 * qd + 512]
                    itm = em6("t", (lambda stg6=stg6: lambda e2: e2.matmul(
                        pa[:, 0:512], ones1_sb[:], stg6,
                        start=True, stop=True))(),
                        deps=[itd6,
                              mb_evs[-1] if mb_evs else
                              (last_g0 if last_g0 else None)],
                        extra=([("P5t", p5_end_t), ("P5s", cnt5["s"])]
                               if e == 0 and qd == 0 else []))
                    last_mask_mm = itm
                    ite = em6("s", (lambda qd=qd: lambda e2: e2.copy(
                        mbc[:, 512 * qd:512 * qd + 512], pa[:, 0:512]))(),
                        deps=[itm])
                    mb_evs.append(ite)
                for tq in range(4):
                    t0 = 512 * tq
                    he_done = []
                    for hc in range(16):
                        pslot = pa[:, 0:512] if False else \
                            (pb[:, 0:512] if hc % 2 == 0 else
                             ptr.ap().bitcast(F32)[:, 0:512])
                        mmdeps = [casts[15]] \
                            if hc == 0 and tq == 0 else []
                        if len(he_done) >= 2:
                            mmdeps.append(he_done[-2])
                        elif tq > 0 or e > 0:
                            pass
                        itm = em6("t", (lambda hc=hc, t0=t0, pslot=pslot,
                                        e=e: lambda e2: [e2.matmul(
                            pslot, w1tile(kk, hc),
                            xn[kk][:, t0:t0 + 512],
                            start=(kk == 0), stop=(kk == NC - 1))
                            for kk in range(NC)][-1])(),
                            deps=mmdeps + [mb_evs[-1]] if hc == 0 and
                            tq == 0 else mmdeps)
                        last_he_use = itm
                        itg = em6("s", (lambda hc=hc, pslot=pslot: lambda e2:
                                        e2.activation(
                            out=evsc(hc % 2), in_=pslot,
                            func=AF.Gelu))(), deps=[itm])
                        itk = em6("v", (lambda hc=hc, t0=t0: lambda e2:
                                        e2.tensor_tensor(
                            hem(hc), evsc(hc % 2),
                            mbc[:, t0:t0 + 512], ALU.mult))(),
                            deps=[itg])
                        he_done.append(itk)
                    oev = []
                    for c in range(NC):
                        oslot = ptr2.ap().bitcast(F32)[:, 0:512] if c % 2 \
                            else pbig[:, 0:512]
                        itw = em6("t", (lambda c=c, oslot=oslot, e=e:
                                        lambda e2: [e2.matmul(
                            oslot, w2tile(hc2, c), hem(hc2),
                            start=(hc2 == 0), stop=(hc2 == 15))
                            for hc2 in range(16)][-1])(),
                            deps=(he_done if c == 0 else []) +
                            ([last_g0] if last_g0 is not None and c < 2
                             else []) +
                            ([oev[c - 2]] if c >= 2 else []))
                        if e == 0:
                            itv = em6("s", (lambda c=c, t0=t0, oslot=oslot:
                                            lambda e2: e2.copy(
                                g0sl(c, t0, 512), oslot))(), deps=[itw])
                            oev.append(itv)
                        else:
                            ite2 = em6("s", (lambda c=c, oslot=oslot,
                                             t0=t0: lambda e2: e2.copy(
                                evsc(2 + c % 2), oslot))(),
                                deps=[itw] + ([last_g0] if c >= 2 else []))
                            itv = em6("v", (lambda c=c, t0=t0: lambda e2:
                                            e2.tensor_tensor(
                                g0sl(c, t0, 512), g0sl(c, t0, 512),
                                evsc(2 + c % 2), ALU.add))(),
                                deps=[ite2])
                            oev.append(ite2)
                        last_g0 = itv
                        last_w_use = itw

            bb = bf(bscr, 0, 6144)
            ub = lambda c: bb[:, 512 * c:512 * c + 512]
            u2s = lambda i: bb[:, 4096 + 512 * i:4608 + 512 * i]
            mu_bc = bb[:, 5120:5632]
            rs_bc = bb[:, 5632:6144]
            x1b = lambda b2: wa[:, 1024 * b2:1024 * b2 + 1024]
            g_f32 = lambda i: wb[:, 1024 + 512 * i:1536 + 512 * i]
            s0sl = lambda i: wb[:, 2048 + 512 * i:2560 + 512 * i]
            stg6b = theta.ap().bitcast(BF16)[0:1, 0:512]
            epi_prev = [last_g0]

            def emit_epi(tq):
                t0 = 512 * tq
                x0ld = []
                for b2 in range(4):
                    itx = em6("d", (lambda b2=b2, t0=t0: lambda e2:
                                    e2.dma_start(
                        out=x1b(b2),
                        in_=d["x"][t0 + 128 * b2:t0 + 128 * b2 + 128, :]))(),
                        deps=[epi_prev[0]] if b2 == 0 else [],
                        par=(b2 != 0))
                    x0ld.append(itx)
                itmf = em6("t", (lambda t0=t0: lambda e2: e2.matmul(
                    pa[:, 0:512], ones1_sb[:],
                    factor_row[0:1, t0:t0 + 512],
                    start=True, stop=True))(), deps=[epi_prev[0]])
                itef = em6("s", (lambda: lambda e2: e2.copy(
                    bb[:, 4096:4608], pa[:, 0:512]))(), deps=[itmf])
                uch = []
                for c in range(NC):
                    itu = em6("v", (lambda c=c, t0=t0: lambda e2:
                                    e2.tensor_tensor(
                        ub(c), g0sl(c, t0, 512), bb[:, 4096:4608],
                        ALU.mult))(), deps=[itef])
                    uch.append(itu)
                its1 = em6("t", (lambda: lambda e2: [e2.matmul(
                    pa[0:1, 0:512], ones64_sb[:, 0:1], ub(c2),
                    start=(c2 == 0), stop=(c2 == NC - 1))
                    for c2 in range(NC)][-1])(), deps=[itef] + uch)
                s2items = []
                for c in range(NC):
                    itq2 = em6("s", (lambda c=c: lambda e2: e2.activation(
                        out=u2s(c % 2), in_=ub(c), func=AF.Square))(),
                        deps=[uch[c], uch[-1]] +
                        ([s2items[c - 2]] if c >= 2 else []))
                    itm2 = em6("t", (lambda c=c: lambda e2: e2.matmul(
                        pb[0:1, 0:512], ones64_sb[:, 0:1], u2s(c % 2),
                        start=(c == 0), stop=(c == NC - 1)))(),
                        deps=[itq2])
                    s2items.append(itm2)
                itmu = em6("v", (lambda: lambda e2: e2.tensor_scalar(
                    out=wb[0:1, 0:512], in0=pa[0:1, 0:512],
                    scalar1=1.0 / C, scalar2=None, op0=ALU.mult))(),
                    deps=[its1])
                itm2r = em6("v", (lambda: lambda e2: e2.tensor_mul(
                    wb[0:1, 512:1024], wb[0:1, 0:512], wb[0:1, 0:512]))(),
                    deps=[itmu])
                itvar = em6("v", (lambda: lambda e2:
                                  e2.scalar_tensor_tensor(
                    out=wb[0:1, 512:1024], in0=pb[0:1, 0:512],
                    scalar=1.0 / C, in1=wb[0:1, 512:1024],
                    op0=ALU.mult, op1=ALU.subtract))(),
                    deps=[itm2r, s2items[-1]])
                itsd = em6("s", (lambda: lambda e2: e2.activation(
                    out=wb[0:1, 512:1024], in_=wb[0:1, 512:1024],
                    func=AF.Sqrt, bias=cvals[0:1, 0:1]))(), deps=[itvar])
                itrs = em6("v", (lambda: lambda e2: e2.reciprocal(
                    wb[0:1, 512:1024], wb[0:1, 512:1024]))(), deps=[itsd])
                itc1 = em6("v", (lambda: lambda e2: e2.tensor_copy(
                    stg6b, wb[0:1, 0:512]))(), deps=[itrs])
                itb1 = em6("t", (lambda: lambda e2: e2.matmul(
                    pa[:, 0:512], ones1_sb[:], stg6b,
                    start=True, stop=True))(), deps=[itc1, itmu])
                ite1 = em6("s", (lambda: lambda e2: e2.copy(
                    mu_bc, pa[:, 0:512]))(), deps=[itb1])
                itc2 = em6("v", (lambda: lambda e2: e2.tensor_copy(
                    stg6b, wb[0:1, 512:1024]))(), deps=[ite1, itb1])
                itb2 = em6("t", (lambda: lambda e2: e2.matmul(
                    pa[:, 0:512], ones1_sb[:], stg6b,
                    start=True, stop=True))(), deps=[itc2, ite1])
                ite2 = em6("s", (lambda: lambda e2: e2.copy(
                    rs_bc, pa[:, 0:512]))(), deps=[itb2])
                last_tr = None
                addl = []
                for c in range(NC):
                    itl1 = em6("v", (lambda c=c: lambda e2:
                                     e2.tensor_tensor(
                        ub(c), ub(c), mu_bc, ALU.subtract))(),
                        deps=[ite1, s2items[-1]])
                    itl2 = em6("v", (lambda c=c: lambda e2:
                                     e2.tensor_tensor(
                        ub(c), ub(c), rs_bc, ALU.mult))(), deps=[ite2, itl1])
                    itl3 = em6("v", (lambda c=c: lambda e2:
                                     e2.tensor_scalar(
                        out=ub(c), in0=ub(c), scalar1=ln_g_sb[:, c:c + 1],
                        scalar2=ln_b_sb[:, c:c + 1], op0=ALU.mult,
                        op1=ALU.add))(), deps=[itl2])
                    itsi = em6("s", (lambda c=c: lambda e2: e2.activation(
                        out=g_f32(c % 2), in_=ub(c), func=AF.Silu))(),
                        deps=[itl3] + ([addl[c - 2]] if c >= 2 else []))
                    itds = em6("d", (lambda c=c, t0=t0: lambda e2:
                                     e2.dma_start(
                        out=s0sl(c % 2),
                        in_=s_hbm[128 * c:128 * c + 128, t0:t0 + 512]))(),
                        deps=[itsi] if False else
                        [addl[c - 2]] if c >= 2 else [epi_prev[0]],
                        extra=[("P5d", p5_end_d)] if tq == 0 and c < 2
                        else [])
                    itsg = em6("v", (lambda c=c: lambda e2:
                                     e2.tensor_tensor(
                        g_f32(c % 2), g_f32(c % 2), s0sl(c % 2),
                        ALU.mult))(), deps=[itsi, itds])
                    ittr = em6("t", (lambda c=c: lambda e2: [e2.transpose(
                        ptr.ap().bitcast(F32)[:, 128 * b2:128 * b2 + 128],
                        g_f32(c % 2)[:, 128 * b2:128 * b2 + 128],
                        idnf_sb[:]) for b2 in range(4)][-1])(),
                        deps=[itsg] + ([last_tr] if last_tr is not None
                                       else []))
                    itad = em6("v", (lambda c=c: lambda e2: [
                        e2.tensor_tensor(
                            x1b(b2)[:, 128 * c:128 * c + 128],
                            x1b(b2)[:, 128 * c:128 * c + 128],
                            ptr.ap().bitcast(F32)[:, 128 * b2:128 * b2 + 128],
                            ALU.add) for b2 in range(4)][-1])(),
                        deps=[ittr, x0ld[3]])
                    last_tr = itad
                    addl.append(itad)
                sts = []
                for b2 in range(4):
                    itst = em6("d", (lambda b2=b2, t0=t0: lambda e2:
                                     e2.dma_start(
                        out=x1_hbm[t0 + 128 * b2:t0 + 128 * b2 + 128, :],
                        in_=x1b(b2)))(),
                        deps=[addl[-1]] if b2 == 0 else [],
                        par=(b2 != 0))
                    sts.append(itst)
                epi_prev[0] = sts[-1]

            for tq in range(4):
                emit_epi(tq)

            replay6()
        if nphase >= 7:
            P7, sched7, cnt7, em7, replay7 = mk_sched("P7")
            b7 = bf(bscr, 0, 6144)
            xn_tok7 = b7[:, 0:1024]
            q_tok7 = b7[:, 1024:2048]
            qstage7 = b7[:, 2048:3072]
            lsccol = b7[0:1, 3072:3200]
            p6_t = cnt6["t"]; p6_s = cnt6["s"]; p6_v = cnt6["v"]
            p6_d = cnt6["d"] * 16
            prev7 = None
            prevq = None
            prev_tr = None
            prev_dq = None
            for tt in range(NT):
                first = (tt == 0)
                itx = em7("d", (lambda tt=tt: lambda e2: e2.dma_start(
                    out=wa[:, (tt % 2) * 1024:(tt % 2) * 1024 + 1024],
                    in_=x1_hbm[tt * 128:(tt + 1) * 128, :]))(),
                    deps=[prev7],
                    extra=[("P6d", p6_d), ("P6t", p6_t), ("P6v", p6_v),
                           ("P6s", p6_s), ("P6e", cnt6["e"] * 16)]
                    if first else [])
                xt = wa[:, (tt % 2) * 1024:(tt % 2) * 1024 + 1024]
                it1 = em7("v", (lambda xt=xt: lambda e2: e2.tensor_reduce(
                    small[:, 0:1], xt, AX.X, ALU.add))(), deps=[itx],
                    extra=[("P6v", p6_v), ("P6s", p6_s)] if first else [])
                it2 = em7("v", (lambda: lambda e2: e2.tensor_scalar(
                    out=small[:, 1:2], in0=small[:, 0:1], scalar1=1.0 / C,
                    scalar2=None, op0=ALU.mult))(), deps=[it1])
                it3 = em7("s", (lambda xt=xt: lambda e2: e2.activation(
                    out=wb[:, 0:1024], in_=xt, func=AF.Square,
                    accum_out=small[:, 2:3]))(), deps=[itx, prev7],
                    extra=[("P6s", p6_s)] if first else [])
                it4 = em7("s", (lambda: lambda e2: e2.activation(
                    out=small[:, 3:4], in_=small[:, 1:2],
                    func=AF.Square))(), deps=[it2])
                it5 = em7("v", (lambda: lambda e2: e2.scalar_tensor_tensor(
                    out=small[:, 4:5], in0=small[:, 2:3], scalar=1.0 / C,
                    in1=small[:, 3:4], op0=ALU.mult, op1=ALU.subtract))(),
                    deps=[it3, it4])
                it6 = em7("s", (lambda: lambda e2: e2.activation(
                    out=small[:, 6:7], in_=small[:, 4:5], func=AF.Sqrt,
                    bias=cvals[:, 0:1]))(), deps=[it5])
                it7 = em7("v", (lambda: lambda e2: e2.reciprocal(
                    small[:, 5:6], small[:, 6:7]))(), deps=[it6])
                it8 = em7("v", (lambda xt=xt: lambda e2: e2.tensor_scalar(
                    out=xn_tok7, in0=xt, scalar1=small[:, 1:2],
                    scalar2=small[:, 5:6], op0=ALU.subtract,
                    op1=ALU.mult))(), deps=[it7, prev_tr])
                prev7 = it8
                it9 = em7("v", (lambda tt=tt: lambda e2: e2.tensor_reduce(
                    amax_col[:, tt:tt + 1], xn_tok7, AX.X, ALU.max,
                    apply_absolute_value=True))(), deps=[it8])
                it10 = em7("v", (lambda: lambda e2: e2.tensor_scalar(
                    out=small[:, 8:9], in0=small[:, 7:8] if False else
                    amax_col[:, 0:1], scalar1=1e-12, scalar2=None,
                    op0=ALU.max))(), deps=[it9]) if False else None
                it10 = em7("v", (lambda tt=tt: lambda e2: e2.tensor_scalar(
                    out=small[:, 8:9], in0=amax_col[:, tt:tt + 1],
                    scalar1=1e-12, scalar2=None, op0=ALU.max))(),
                    deps=[it9])
                it11 = em7("v", (lambda: lambda e2: e2.reciprocal(
                    small[:, 9:10], small[:, 8:9]))(), deps=[it10])
                it12 = em7("v", (lambda: lambda e2: e2.tensor_scalar(
                    out=small[:, 10:11], in0=small[:, 9:10], scalar1=127.0,
                    scalar2=None, op0=ALU.mult))(), deps=[it11])
                it13 = em7("v", (lambda: lambda e2: e2.tensor_scalar(
                    out=small[:, 11:12], in0=small[:, 8:9],
                    scalar1=1.0 / 127.0, scalar2=None, op0=ALU.mult))(),
                    deps=[it10])
                it14 = em7("v", (lambda: lambda e2: e2.tensor_copy(
                    lsccol.tensor if False else b7[:, 3072:3073][0:128, :],
                    small[:, 11:12]))(), deps=[it13, prevq])
                it15 = em7("d", (lambda tt=tt: lambda e2: e2.dma_start(
                    out=lsc_row[0:1, tt * 128:(tt + 1) * 128],
                    in_=b7[:, 3072:3073].rearrange("p m -> (p m)")
                    if False else b7[:, 3072:3073]))(), deps=[it14])
                it16 = em7("v", (lambda: lambda e2: e2.tensor_scalar(
                    out=wb[:, 1024:2048], in0=xn_tok7,
                    scalar1=small[:, 10:11], scalar2=None,
                    op0=ALU.mult))(), deps=[it12, it8])
                it17 = em7("v", (lambda: lambda e2: e2.tensor_copy(
                    wb[:, 2048:3072].ap().bitcast(I32)
                    if False else wb[:, 2048:3072].bitcast(I32),
                    wb[:, 1024:2048]))(), deps=[it16])
                it18 = em7("v", (lambda: lambda e2: e2.tensor_copy(
                    q_tok7, wb[:, 2048:3072].bitcast(I32)))(),
                    deps=[it17])
                prevq = it18
                ittx = em7("t", (lambda tt=tt: lambda e2: [e2.transpose(
                    ptr[:, cb * 128:(cb + 1) * 128],
                    xn_tok7[:, cb * 128:(cb + 1) * 128], idn_sb[:])
                    for cb in range(NC)][-1])(),
                    deps=[it8, prev_tr],
                    extra=[("P6t", p6_t)] if first else [])
                ittq = em7("t", (lambda tt=tt: lambda e2: [e2.transpose(
                    ptr2[:, cb * 128:(cb + 1) * 128],
                    q_tok7[:, cb * 128:(cb + 1) * 128], idn_sb[:])
                    for cb in range(NC)][-1])(), deps=[it18])
                itex = em7("s", (lambda tt=tt: lambda e2: [e2.copy(
                    xn[cb][:, tt * 128:(tt + 1) * 128],
                    ptr[:, cb * 128:(cb + 1) * 128])
                    for cb in range(NC)][-1])(), deps=[ittx])
                iteq = em7("s", (lambda: lambda e2: [e2.copy(
                    qstage7[:, cb * 128:(cb + 1) * 128],
                    ptr2[:, cb * 128:(cb + 1) * 128])
                    for cb in range(NC)][-1])(), deps=[ittq, prev_dq])
                itdq = None
                for cb in range(NC):
                    itdq = em7("d", (lambda tt=tt, cb=cb: lambda e2:
                                     e2.dma_start(
                        out=qxn_hbm[cb * 128:(cb + 1) * 128,
                                    tt * 128:(tt + 1) * 128],
                        in_=qstage7[:, cb * 128:(cb + 1) * 128]))(),
                        deps=[iteq] if cb == 0 else [], par=(cb != 0))
                prev_dq = itdq
                prev_tr = iteq
            replay7()
        if nphase >= 8:
            P8, sched8, cnt8, em8, replay8 = mk_sched("P8")
            b8 = bf(bscr, 0, 6144)
            binT = b8[:, 0:1024]
            winT = b8[:, 1024:2048]
            vstg = b8[:, 2048:2560]
            q1_t = [bf(PL[1], i) for i in range(4)] + \
                [bf(PL[2], i - 4) for i in range(4, NC)]
            k1_t = [bf(PL[3], i) for i in range(4)] + \
                [bf(PL[4], i - 4) for i in range(4, NC)]
            v1tok_t = ([bf(PL[5], 0, 8192)[:, i * 1024:(i + 1) * 1024]
                        for i in range(8)]
                       + [bf(PL[0], 0, 8192)[:, i * 1024:(i + 1) * 1024]
                          for i in range(8)])
            p7_t = cnt7["t"]; p7_s = cnt7["s"]; p7_v = cnt7["v"]
            p7_d = cnt7["d"] * 16

            prev_ev = None
            prev_tr8 = None
            prev_cast8 = None
            cast8_l = []
            for fb in range(24):
                first = (fb == 0)
                itl = em8("d", (lambda fb=fb: lambda e2: e2.dma_start(
                    out=wb[:, (fb % 4) * 1024:(fb % 4) * 1024 + 1024],
                    in_=d["inw"][fb * 128:(fb + 1) * 128, :]))(),
                    deps=[cast8_l[-1] if cast8_l else None]
                    if fb % 4 == 0 else [],
                    par=(fb % 4 != 0),
                    extra=[("P7d", p7_d), ("P7s", p7_s), ("P7v", p7_v)]
                    if first else [])
                itc = em8("v", (lambda fb=fb: lambda e2: e2.tensor_copy(
                    binT, wb[:, (fb % 4) * 1024:(fb % 4) * 1024 + 1024]))(),
                    deps=[itl, prev_tr8],
                    extra=[("P7v", p7_v), ("P7t", p7_t), ("P7s", p7_s)]
                    if first else [])
                cast8_l.append(itc)
                prev_cast8 = itc
                ittr = em8("t", (lambda: lambda e2: [e2.transpose(
                    ptr2[:, cb * 128:(cb + 1) * 128],
                    binT[:, cb * 128:(cb + 1) * 128], idn_sb[:])
                    for cb in range(NC)][-1])(),
                    deps=[itc],
                    extra=[("P7t", p7_t)] if first else [])
                itev = em8("s", (lambda: lambda e2: [e2.copy(
                    winT[:, cb * 128:(cb + 1) * 128],
                    ptr2[:, cb * 128:(cb + 1) * 128])
                    for cb in range(NC)][-1])(), deps=[ittr])
                prev_tr8 = itev
                evs_fb = []
                for tq in range(4):
                    t0 = 512 * tq
                    pslot = pa[:, 0:512] if tq % 2 == 0 else pb[:, 0:512]
                    itm = em8("t", (lambda t0=t0, pslot=pslot: lambda e2:
                                    [e2.matmul(
                        pslot, winT[:, ec * 128:(ec + 1) * 128],
                        xn[ec][:, t0:t0 + 512],
                        start=(ec == 0), stop=(ec == NC - 1))
                        for ec in range(NC)][-1])(),
                        deps=[itev] + ([prev_ev] if tq == 0 else []) +
                        ([evs_fb[tq - 2]] if tq >= 2 else []))
                    if fb < 16:
                        dstl = q1_t if fb < 8 else k1_t
                        dst = dstl[fb % 8]
                        itv = em8("s", (lambda dst=dst, t0=t0, fb=fb,
                                        pslot=pslot: lambda e2: e2.add(
                            dst[:, t0:t0 + 512], pslot,
                            inb_sb[:, fb:fb + 1]))(),
                            deps=[itm])
                        prev_ev = itv
                        evs_fb.append(itv)
                    else:
                        itv = em8("s", (lambda fb=fb, pslot=pslot:
                                        lambda e2: e2.add(
                            vstg, pslot, inb_sb[:, fb:fb + 1]))(),
                            deps=[itm])
                        ittv = em8("t", (lambda: lambda e2: [e2.transpose(
                            ptr[:, bb2 * 128:(bb2 + 1) * 128],
                            vstg[:, bb2 * 128:(bb2 + 1) * 128], idn_sb[:])
                            for bb2 in range(4)][-1])(), deps=[itv])
                        itve = em8("s", (lambda fb=fb, tq=tq: lambda e2:
                                         [e2.copy(
                            v1tok_t[tq * 4 + bb2][:, (fb - 16) * 128:
                                                  (fb - 16) * 128 + 128],
                            ptr[:, bb2 * 128:(bb2 + 1) * 128])
                            for bb2 in range(4)][-1])(), deps=[ittv])
                        prev_ev = itve
                        evs_fb.append(itv)

            woT = bf(wa, 0, 8192)
            prev_c2 = None
            for fb in range(NC):
                itl = em8("d", (lambda fb=fb: lambda e2: e2.dma_start(
                    out=wb[:, (fb % 4) * 1024:(fb % 4) * 1024 + 1024],
                    in_=d["outw"][fb * 128:(fb + 1) * 128, :]))(),
                    deps=[cast8_l[-1] if fb == 0 else
                          (prev_c2 if fb == 4 else None)]
                    if fb % 4 == 0 else [],
                    par=(fb % 4 != 0),
                    extra=[("P7d", p7_d)] if fb == 0 else [])
                itc = em8("v", (lambda fb=fb: lambda e2: e2.tensor_copy(
                    binT, wb[:, (fb % 4) * 1024:(fb % 4) * 1024 + 1024]))(),
                    deps=[itl, prev_cast8 if fb == 0 else None, prev_tr8])
                prev_cast8 = itc
                ittr = em8("t", (lambda: lambda e2: [e2.transpose(
                    ptr2[:, cb * 128:(cb + 1) * 128],
                    binT[:, cb * 128:(cb + 1) * 128], idn_sb[:])
                    for cb in range(NC)][-1])(),
                    deps=[itc])
                itev = em8("s", (lambda fb=fb: lambda e2: [e2.copy(
                    woT[:, cb * 1024 + fb * 128:cb * 1024 + fb * 128 + 128],
                    ptr2[:, cb * 128:(cb + 1) * 128])
                    for cb in range(NC)][-1])(), deps=[ittr])
                prev_c2 = itev
                prev_tr8 = itev
            wo_ready = prev_c2

            e_sl8 = lambda i: b8[:, 2560 + 256 * i:2560 + 256 * i + 256]
            s1pre = lambda k2: b8[:, 256 * k2:256 * k2 + 256]
            prev_epi8 = [prev_ev]
            lastwo8 = [None]
            escore8_free = [None, None]
            e8_free = [None, None, None, None]
            pbig01_free = [None]
            pbig23_free = [None]
            st_carry8 = [None]

            def emit_tc8(tcc):
                tsl = slice(tcc * 256, tcc * 256 + 256)
                pair_done = [None]
                s0_done = []

                def emit_pair(cc):
                    h1 = 2 * cc
                    pp = cc % 2
                    last_sums = None
                    for ss in range(NS):
                        for hh in range(2):
                            h = h1 + hh
                            eslot = hh + 2 * pp
                            esl = e_sl8(eslot)
                            ssl = scores_ps[:, 256 * hh:256 * hh + 256]
                            deps = [escore8_free[hh]]
                            if ss == 0 and cc == 0 and hh == 0 and tcc == 0:
                                deps.append(prev_epi8[0])
                                deps.append(wo_ready)
                            itm = em8("t", (lambda h=h, ss=ss, ssl=ssl,
                                            tsl=tsl: lambda e2:
                                           e2.matmul(
                                ssl,
                                k1_t[h // 2][64 * (h % 2):64 * (h % 2) + 64,
                                             ss * 128:(ss + 1) * 128],
                                q1_t[h // 2][64 * (h % 2):64 * (h % 2) + 64,
                                             tsl],
                                start=True, stop=True,
                                tile_position=(64 * (h % 2), 0)))(),
                                deps=deps)
                            ite = em8("s", (lambda esl=esl, ssl=ssl:
                                            lambda e2: e2.activation(
                                out=esl, in_=ssl, func=AF.Exp,
                                scale=0.125))(),
                                deps=[itm, e8_free[eslot]])
                            escore8_free[hh] = ite
                            itlast = ite
                            delta = 128 * ss - 256 * tcc
                            if delta in (-128, 0, 128):
                                bnm = {-128: "band_m", 0: "band_0",
                                       128: "band_p"}[delta]
                                itlast = em8("v", (lambda esl=esl, bnm=bnm:
                                                   lambda e2:
                                                   e2.tensor_tensor(
                                    esl, esl, band_sb[bnm][:],
                                    ALU.mult))(), deps=[ite])
                            bk = pbig[0:64, 0:256] if hh == 0 else \
                                pbig[64:128, 512:768]
                            av_deps = [itlast]
                            if ss == 0:
                                av_deps.append(pair_done[0] if pair_done[0]
                                               is not None else
                                               pbig01_free[0])
                            itav = em8("t", (lambda hh=hh, ss=ss, bk=bk,
                                             h=h, esl=esl: lambda e2:
                                            e2.matmul(
                                bk, v1tok_t[ss][:, 64 * h:64 * h + 64],
                                esl, start=(ss == 0),
                                stop=(ss == NS - 1),
                                tile_position=(0, 64 * hh)))(),
                                deps=av_deps)
                            su = pa[0:64, 0:256] if hh == 0 \
                                else pb[64:128, 0:256]
                            itsu = em8("t", (lambda hh=hh, ss=ss, su=su,
                                             esl=esl: lambda e2:
                                            e2.matmul(
                                su, ones64_sb[:], esl,
                                start=(ss == 0), stop=(ss == NS - 1),
                                tile_position=(0, 64 * hh)))(),
                                deps=[])
                            e8_free[eslot] = itsu
                            last_sums = itsu

                    def mk_rs8(e2):
                        e2.reciprocal(rsum_bc[0:64, :], pa[0:64, 0:256])
                        return e2.reciprocal(rsum_bc[64:128, :],
                                             pb[64:128, 0:256])
                    itr = em8("v", mk_rs8,
                              deps=[last_sums,
                                    s0_done[-1] if s0_done else None])
                    itr2 = em8("v", (lambda cc=cc: lambda e2: [
                        e2.tensor_tensor(s1pre(cc)[0:64, :],
                                         pbig[0:64, 0:256],
                                         rsum_bc[0:64, :], ALU.mult),
                        e2.tensor_tensor(s1pre(cc)[64:128, :],
                                         pbig[64:128, 512:768],
                                         rsum_bc[64:128, :], ALU.mult)][-1]
                        )(), deps=[itr])
                    pair_done[0] = itr2
                    s0_done.append(itr2)

                for cc in range(NC):
                    emit_pair(cc)
                wo_list = []
                for mc in range(NC):
                    itwo = em8("t", (lambda mc=mc, tsl=tsl: lambda e2:
                                     [e2.matmul(
                        pbig[:, 256 * mc:256 * mc + 256],
                        woT[:, kk * 1024 + 128 * mc:
                            kk * 1024 + 128 * mc + 128],
                        s1pre(kk), start=(kk == 0), stop=(kk == NC - 1))
                        for kk in range(NC)][-1])(),
                        deps=(s0_done + [pbig23_free[0]] if mc == 0
                              else []))
                    wo_list.append(itwo)
                st_prev = st_carry8[0]
                its0_l = []
                st_l = []
                for mc in range(NC):
                    its0 = em8("s", (lambda mc=mc: lambda e2: e2.add(
                        s0stage(mc % 4),
                        pbig[:, 256 * mc:256 * mc + 256],
                        outb_sb[:, mc:mc + 1]))(),
                        deps=[wo_list[mc | 1], st_prev])
                    its0_l.append(its0)
                    itd2 = em8("d", (lambda mc=mc, tsl=tsl: lambda e2:
                                     e2.dma_start(
                        out=s_hbm[128 * mc:128 * mc + 128, tsl],
                        in_=s0stage(mc % 4)))(), deps=[its0],
                        par=(mc % 4 != 0))
                    st_l.append(itd2)
                    if mc >= 3:
                        st_prev = st_l[mc - 3]
                    lastwo8[0] = itd2
                pbig01_free[0] = its0_l[3]
                pbig23_free[0] = its0_l[7]
                st_carry8[0] = st_l[4]
                prev_epi8[0] = lastwo8[0]

            for tcc in range(NTCA):
                emit_tc8(tcc)
            replay8()
        if nphase >= 9:
            P9, sched9, cnt9, em9, replay9 = mk_sched("P9")
            b9 = bf(bscr, 0, 6144)
            qx9 = lambda par, dc: b9[:, 2048 * par + 256 * dc:
                                     2048 * par + 256 * dc + 256]
            qrhs = lambda dc, nc2: bf(PL[1], 0, 8192)[
                :, nc2 * 1024 + 128 * dc:nc2 * 1024 + 128 * dc + 128]
            rhsT9 = wa[:, 0:1024]
            qf32_9 = wa[:, 1024:2048]
            qi32_9 = wa[:, 2048:3072]
            qbf_9 = bf(wa, 0, 8192)[:, 6144:7168]
            scale_n9 = small2[:, 32:40]
            lscbc9 = lambda i: wb[:, 512 * i:512 * i + 256]
            g1_9 = lambda i: wb[:, 1024 + 512 * i:1024 + 512 * i + 256]
            s1sl9 = lambda i: wb[:, 2048 + 512 * i:2048 + 512 * i + 256]
            ybuf = lambda b2: wb[:, 3072 + 0:3072 + 1024] if False else None
            p8_t = cnt8["t"]; p8_s = cnt8["s"]; p8_v = cnt8["v"]
            p8_d = cnt8["d"] * 16

            prev9 = None
            qr_done = []
            for rc in range(NC):
                trs = []
                for dc in range(NC):
                    itl = em9("d", (lambda rc=rc, dc=dc: lambda e2:
                                    e2.dma_start(
                        out=wb[:, 128 * dc:128 * dc + 128],
                        in_=d["rhs"][dc * 128:(dc + 1) * 128,
                                     rc * 128:(rc + 1) * 128]))(),
                        deps=[prev9] if dc == 0 else [], par=False,
                        extra=[("P8d", p8_d), ("P8t", p8_t), ("P8s", p8_s),
                               ("P8v", p8_v)]
                        if rc == 0 and dc == 0 else [])
                    ittr = em9("t", (lambda dc=dc: lambda e2: e2.transpose(
                        ptr.ap().bitcast(F32)[:, 128 * (dc % 4):
                                              128 * (dc % 4) + 128],
                        wb[:, 128 * dc:128 * dc + 128],
                        idnf_sb[:]))(),
                        deps=[itl, trs[dc - 4] if dc >= 4 else None],
                        extra=[("P8t", p8_t), ("P8s", cnt8["s"]),
                               ("P8v", p8_v)]
                        if rc == 0 and dc == 0 else [])
                    itev = em9("s", (lambda dc=dc: lambda e2: e2.copy(
                        rhsT9[:, 128 * dc:128 * dc + 128],
                        ptr.ap().bitcast(F32)[:, 128 * (dc % 4):
                                              128 * (dc % 4) + 128]))(),
                        deps=[ittr])
                    trs.append(itev)
                prev9 = trs[-1]
                ita = em9("v", (lambda: lambda e2: e2.tensor_reduce(
                    small[:, 0:1], rhsT9, AX.X, ALU.max,
                    apply_absolute_value=True))(), deps=trs)
                itb = em9("v", (lambda: lambda e2: e2.tensor_scalar(
                    out=small[:, 1:2], in0=small[:, 0:1], scalar1=1e-12,
                    scalar2=None, op0=ALU.max))(), deps=[ita])
                itc9 = em9("v", (lambda rc=rc: lambda e2: e2.tensor_scalar(
                    out=scale_n9[:, rc:rc + 1], in0=small[:, 1:2],
                    scalar1=1.0 / 127.0, scalar2=None, op0=ALU.mult))(),
                    deps=[itb])
                itd9 = em9("v", (lambda: lambda e2: e2.reciprocal(
                    small[:, 2:3], small[:, 1:2]))(), deps=[itc9])
                ite9 = em9("v", (lambda: lambda e2: e2.tensor_scalar(
                    out=small[:, 3:4], in0=small[:, 2:3], scalar1=127.0,
                    scalar2=None, op0=ALU.mult))(), deps=[itd9])
                itf9 = em9("v", (lambda: lambda e2: e2.tensor_scalar(
                    out=qf32_9, in0=rhsT9, scalar1=small[:, 3:4],
                    scalar2=None, op0=ALU.mult))(), deps=[ite9])
                itg9 = em9("v", (lambda: lambda e2: e2.tensor_copy(
                    qi32_9.bitcast(I32), qf32_9))(), deps=[itf9])
                ith9 = em9("v", (lambda: lambda e2: e2.tensor_copy(
                    qbf_9, qi32_9.bitcast(I32)))(), deps=[itg9])
                itj = em9("t", (lambda: lambda e2: [e2.transpose(
                    ptr2[:, 128 * b3:128 * b3 + 128],
                    qbf_9[:, 128 * b3:128 * b3 + 128], idn_sb[:])
                    for b3 in range(NC)][-1])(), deps=[ith9])
                itk = em9("s", (lambda rc=rc: lambda e2: [e2.copy(
                    qrhs(dc2, rc), ptr2[:, 128 * dc2:128 * dc2 + 128])
                    for dc2 in range(NC)][-1])(), deps=[itj])
                qr_done.append(itk)
                prev9 = itk

            stg9 = theta.ap().bitcast(BF16)[0:1, 0:256]
            yprev = [qr_done[-1]]
            for tcc in range(NTCA):
                t0 = 256 * tcc
                par = tcc % 2
                qld = []
                for dc in range(NC):
                    itq = em9("d", (lambda dc=dc, t0=t0, par=par: lambda e2:
                                    e2.dma_start(
                        out=qx9(par, dc),
                        in_=qxn_hbm[dc * 128:(dc + 1) * 128,
                                    t0:t0 + 256]))(),
                        deps=[yprev[0] if tcc >= 2 else qr_done[-1]]
                        if dc == 0 else [],
                        par=(dc != 0))
                    qld.append(itq)
                accs = []
                for nc2 in range(NC):
                    itm = em9("t", (lambda nc2=nc2, par=par: lambda e2:
                                    [e2.matmul(
                        pbig[:, 256 * nc2:256 * nc2 + 256],
                        qrhs(dc2, nc2), qx9(par, dc2),
                        start=(dc2 == 0), stop=(dc2 == NC - 1))
                        for dc2 in range(NC)][-1])(),
                        deps=qld + ([yprev[0]] if nc2 < 2 and tcc >= 1
                                    else []) +
                        ([qr_done[-1]] if tcc == 0 and nc2 == 0 else []))
                    accs.append(itm)
                itml = em9("t", (lambda t0=t0: lambda e2: e2.matmul(
                    pa[:, 0:256], ones1_sb[:],
                    lsc_row[0:1, t0:t0 + 256],
                    start=True, stop=True))(), deps=[accs[-1]])
                itel = em9("s", (lambda: lambda e2: e2.copy(
                    lscbc9(0), pa[:, 0:256]))(), deps=[itml])
                ytile = [wa[:, 1024 + 1024 * b2:1024 + 1024 * b2 + 1024]
                         for b2 in range(2)]
                x1ld = []
                for b2 in range(2):
                    itx = em9("d", (lambda b2=b2, t0=t0: lambda e2:
                                    e2.dma_start(
                        out=ytile[b2],
                        in_=x1_hbm[t0 + 128 * b2:t0 + 128 * b2 + 128, :]))(),
                        deps=[yprev[0] if tcc >= 1 else qr_done[-1]]
                        if b2 == 0 else [], par=(b2 != 0))
                    x1ld.append(itx)
                addl = []
                last_tr9 = None
                for c in range(NC):
                    itd1 = em9("v", (lambda c=c: lambda e2: e2.tensor_tensor(
                        g1_9(c % 2), pbig[:, 256 * c:256 * c + 256],
                        lscbc9(0), ALU.mult))(),
                        deps=[itel, accs[c | 1]] +
                        ([addl[c - 2]] if c >= 2 else []))
                    itsi = em9("s", (lambda c=c: lambda e2: e2.activation(
                        out=g1_9(c % 2), in_=g1_9(c % 2), func=AF.Silu,
                        scale=scale_n9[:, c:c + 1]))(), deps=[itd1])
                    itds = em9("d", (lambda c=c, t0=t0: lambda e2:
                                     e2.dma_start(
                        out=s1sl9(c % 2),
                        in_=s_hbm[128 * c:128 * c + 128, t0:t0 + 256]))(),
                        deps=[addl[c - 2]] if c >= 2 else [x1ld[-1]],
                        extra=[("P8d", p8_d)] if tcc == 0 and c < 2 else [])
                    itsg = em9("v", (lambda c=c: lambda e2: e2.tensor_tensor(
                        g1_9(c % 2), g1_9(c % 2), s1sl9(c % 2),
                        ALU.mult))(), deps=[itsi, itds])
                    ittr = em9("t", (lambda c=c: lambda e2: [e2.transpose(
                        ptr.ap().bitcast(F32)[:, 128 * b2:128 * b2 + 128],
                        g1_9(c % 2)[:, 128 * b2:128 * b2 + 128],
                        idnf_sb[:]) for b2 in range(2)][-1])(),
                        deps=[itsg] + ([last_tr9] if last_tr9 else []))
                    itad = em9("v", (lambda c=c: lambda e2: [
                        e2.tensor_tensor(
                            ytile[b2][:, 128 * c:128 * c + 128],
                            ytile[b2][:, 128 * c:128 * c + 128],
                            ptr.ap().bitcast(F32)[:, 128 * b2:128 * b2 + 128],
                            ALU.add) for b2 in range(2)][-1])(),
                        deps=[ittr] + x1ld)
                    last_tr9 = itad
                    addl.append(itad)
                sts = []
                for b2 in range(2):
                    itst = em9("d", (lambda b2=b2, t0=t0: lambda e2:
                                     e2.dma_start(
                        out=y[t0 + 128 * b2:t0 + 128 * b2 + 128, :],
                        in_=ytile[b2]))(), deps=[addl[-1]] if b2 == 0
                        else [], par=(b2 != 0))
                    sts.append(itst)
                yprev[0] = sts[-1]
            replay9()
        tap_specs = {
            "xn0": ([C, T], BF16, lambda: xn),
            "qi": ([128, T], BF16, lambda: qi),
            "ki": ([128, T], BF16, lambda: ki),
            "wtok": ([128, NT * G], F32, lambda: wtok),
            "onehotT": ([EMOE, T], BF16, lambda: onehotT),
            "factor_row": ([1, T], BF16, lambda: factor_row),
            "theta": ([128, NT * H], F32, lambda: theta),
            "sortv": ([128, NSORT * G], F32, lambda: sortv),
            "qf": ([C, T], BF16, lambda: qf_t),
            "kf": ([C, T], BF16, lambda: kf_t),
            "vtok": ([T, C], BF16, lambda: vtok_t),
            "s_hbm": ([C, T], F32, lambda: s_hbm[:, 0:256 * ntca]
                      if ntca < NTCA else s_hbm),
            "throws": ([16, T], BF16, lambda: throws),
            "bv5": ([128, 6144], BF16, lambda: bf(bscr, 0, 6144)),
            "g0acc": ([C, T], BF16, lambda: [g0sl(c, 0, 2048)
                                             for c in range(NC)]),
            "x1": ([T, C], F32, lambda: x1_hbm),
            "xn1": ([C, T], BF16, lambda: xn),
            "qxn": ([C, T], BF16, lambda: qxn_hbm),
            "lscr": ([1, T], BF16, lambda: lsc_row),
        }
        FIN = sem("FIN")
        chosen = [(nm, tap_specs[nm]) for nm in dbg if nm in tap_specs]
        for nm, spec in chosen:
            taps[nm] = nc.dram_tensor("dbg_" + nm, list(spec[0]), spec[1],
                                      kind="ExternalOutput").ap()

        @block.gpsimd
        def _(gpsimd):
            gpsimd.wait_ge(ln0_done, NT)
            if nphase >= 2:
                gpsimd.wait_ge(sems["P2s"], 3 + 4 * (NT if p2sub >= 3 else 0))
            if nphase >= 3:
                gpsimd.wait_ge(sems["P3v"], 139 * NT)
            if nphase >= 4:
                gpsimd.wait_ge(sems["P4vv"], NT)
                gpsimd.wait_ge(sems["P4qs"], 2 * NT)
                gpsimd.wait_ge(sems["P4ks"], 2 * NT)
            if nphase >= 5:
                for e5 in "tsv":
                    gpsimd.wait_ge(sems["P5" + e5], cnt5[e5])
                gpsimd.wait_ge(sems["P5d"], cnt5["d"] * 16)
                gpsimd.wait_ge(sems["P5e"], cnt5["e"] * 16)
            if nphase >= 6:
                for e6 in "tsv":
                    gpsimd.wait_ge(sems["P6" + e6], cnt6[e6])
                gpsimd.wait_ge(sems["P6d"], cnt6["d"] * 16)
                gpsimd.wait_ge(sems["P6e"], cnt6["e"] * 16)
            if nphase >= 7:
                for e7 in "tsv":
                    gpsimd.wait_ge(sems["P7" + e7], cnt7[e7])
                gpsimd.wait_ge(sems["P7d"], cnt7["d"] * 16)
                gpsimd.wait_ge(sems["P7e"], cnt7["e"] * 16)
            if nphase >= 8:
                for e8 in "tsv":
                    gpsimd.wait_ge(sems["P8" + e8], cnt8[e8])
                gpsimd.wait_ge(sems["P8d"], cnt8["d"] * 16)
                gpsimd.wait_ge(sems["P8e"], cnt8["e"] * 16)
            if nphase >= 9:
                for e9 in "tsv":
                    gpsimd.wait_ge(sems["P9" + e9], cnt9[e9])
                gpsimd.wait_ge(sems["P9d"], cnt9["d"] * 16)
                gpsimd.wait_ge(sems["P9e"], cnt9["e"] * 16)
            nst = 0
            for nm, (shape, dt_, get) in chosen:
                src = get()
                if isinstance(src, list):
                    for c, tile in enumerate(src):
                        gpsimd.dma_start(
                            out=taps[nm][c * 128:(c + 1) * 128, :],
                            in_=tile[:]).then_inc(FIN, 16)
                        nst += 1
                else:
                    try:
                        src_ap = src[:]
                    except TypeError:
                        src_ap = src
                    dst_ap = taps[nm]
                    if list(src_ap.shape) != list(dst_ap.shape):
                        dst_ap = dst_ap[:, 0:src_ap.shape[1]]
                    gpsimd.dma_start(out=dst_ap, in_=src_ap).then_inc(FIN, 16)
                    nst += 1
            if nphase < 9:
                gpsimd.dma_start(out=y[0:128, :],
                                 in_=wa[:, 0:1024]).then_inc(FIN, 16)
                nst += 1
            gpsimd.wait_ge(FIN, 16 * nst)

    return nc, d


def make_in_maps(inputs):
    cst = host_consts()
    base = {
        "w_iq": inputs["L0_attn_W_Iq"], "w_ik": inputs["L0_attn_W_Ik"],
        "w_iw": inputs["L0_attn_W_Iw"],
        "gate_bias": np.asarray(inputs["L0_attn_gate_bias"],
                                np.float32).reshape(1, G),
        "head_bias": np.asarray(inputs["L0_attn_head_bias"],
                                np.float32).reshape(1, H),
        "var_ema": np.asarray(inputs["L0_attn_variance_ema"],
                              np.float32).reshape(1, 1),
        "w_q": inputs["L0_attn_W_q"], "w_k": inputs["L0_attn_W_k"],
        "w_v": inputs["L0_attn_W_v"], "w_gv": inputs["L0_attn_W_gv"],
        "w_go": inputs["L0_attn_W_go"], "w_o": inputs["L0_attn_W_o"],
        "gate_w": inputs["L0_ffn_gate_w"],
        "ew1": inputs["L0_ffn_expert_w1"], "ew2": inputs["L0_ffn_expert_w2"],
        "ln_g": inputs["L0_ffn_ln_gamma"], "ln_b": inputs["L0_ffn_ln_beta"],
        "inw": inputs["L1_attn_in_proj_weight"],
        "inb": inputs["L1_attn_in_proj_bias"],
        "outw": inputs["L1_attn_out_proj_weight"],
        "outb": inputs["L1_attn_out_proj_bias"],
        "rhs": inputs["L1_ffn_rhs"],
        "cc": cst["cc"], "ss": cst["ss"], "tril": cst["tril"],
        "neg240": cst["neg240"], "sinkrow": cst["sinkrow"], "triu": cst["triu"],
        "iota64": cst["iota64"], "iota_t": cst["iota_t"],
        "idn": cst["idn"], "idnf": cst["idnf"],
        "ones1": cst["ones1"], "ones1f": cst["ones1f"],
        "ones64": cst["ones64"],
        "band_m": cst["band_m"], "band_0": cst["band_0"],
        "band_p": cst["band_p"], "cvals": cst["cvals"],
        "dm0": cst["dm0"], "dm1": cst["dm1"], "neg240w": cst["neg240w"],
    }
    base = {k: np.ascontiguousarray(np.asarray(v)) for k, v in base.items()}
    xfull = np.asarray(inputs["L0_attn_x"], np.float32)
    maps = []
    for b in range(8):
        m = dict(base)
        m["x"] = np.ascontiguousarray(xfull[b])
        maps.append(m)
    return maps


def kernel(**inputs):
    nc, _ = build_nc()
    maps = make_in_maps(inputs)
    res = run_bass_kernel_spmd(nc, maps, core_ids=list(range(8)))
    return np.stack([np.asarray(r["y"], np.float32) for r in res.results])
